# version 16
# baseline (speedup 1.0000x reference)
import numpy as np
import ml_dtypes
from contextlib import ExitStack
import concourse.bass as bass
import concourse.mybir as mybir
from concourse.bass_utils import run_bass_kernel_spmd

F32 = mybir.dt.float32
BF16 = mybir.dt.bfloat16
AF = mybir.ActivationFunctionType
ALU = mybir.AluOpType

TOK = 1024
D = 4096
DMC = 32
TC = 8
INV_SQRT512 = float(np.float32(1.0 / np.sqrt(512.0)))
INV_SQRT128 = float(np.float32(1.0 / np.sqrt(128.0)))
L0_SHIFT = -40.0


class Sem:
    def __init__(self, nc, ctx, name):
        self.h = ctx.enter_context(nc.semaphore(name))
        self.n = 0

    def inc(self, inst, k):
        inst.then_inc(self.h, k)
        self.n += k
        return inst


def _dram(nc, name, shape, dtype, kind="Internal"):
    return nc.dram_tensor(name, list(shape), dtype, kind=kind).ap()


ENGS = ["sync", "gpsimd", "vector", "scalar", "tensor"]


ENGS_NOG = ["sync", "vector", "scalar", "tensor"]


def barrier(block, S, waits=(), engines=None):
    engines = ENGS_NOG if engines is None else engines
    b = S["bar"]
    target = b.n + len(engines)

    for ename in engines:
        def mk(e):
            b.inc(e.nop(), 1)
            e.wait_ge(b.h, target)
            for (sem, tgt) in waits:
                e.wait_ge(sem.h, tgt)
        getattr(block, ename)(mk)



def dmaw(eng, sem, out, in_):
    eng.wait_ge(sem.h, sem.n)
    return sem.inc(eng.dma_start(out=out, in_=in_), 16)

SEM_NAMES = ["lw", "lwc", "lwo", "out2", "w2", "lbr", "const", "bar", "x", "v1", "sc", "xn", "tp", "tpev",
             "vch", "w", "wg", "wc", "mm", "ev", "kv", "cc", "kld", "sS", "pexp", "pvm",
             "ep", "att", "atev", "r1", "r2", "xr", "po", "resv", "out", "aux"]


def _mk_sems(nc, ctx):
    return {k: Sem(nc, ctx, "s_" + k) for k in SEM_NAMES}


def _alloc_common(nc, ctx, S=None, sfx=""):
    sb = lambda n, shp, dt: ctx.enter_context(nc.sbuf_tensor(n + sfx, shp, dt))
    if S is None:
        S = _mk_sems(nc, ctx)
    return sb, S


def _emit_ln(nc, block, S, x_ap, x_sb, xn_tm, xn_T, ln, ps_tp, consts, cN=1):
    bx, bxn, bv1, bsc, btp, btpev = (S["x"].n, S["xn"].n, S["v1"].n,
                                     S["sc"].n, S["tp"].n, S["tpev"].n)

    @block.sync
    def _(sy):
        for tc in range(TC):
            if tc >= 2:
                sy.wait_ge(S["xn"].h, bxn + tc - 1)
            dmaw(sy, S["x"], x_sb[:, tc % 2, 0:D], x_ap[tc * 128:(tc + 1) * 128, :])

    @block.vector
    def _(v):
        vch = S["vch"]

        def edge(inst):
            vch.inc(inst, 1)
            v.wait_ge(vch.h, vch.n)
        v.wait_ge(S["const"].h, cN)
        for tc in range(TC):
            v.wait_ge(S["x"].h, bx + (tc + 1) * 16)
            for i in range(8):
                last = v.bn_stats(ln["stats"][:, i, :], x_sb[:, tc % 2, i * 512:(i + 1) * 512])
            edge(last)
            edge(v.bn_aggr(ln["mv"][:, :], ln["stats"][:, :, :]))
            v.tensor_tensor(ln["vtmp"][:, :], ln["mv"][:, 1:2], consts["eps"][:, :], ALU.add)
            S["v1"].inc(v.tensor_scalar_mul(ln["negmean"][:, :], ln["mv"][:, 0:1], -1.0), 1)
            v.wait_ge(S["sc"].h, bsc + tc + 1)
            edge(v.reciprocal(ln["rstd"][:, :], ln["sd"][:, :]))
            if tc >= 2:
                v.wait_ge(S["tp"].h, btp + (tc - 1) * 4)
            S["xn"].inc(v.tensor_scalar(xn_tm[:, tc % 2, 0:D], x_sb[:, tc % 2, 0:D],
                                        ln["negmean"][:, :], ln["rstd"][:, :],
                                        ALU.add, ALU.mult), 1)

    @block.tensor
    def _(t):
        t.wait_ge(S["const"].h, cN)
        for tc in range(TC):
            t.wait_ge(S["xn"].h, bxn + tc + 1)
            for grp in range(4):
                gi = tc * 4 + grp
                if gi >= 2:
                    t.wait_ge(S["tpev"].h, btpev + gi - 1)
                last = None
                for i in range(8):
                    dmc = grp * 8 + i
                    last = t.transpose(ps_tp[:, gi % 2, i * 128:(i + 1) * 128],
                                       xn_tm[:, tc % 2, dmc * 128:(dmc + 1) * 128],
                                       consts["ident_bf"][:, :])
                S["tp"].inc(last, 1)

    @block.scalar
    def _(s):
        def do_sqrt(tc):
            s.wait_ge(S["v1"].h, bv1 + tc + 1)
            S["sc"].inc(s.sqrt(ln["sd"][:, :], ln["vtmp"][:, :]), 1)
        do_sqrt(0)
        do_sqrt(1)
        for tc in range(TC):
            for grp in range(4):
                gi = tc * 4 + grp
                s.wait_ge(S["tp"].h, btp + gi + 1)
                dst = xn_T[:, grp * 8:(grp + 1) * 8, tc * 128:(tc + 1) * 128]
                S["tpev"].inc(
                    s.copy(dst, ps_tp[:, gi % 2, :].rearrange("p (c n) -> p c n", c=8)), 1)
            if tc + 2 < TC:
                do_sqrt(tc + 2)


def build_l0(dbg=False, shell=None, io=None, prep=None):
    if shell is None:
        nc = bass.Bass("TRN2", target_bir_lowering=False, debug=False, detect_race_conditions=False)
        S0 = None
    else:
        nc, S0 = shell
    if io is None:
        x_ap = _dram(nc, "x", [TOK, D], F32, "ExternalInput")
        xout = _dram(nc, "xout", [TOK, D], F32, "ExternalOutput")
    else:
        x_ap, xout = io
    wqkT_q = _dram(nc, "wqkT_q", [D, 512], F32, "ExternalInput")
    wqkT_k = _dram(nc, "wqkT_k", [D, 512], F32, "ExternalInput")
    wvT = _dram(nc, "wvT", [D, 512], F32, "ExternalInput")
    woT = _dram(nc, "woT", [512, D], F32, "ExternalInput")
    bq_h = _dram(nc, "bq_h", [4, 128], F32, "ExternalInput")
    bk_h = _dram(nc, "bk_h", [4, 128], F32, "ExternalInput")
    bv_ap = _dram(nc, "bv", [512], F32, "ExternalInput")
    bo_ap = _dram(nc, "bo", [D], F32, "ExternalInput")
    AT_ap = _dram(nc, "AT0", [D, 128], F32, "ExternalInput")
    BT_ap = _dram(nc, "BT0", [128, D], F32, "ExternalInput")
    WrT_ap = _dram(nc, "WrT", [128, 256], F32, "ExternalInput")
    br_ap = _dram(nc, "br", [8], F32, "ExternalInput")

    kt_own_p = [_dram(nc, "kt_own0_%d" % j, [128, TOK], BF16) for j in range(4)]
    kt_all_p = [_dram(nc, "kt_all0_%d" % j, [4 * 128, TOK], BF16) for j in range(4)]
    v_own_p = [_dram(nc, "v_own0_%d" % j, [TOK, 130], BF16) for j in range(4)]
    v_all_p = [_dram(nc, "v_all0_%d" % j, [4 * TOK, 130], BF16) for j in range(4)]

    if dbg:
        d_xnT = _dram(nc, "d_xnT", [128, DMC, TOK], BF16, "ExternalOutput")
        d_QT = _dram(nc, "d_QT", [128, 4, TOK], BF16, "ExternalOutput")
        d_KT = _dram(nc, "d_KT", [128, 4, TOK], BF16, "ExternalOutput")
        d_V = _dram(nc, "d_V", [128, TC, 520], BF16, "ExternalOutput")
        d_logits = _dram(nc, "d_logits", [128, TC, 8], F32, "ExternalOutput")
        d_lows = _dram(nc, "d_lows", [128, TOK], BF16, "ExternalOutput")
        d_attnT = _dram(nc, "d_attnT", [128, 4, TOK], BF16, "ExternalOutput")
        d_xsb = _dram(nc, "d_xsb", [128, D], F32, "ExternalOutput")
        d_mv = _dram(nc, "d_mv", [128, 2], F32, "ExternalOutput")
        d_rstd = _dram(nc, "d_rstd", [128, 1], F32, "ExternalOutput")
        d_xnl = _dram(nc, "d_xnl", [128, D], BF16, "ExternalOutput")
        d_xnT_early = _dram(nc, "d_xnT_early", [128, DMC, TOK], BF16, "ExternalOutput")

    ctx = ExitStack()
    sb, S = _alloc_common(nc, ctx, S0, sfx="_a")
    ps = lambda n, shp, dt: ctx.enter_context(nc.psum_tensor(n + "_a", shp, dt))

    x_sb = sb("x_sb0", [128, 2, 4224], F32)
    xn_tm = sb("xn_tm", [128, 2, D], BF16)
    wfstage = xn_tm.ap().rearrange("p b n -> p (b n)").bitcast(F32)
    xn_T = sb("xn_T", [128, DMC, TOK], BF16)
    ln = {"stats": sb("ln_stats", [128, 8, 6], F32), "mv": sb("ln_mv", [128, 2], F32),
          "vtmp": sb("ln_vtmp", [128, 1], F32), "negmean": sb("ln_negmean", [128, 1], F32),
          "rstd": sb("ln_rstd", [128, 1], F32), "sd": sb("ln_sd", [128, 1], F32)}
    consts = {"ident_bf": sb("ident_bf", [128, 128], BF16),
              "eps": sb("eps", [128, 1], F32),
              "ones_bf": sb("ones_bf", [1, 128], BF16),
              "ones_f32": sb("ones_f32", [1, 128], F32),
              "shift": sb("shift", [128, 1], F32),
              "maskAB": sb("maskAB", [1, 2, 128], BF16)}
    wf32 = sb("wf32", [128, 2, 1024], F32)
    wbf = sb("wbf", [128, 2, 1024], BF16)
    QT = sb("QT", [128, 4, TOK], BF16)
    KT = sb("KT", [128, 4, TOK], BF16)
    v_full = sb("v_full", [128, TC, 520], BF16)
    AT_sb = sb("AT_sb", [128, DMC, 128], BF16)
    BT_sb = sb("BT_sb", [128, D], BF16)
    WrT_sb = sb("WrT_sb", [128, DMC, 8], BF16)
    low_T = sb("low_T", [128, TOK], F32)
    low_s = sb("low_s", [128, TOK], BF16)
    logits = sb("logits", [128, TC, 8], F32)
    sroute = sb("sroute", [128, TC, 8], F32)
    sroute_bf = sb("sroute_bf", [128, TC, 8], BF16)
    sT = sb("sT", [8, TOK], BF16)
    E_exp = sb("E_exp", [8, 128], BF16)
    b_pair_q = sb("b_pair_q", [128, 4], F32)
    b_pair_k = sb("b_pair_k", [128, 4], F32)
    bvrow = sb("bvrow", [1, 512], F32)
    bias_bcast = sb("bias_bcast", [128, 512], F32)
    borow_bf = sb("borow_bf", [1, D], BF16)
    brrow = sb("brrow", [1, 8], F32)
    br_bcast = sb("br_bcast", [128, 8], F32)
    rmax = sb("rmax", [128, 1], F32)
    rnegmax = sb("rnegmax", [128, 1], F32)
    rsum = sb("rsum", [128, 1], F32)
    rrecip = sb("rrecip", [128, 1], F32)
    KVar = x_sb.ap().rearrange("p b n -> p (b n)").bitcast(BF16)
    K_pair = KVar[:, 0:8192].rearrange("p (b r t) -> p b r t", b=2, r=4)
    V_pair = KVar[:, 8192:16512].rearrange("p (b k c) -> p b k c", b=2, k=32)
    P_sb = sb("P_sb", [128, 4, 512], BF16)
    attn_T = sb("attn_T", [128, 4, TOK], BF16)
    lbf_pair = sb("lbf_pair", [1, 4, 512], BF16)
    rcpD = sb("rcpD", [128, 512], F32)
    if prep is not None:
        _pxt = xn_tm.ap().rearrange("p b n -> p (b n)").bitcast(F32).rearrange(
            "p (b n) -> p b n", b=2)
        pf32s = [_pxt[:, 0, :], _pxt[:, 1, :],
                 KT.ap().rearrange("p b n -> p (b n)").bitcast(F32),
                 v_full.ap().rearrange("p b n -> p (b n)").bitcast(F32)[:, 0:2048]]
        pbf = wf32.ap().rearrange("p b n -> p (b n)").bitcast(BF16).rearrange(
            "p (b n) -> p b n", b=2)
        pscr = wbf.ap().rearrange("p b n -> p (b n)").bitcast(F32)[:, 0:512]
    wof32 = x_sb[:, 0, 0:2048].rearrange("p (c n) -> p c n", c=4)
    wobf = x_sb[:, 0, 2048:4096].bitcast(BF16).rearrange("p (b c n) -> p b c n", b=2, c=4)
    x_res = x_sb[:, 1, 0:1024].rearrange("p (b n) -> p b n", b=2)
    xo_sb = x_sb[:, 1, 1024:2048].rearrange("p (b n) -> p b n", b=2)

    PS = ps("PS0", [128, 8, 512], F32)
    ps_tp = PS[:, 0:2, :].rearrange("p b n -> p (b n)").bitcast(BF16).rearrange(
        "p (b n) -> p b n", b=2)
    pvflat = PS[:, 4:6, :].rearrange("p b n -> p (b n)")
    ps_sb8 = PS[:, 7, :].bitcast(BF16)


    with nc.Block() as block:
        @block.gpsimd
        def _(g):
            aux = S["aux"]

            def step(inst):
                aux.inc(inst, 1)
                g.wait_ge(aux.h, aux.n)
            step(g.memset(consts["ident_bf"][:, :], 0.0))
            step(g.affine_select(out=consts["ident_bf"][:, :], in_=consts["ident_bf"][:, :],
                                 compare_op=ALU.not_equal, fill=1.0, base=0,
                                 pattern=[[-1, 128]], channel_multiplier=1))
            step(g.memset(consts["eps"][:, :], 1e-5))
            step(g.memset(consts["ones_bf"][:, :], 1.0))
            step(g.memset(consts["ones_f32"][:, :], 1.0))
            step(g.memset(consts["shift"][:, :], L0_SHIFT))
            step(g.memset(consts["maskAB"][:, :, :], 0.0))
            step(g.memset(consts["maskAB"][:, 0, 0:64], 1.0))
            step(g.memset(consts["maskAB"][:, 1, 64:128], 1.0))
            step(g.memset(E_exp[:, :], 1.0))
            step(g.affine_select(out=E_exp[:, :], in_=E_exp[:, :],
                                 compare_op=ALU.is_ge, fill=0.0, base=0,
                                 pattern=[[1, 128]], channel_multiplier=-16))
            step(g.affine_select(out=E_exp[:, :], in_=E_exp[:, :],
                                 compare_op=ALU.is_ge, fill=0.0, base=15,
                                 pattern=[[-1, 128]], channel_multiplier=16))
            step(g.memset(v_full[:, :, :].rearrange("p t (h c) -> p t h c", c=65)[:, :, :, 64:65], 1.0))
            S["const"].inc(g.nop(), 1)

        _emit_ln(nc, block, S, x_ap, x_sb, xn_tm, xn_T, ln, ps_tp, consts, cN=S["const"].n)

        @block.gpsimd
        def _(g):
            for j in range(4):
                dmaw(g, S["wg"], b_pair_q[:, j:j + 1], bq_h[j:j + 1, :].rearrange("a p -> p a"))
                dmaw(g, S["wg"], b_pair_k[:, j:j + 1], bk_h[j:j + 1, :].rearrange("a p -> p a"))
            dmaw(g, S["wg"], bvrow[:, :], bv_ap.rearrange("(a n) -> a n", a=1))
            dmaw(g, S["wg"], brrow[:, :], br_ap.rearrange("(a n) -> a n", a=1))

        prep_casts = []
        PRE = 32
        if prep is not None:
            blw, blwc, blwo = S["lw"].n, S["lwc"].n, S["lwo"].n
            prep["wo0"] = woT
            plan = []
            for dmc in range(DMC):
                plan.append(("wk", dmc * 128, 0, 1024, "cast"))
            for dmc in range(DMC):
                plan.append(("wv", dmc * 128, 0, 1024, "cast"))
            for c4 in range(4):
                for ch in range(2):
                    plan.append(("wo0", c4 * 128, ch * 2048, 2048, "cast"))
            prep["lwo_wo0"] = blwo + 72 * 16
            for dmc in range(DMC):
                for ch in range(2):
                    plan.append(("wq", dmc * 128, ch * 2048, 2048, "presum"))
            for c8 in range(8):
                for ch in range(2):
                    plan.append(("wo", c8 * 128, ch * 2048, 2048, "cast"))

            pre_f32 = [attn_T.ap().rearrange("p b n -> p (b n)").bitcast(F32)[:, 0:1024],
                       attn_T.ap().rearrange("p b n -> p (b n)").bitcast(F32)[:, 1024:2048],
                       P_sb.ap().rearrange("p b n -> p (b n)").bitcast(F32)]
            pre_bf = rcpD.ap().bitcast(BF16)

            def _dst(w, r0, c0, ncols, kind):
                if kind == "presum":
                    return prep["c_wqe"][r0:r0 + 128, (c0 // 2048) * 512:(c0 // 2048) * 512 + 512], 512
                return prep["c_" + w][r0:r0 + 128, c0:c0 + ncols], ncols

            @block.gpsimd
            def _(g):
                for k in range(PRE):
                    w, r0, c0, ncols, kind = plan[k]
                    if k >= 3:
                        g.wait_ge(S["lwc"].h, blwc + k - 2)
                    dmaw(g, S["lw"], pre_f32[k % 3][:, 0:ncols], prep[w][r0:r0 + 128, c0:c0 + ncols])
                    if k >= 1:
                        dst, dn = _dst(*plan[k - 1])
                        g.wait_ge(S["lwc"].h, blwc + k)
                        dmaw(g, S["lwo"], dst, pre_bf[:, 0:dn])
                dst, dn = _dst(*plan[PRE - 1])
                g.wait_ge(S["lwc"].h, blwc + PRE)
                dmaw(g, S["lwo"], dst, pre_bf[:, 0:dn])

            def mk_cast(k, w, r0, c0, ncols, kind):
                def go(v):
                    vch = S["vch"]
                    v.wait_ge(S["lw"].h, blw + (k + 1) * 16)
                    if k < PRE:
                        if k >= 1:
                            v.wait_ge(S["lwo"].h, blwo + k * 16)
                        S["lwc"].inc(v.tensor_copy(pre_bf[:, 0:ncols],
                                                   pre_f32[k % 3][:, 0:ncols]), 1)
                        return
                    if k >= 2:
                        v.wait_ge(S["lwo"].h, blwo + (k - 1) * 16)
                    if kind == "presum":
                        wv4 = pf32s[k % 4].rearrange("p (kv g c) -> p kv g c", kv=4, g=4)
                        scr = pscr.rearrange("p (kv c) -> p kv c", kv=4)
                        vch.inc(v.tensor_tensor(scr, wv4[:, :, 0, :], wv4[:, :, 1, :], ALU.add), 1)
                        v.wait_ge(vch.h, vch.n)
                        vch.inc(v.tensor_tensor(scr, scr, wv4[:, :, 2, :], ALU.add), 1)
                        v.wait_ge(vch.h, vch.n)
                        S["lwc"].inc(v.tensor_tensor(
                            pbf[:, k % 2, 0:512].rearrange("p (kv c) -> p kv c", kv=4),
                            scr, wv4[:, :, 3, :], ALU.add), 1)
                    else:
                        S["lwc"].inc(v.tensor_copy(pbf[:, k % 2, 0:ncols], pf32s[k % 4][:, 0:ncols]), 1)
                return go
            prep_casts = [mk_cast(k, *cfg) for k, cfg in enumerate(plan)]

        _ci = [0]

        def drain_casts_on(v, n):
            while _ci[0] < len(prep_casts) and n > 0:
                prep_casts[_ci[0]](v)
                _ci[0] += 1
                n -= 1

        barrier(block, S, waits=[(S["tpev"], S["tpev"].n), (S["const"], S["const"].n)])

        if dbg:
            @block.gpsimd
            def _(g):
                dmaw(g, S["kv"], d_xnT_early, xn_T[:, :, :])
                g.wait_ge(S["kv"].h, S["kv"].n)

        bWC2 = S["wc"].n
        bMM2 = S["mm"].n

        @block.sync
        def _(sy):
            dmaw(sy, S["w"], wfstage.rearrange("p (c n) -> p c n", c=DMC), AT_ap.rearrange("(c p) n -> p c n", p=128))

        @block.vector
        def _(v):
            v.wait_ge(S["w"].h, S["w"].n)
            S["wc"].inc(v.tensor_copy(AT_sb[:, :, :],
                                      wfstage.rearrange("p (c n) -> p c n", c=DMC)), 1)

        @block.sync
        def _(sy):
            sy.wait_ge(S["wc"].h, bWC2 + 1)
            dmaw(sy, S["w"], wfstage, BT_ap)

        @block.vector
        def _(v):
            v.wait_ge(S["w"].h, S["w"].n)
            S["wc"].inc(v.tensor_copy(BT_sb[:, :], wfstage), 1)

        @block.sync
        def _(sy):
            sy.wait_ge(S["wc"].h, bWC2 + 2)
            dmaw(sy, S["w"], wfstage[:, 0:256], WrT_ap[:, :])
            sy.wait_ge(S["wc"].h, bWC2 + 3)
            dmaw(sy, S["w"], wfstage[0:1, 0:D], bo_ap.rearrange("(a n) -> a n", a=1))

        @block.vector
        def _(v):
            v.wait_ge(S["w"].h, S["w"].n - 16)
            S["wc"].inc(v.tensor_copy(WrT_sb[:, :, :],
                                      wfstage[:, 0:256].rearrange("p (c e) -> p c e", c=DMC)), 1)
            v.wait_ge(S["w"].h, S["w"].n)
            S["wc"].inc(v.tensor_copy(borow_bf[:, :], wfstage[0:1, 0:D]), 1)

        @block.tensor
        def _(t):
            t.wait_ge(S["w"].h, S["w"].n)
            t.wait_ge(S["wg"].h, S["wg"].n)
            t.matmul(PS[:, 3, 0:512], consts["ones_f32"][0:1, :], bvrow[0:1, :],
                     start=True, stop=True)
            S["mm"].inc(t.matmul(PS[:, 2, 0:8], consts["ones_f32"][0:1, :], brrow[0:1, :],
                                 start=True, stop=True), 1)

        @block.scalar
        def _(s):
            s.wait_ge(S["mm"].h, bMM2 + 1)
            s.copy(bias_bcast[:, :], PS[:, 3, 0:512])
            S["ev"].inc(s.copy(br_bcast[:, :], PS[:, 2, 0:8]), 1)

        barrier(block, S, waits=[(S["wc"], S["wc"].n), (S["ev"], S["ev"].n), (S["wg"], S["wg"].n)])

        nMM = S["mm"].n
        nEV = S["ev"].n
        bWC1 = S["wc"].n
        wf32s = wf32.ap().rearrange("p b n -> p (b n)").rearrange("p (b n) -> p b n", b=4)
        wbfs = wbf.ap().rearrange("p b n -> p (b n)").rearrange("p (b n) -> p b n", b=4)
        wsem = {0: S["w"], 1: S["w2"]}
        bW1 = {0: S["w"].n, 1: S["w2"].n}

        def c1_load(eng, i):
            og, dmc = divmod(i, DMC)
            if i >= 4:
                eng.wait_ge(S["wc"].h, bWC1 + i - 3)
            q = wsem[i % 2]
            dmaw(eng, q, wf32s[:, i % 4, 0:256],
                 wqkT_q[dmc * 128:(dmc + 1) * 128, og * 256:(og + 1) * 256])
            dmaw(eng, q, wf32s[:, i % 4, 256:512],
                 wqkT_k[dmc * 128:(dmc + 1) * 128, og * 256:(og + 1) * 256])

        @block.sync
        def _(sy):
            for i in range(0, 2 * DMC, 2):
                c1_load(sy, i)

        @block.vector
        def _(v):
            nld = {0: 0, 1: 0}
            for i in range(2 * DMC):
                nld[i % 2] += 32
                v.wait_ge(wsem[i % 2].h, bW1[i % 2] + nld[i % 2])
                if i >= 4:
                    v.wait_ge(S["mm"].h, nMM + i - 3)
                S["wc"].inc(v.tensor_copy(wbfs[:, i % 4, :], wf32s[:, i % 4, :]), 1)
                if prep is not None and i % 2 == 1:
                    drain_casts_on(v, 1)

        @block.tensor
        def _(t):
            for i in range(2 * DMC):
                og, dmc = divmod(i, DMC)
                t.wait_ge(S["wc"].h, bWC1 + i + 1)
                if dmc == 0 and og == 1:
                    t.wait_ge(S["ev"].h, nEV + 8)
                last = None
                for j2 in range(2):
                    for half in range(2):
                        t.matmul(PS[:, j2 * 2 + half, :], wbfs[:, i % 4, j2 * 128:(j2 + 1) * 128],
                                 xn_T[:, dmc, half * 512:(half + 1) * 512],
                                 start=(dmc == 0), stop=(dmc == DMC - 1))
                        last = t.matmul(PS[:, 4 + j2 * 2 + half, :],
                                        wbfs[:, i % 4, 256 + j2 * 128:256 + (j2 + 1) * 128],
                                        xn_T[:, dmc, half * 512:(half + 1) * 512],
                                        start=(dmc == 0), stop=(dmc == DMC - 1))
                S["mm"].inc(last, 1)

        @block.scalar
        def _(s):
            for og in range(2):
                for i in range(og * DMC + 1, (og + 1) * DMC, 2):
                    c1_load(s, i)
                s.wait_ge(S["mm"].h, nMM + (og + 1) * DMC)
                last = None
                for j2 in range(2):
                    j = og * 2 + j2
                    for half in range(2):
                        s.activation(out=QT[:, j, half * 512:(half + 1) * 512],
                                     in_=PS[:, j2 * 2 + half, :],
                                     func=AF.Identity, bias=b_pair_q[:, j:j + 1], scale=1.0)
                        last = s.activation(out=KT[:, j, half * 512:(half + 1) * 512],
                                            in_=PS[:, 4 + j2 * 2 + half, :],
                                            func=AF.Identity, bias=b_pair_k[:, j:j + 1], scale=1.0)
                S["ev"].inc(last, 8)

        barrier(block, S, waits=[(S["ev"], S["ev"].n)])

        nEVk0 = S["ev"].n

        ncc0 = S["cc"].n
        CC_K = [ncc0 + 1, ncc0 + 2, ncc0 + 4, ncc0 + 5]
        CC_V = [ncc0 + 3, ncc0 + 6, ncc0 + 7, ncc0 + 8]

        @block.gpsimd
        def _(g):
            for j in range(4):
                g.wait_ge(S["ev"].h, nEVk0 - 16 + (8 if j < 2 else 16))
                dmaw(g, S["kv"], kt_own_p[j], KT[:, j, :])
            g.wait_ge(S["kv"].h, S["kv"].n)
            for j in range(2):
                S["cc"].inc(g.collective_compute(
                    "AllGather", ALU.bypass,
                    replica_groups=[[0, 1, 2, 3], [4, 5, 6, 7]],
                    ins=[kt_own_p[j].opt()], outs=[kt_all_p[j].opt()]), 1)

        nWC2 = S["wc"].n
        nMM2 = S["mm"].n
        bW3 = {0: S["w"].n, 1: S["w2"].n}

        def c3_load(eng, dmc):
            if dmc >= 4:
                eng.wait_ge(S["wc"].h, nWC2 + dmc - 3)
            dmaw(eng, wsem[dmc % 2], wf32s[:, dmc % 4, 0:512],
                 wvT[dmc * 128:(dmc + 1) * 128, :])

        @block.sync
        def _(sy):
            for dmc in range(0, DMC, 2):
                c3_load(sy, dmc)

        @block.scalar
        def _(s):
            for dmc in range(1, DMC, 2):
                c3_load(s, dmc)

        @block.vector
        def _(v):
            nld = {0: 0, 1: 0}
            for dmc in range(DMC):
                nld[dmc % 2] += 16
                v.wait_ge(wsem[dmc % 2].h, bW3[dmc % 2] + nld[dmc % 2])
                if dmc >= 4:
                    v.wait_ge(S["mm"].h, nMM2 + dmc - 3)
                S["wc"].inc(v.tensor_copy(wbfs[:, dmc % 4, :], wf32s[:, dmc % 4, :]), 1)

        @block.tensor
        def _(t):
            for dmc in range(DMC):
                t.wait_ge(S["wc"].h, nWC2 + dmc + 1)
                last = None
                for tc in range(TC):
                    last = t.matmul(PS[:, tc, :], xn_T[:, dmc, tc * 128:(tc + 1) * 128],
                                    wbfs[:, dmc % 4, :],
                                    start=(dmc == 0), stop=(dmc == DMC - 1))
                S["mm"].inc(last, 1)

        @block.vector
        def _(v):
            v.wait_ge(S["mm"].h, nMM2 + DMC)
            last = None
            for tc in range(TC):
                dst = v_full[:, tc, :].rearrange("p (h c) -> p h c", c=65)[:, :, 0:64]
                last = v.scalar_tensor_tensor(
                    out=dst, in0=PS[:, tc, :].rearrange("p (h e) -> p h e", e=64),
                    scalar=1.0, in1=bias_bcast[:, :].rearrange("p (h e) -> p h e", e=64),
                    op0=ALU.mult, op1=ALU.add)
            S["ev"].inc(last, 1)

        @block.gpsimd
        def _(g):
            g.wait_ge(S["ev"].h, S["ev"].n)
            for j in range(4):
                dmaw(g, S["kv"], v_own_p[j].rearrange("(c p) n -> p c n", p=128),
                     v_full[:, :, 130 * j:130 * (j + 1)])
            g.wait_ge(S["kv"].h, S["kv"].n)

            def ag(src, dst):
                S["cc"].inc(g.collective_compute(
                    "AllGather", ALU.bypass,
                    replica_groups=[[0, 1, 2, 3], [4, 5, 6, 7]],
                    ins=[src.opt()], outs=[dst.opt()]), 1)
            ag(v_own_p[0], v_all_p[0])
            ag(kt_own_p[2], kt_all_p[2])
            ag(kt_own_p[3], kt_all_p[3])
            for j in range(1, 4):
                ag(v_own_p[j], v_all_p[j])

        barrier(block, S, waits=[(S["ev"], S["ev"].n)])

        if prep is not None:
            @block.gpsimd
            def _(g):
                for k in range(PRE, len(plan)):
                    w, r0, c0, ncols, kind = plan[k]
                    if k >= PRE + 4:
                        g.wait_ge(S["lwc"].h, blwc + k - 3)
                    dmaw(g, S["lw"], pf32s[k % 4][:, 0:ncols], prep[w][r0:r0 + 128, c0:c0 + ncols])
                    if k >= PRE + 1:
                        pw = k - 1
                        dst, dn = _dst(*plan[pw])
                        g.wait_ge(S["lwc"].h, blwc + pw + 1)
                        dmaw(g, S["lwo"], dst, pbf[:, pw % 2, 0:dn])
                pw = len(plan) - 1
                dst, dn = _dst(*plan[pw])
                g.wait_ge(S["lwc"].h, blwc + pw + 1)
                dmaw(g, S["lwo"], dst, pbf[:, pw % 2, 0:dn])

        nMM3 = S["mm"].n
        nR1c = S["r1"].n

        @block.tensor
        def _(t):
            last = None
            for dmc in range(DMC):
                for tc in range(TC):
                    last = t.matmul(PS[:, tc, 0:8], xn_T[:, dmc, tc * 128:(tc + 1) * 128],
                                    WrT_sb[:, dmc, :],
                                    start=(dmc == 0), stop=(dmc == DMC - 1))
            S["mm"].inc(last, 1)

        @block.vector
        def _(v):
            v.wait_ge(S["mm"].h, nMM3 + 1)
            last = None
            for tc in range(TC):
                last = v.tensor_tensor(logits[:, tc, :], PS[:, tc, 0:8],
                                       br_bcast[:, :], ALU.add)
            S["r1"].inc(last, 1)

        @block.tensor
        def _(t):
            t.wait_ge(S["r1"].h, nR1c + 1)
            for half in range(2):
                last = None
                for dmc in range(DMC):
                    last = t.matmul(PS[:, 1 + half, :], AT_sb[:, dmc, :],
                                    xn_T[:, dmc, half * 512:(half + 1) * 512],
                                    start=(dmc == 0), stop=(dmc == DMC - 1))
                S["mm"].inc(last, 1)

        @block.scalar
        def _(s):
            s.wait_ge(S["mm"].h, nMM3 + 3)
            for half in range(2):
                S["ev"].inc(s.copy(low_T[:, half * 512:(half + 1) * 512], PS[:, 1 + half, :]), 1)

        @block.vector
        def _(v):
            v.wait_ge(S["r1"].h, S["r1"].n)
            vch = S["vch"]
            bR2 = S["r2"].n
            for tc in range(TC):
                vch.inc(v.reduce_max(rmax[:, :], logits[:, tc, :], axis=mybir.AxisListType.X), 1)
                v.wait_ge(vch.h, vch.n)
                S["r1"].inc(v.tensor_scalar_mul(rnegmax[:, :], rmax[:, :], -1.0), 1)
                v.wait_ge(S["r2"].h, bR2 + tc + 1)
                vch.inc(v.reciprocal(rrecip[:, :], rsum[:, :]), 1)
                v.wait_ge(vch.h, vch.n)
                S["r1"].inc(v.tensor_scalar(sroute_bf[:, tc, :], sroute[:, tc, :],
                                            rrecip[:, :], 16.0, ALU.mult, ALU.mult), 1)

        @block.scalar
        def _(s):
            base = S["r1"].n - 2 * TC
            for tc in range(TC):
                s.wait_ge(S["r1"].h, base + 2 * tc + 1)
                S["r2"].inc(s.activation(out=sroute[:, tc, :], in_=logits[:, tc, :],
                                         func=AF.Exp, bias=rnegmax[:, :], scale=1.0,
                                         accum_out=rsum[:, :]), 1)

        @block.tensor
        def _(t):
            t.wait_ge(S["r1"].h, S["r1"].n)
            last = None
            for tc in range(TC):
                last = t.transpose(ps_sb8[0:8, tc * 128:(tc + 1) * 128],
                                   sroute_bf[:, tc, :], consts["ident_bf"][:, :])
            S["att"].inc(last, 1)

        @block.scalar
        def _(s):
            s.wait_ge(S["att"].h, S["att"].n)
            S["ev"].inc(s.copy(sT[0:8, :], ps_sb8[0:8, :]), 1)

        @block.tensor
        def _(t):
            t.wait_ge(S["ev"].h, S["ev"].n)
            for half in range(2):
                S["mm"].inc(t.matmul(PS[:, 4 + half, :], E_exp[0:8, :],
                                     sT[0:8, half * 512:(half + 1) * 512],
                                     start=True, stop=True), 1)

        @block.vector
        def _(v):
            v.wait_ge(S["mm"].h, S["mm"].n)
            v.wait_ge(S["ev"].h, S["ev"].n)
            last = None
            for half in range(2):
                last = v.tensor_tensor(low_s[:, half * 512:(half + 1) * 512],
                                       low_T[:, half * 512:(half + 1) * 512],
                                       PS[:, 4 + half, :], ALU.mult)
            S["r1"].inc(last, 1)

        barrier(block, S, waits=[(S["r1"], S["r1"].n)])

        if dbg:
            @block.sync
            def _(sy):
                dmaw(sy, S["aux"], d_xnT, xn_T[:, :, :])
                dmaw(sy, S["aux"], d_QT, QT[:, :, :])
                dmaw(sy, S["aux"], d_KT, KT[:, :, :])
                dmaw(sy, S["aux"], d_V, v_full[:, :, :])
                dmaw(sy, S["aux"], d_logits, logits[:, :, :])
                dmaw(sy, S["aux"], d_lows, low_s[:, :])
                dmaw(sy, S["aux"], d_xsb, x_sb[:, 1, 0:D])
                dmaw(sy, S["aux"], d_mv, ln["mv"][:, :])
                dmaw(sy, S["aux"], d_rstd, ln["rstd"][:, :])
                dmaw(sy, S["aux"], d_xnl, xn_tm[:, 1, 0:D])

        kt_views = [ka.rearrange("(r p) t -> p r t", p=128) for ka in kt_all_p]
        v_views = [va.rearrange("(kc p) n -> p kc n", p=128) for va in v_all_p]
        cnt = {"kld": S["kld"].n}
        bxrD = S["xr"].n
        bsS, bpexp, bpvm, bep, batev, blbr, batt = (S["sS"].n, S["pexp"].n, S["pvm"].n,
                                                    S["ep"].n, S["atev"].n, S["lbr"].n,
                                                    S["att"].n)
        s_idx = [0]

        @block.sync
        def _(sy):
            for j in range(4):
                if j >= 2:
                    sy.wait_ge(S["pvm"].h, bpvm + (j - 1) * 128)
                sy.wait_ge(S["cc"].h, CC_K[j])
                dmaw(sy, S["kld"], K_pair[:, j % 2, :, :], kt_views[j][:, :, :])
                sy.wait_ge(S["cc"].h, CC_V[j])
                for q in range(4):
                    dmaw(sy, S["xr"], V_pair[:, j % 2, q * 8:(q + 1) * 8, :],
                         v_views[j][:, q * 8:(q + 1) * 8, :])

        @block.tensor
        def _(t):
            def emit_S(j, kc, u, si):
                hh, qh = divmod(u, 2)
                S["sS"].inc(t.matmul(
                    PS[:, si % 4, :],
                    K_pair[hh * 64:(hh + 1) * 64, j % 2, kc // 8, (kc % 8) * 128:(kc % 8 + 1) * 128],
                    QT[hh * 64:(hh + 1) * 64, j, qh * 512:(qh + 1) * 512],
                    start=True, stop=True, tile_position=(hh * 64, 0)), 1)

            def emit_PV(j, kc, u, si):
                hh = u // 2
                S["pvm"].inc(t.matmul(
                    PS[0:65, 4 + u, :],
                    V_pair[:, j % 2, kc, hh * 65:(hh + 1) * 65],
                    P_sb[:, si % 4, :],
                    start=(kc == 0), stop=(kc == 31)), 1)

            for j in range(4):
                t.wait_ge(S["kld"].h, cnt["kld"] + (j + 1) * 16)
                base_si = s_idx[0]
                for kc in range(32):
                    if s_idx[0] >= 4:
                        t.wait_ge(S["pexp"].h, bpexp + s_idx[0])
                    for u in range(4):
                        emit_S(j, kc, u, s_idx[0])
                        s_idx[0] += 1
                    if kc >= 1:
                        if kc == 1:
                            t.wait_ge(S["ep"].h, bep + j)
                            t.wait_ge(S["lbr"].h, blbr + j * 2)
                            t.wait_ge(S["xr"].h, bxrD + (j + 1) * 64)
                        t.wait_ge(S["pexp"].h, bpexp + base_si + 4 * kc)
                        for u in range(4):
                            emit_PV(j, kc - 1, u, base_si + 4 * (kc - 1) + u)
                t.wait_ge(S["pexp"].h, bpexp + base_si + 128)
                for u in range(4):
                    emit_PV(j, 31, u, base_si + 4 * 31 + u)
                for qh in range(2):
                    t.wait_ge(S["ep"].h, bep + j + 1)
                    t.matmul(PS[:, 4 + qh, :], consts["maskAB"][:, 0, :], lbf_pair[:, qh, :],
                             start=True, stop=False)
                    S["atev"].inc(t.matmul(PS[:, 4 + qh, :], consts["maskAB"][:, 1, :],
                                           lbf_pair[:, 2 + qh, :],
                                           start=False, stop=True), 1)

        @block.scalar
        def _(s):
            si = 0
            for j in range(4):
                for kc in range(32):
                    for u in range(4):
                        s.wait_ge(S["sS"].h, bsS + si + 1)
                        if si >= 4:
                            s.wait_ge(S["pvm"].h, bpvm + si - 3)
                        S["pexp"].inc(s.activation(
                            out=P_sb[:, si % 4, :], in_=PS[:, si % 4, :],
                            func=AF.Exp, bias=consts["shift"][:, :], scale=1.0), 1)
                        si += 1

        @block.vector
        def _(v):
            for j in range(4):
                drain_casts_on(v, 16)
                v.wait_ge(S["pvm"].h, bpvm + (j + 1) * 128)
                v.tensor_copy(attn_T[0:64, j, 0:512], PS[0:64, 4, :])
                v.tensor_copy(attn_T[0:64, j, 512:1024], PS[0:64, 5, :])
                v.tensor_copy(attn_T[64:128, j, 0:512], PS[0:64, 6, :])
                v.tensor_copy(attn_T[64:128, j, 512:1024], PS[0:64, 7, :])
                last = None
                for u in range(4):
                    last = v.tensor_copy(lbf_pair[:, u, :], PS[64:65, 4 + u, :])
                S["ep"].inc(last, 1)
                v.wait_ge(S["ep"].h, S["ep"].n)
                for qh in range(2):
                    v.wait_ge(S["atev"].h, batev + j * 2 + qh + 1)
                    S["lbr"].inc(v.reciprocal(rcpD[:, :], PS[:, 4 + qh, :]), 1)
                    v.wait_ge(S["lbr"].h, S["lbr"].n)
                    inst = v.scalar_tensor_tensor(
                        out=attn_T[:, j, qh * 512:(qh + 1) * 512], in0=rcpD[:, :],
                        scalar=INV_SQRT512, in1=attn_T[:, j, qh * 512:(qh + 1) * 512],
                        op0=ALU.mult, op1=ALU.mult)
                    if qh == 1:
                        S["att"].inc(inst, 1)

        barrier(block, S, waits=[(S["att"], S["att"].n)])

        if dbg:
            @block.sync
            def _(sy):
                dmaw(sy, S["aux"], d_attnT, attn_T[:, :, :])

        nW3 = S["w"].n
        nWC3 = S["wc"].n
        nPO = S["po"].n
        bXR = S["xr"].n
        bOUT = S["out2"].n
        bRESV = S["resv"].n

        @block.sync
        def _(sy):
            def load_wo(oc):
                if prep is not None:
                    if oc == 0:
                        sy.wait_ge(S["lwo"].h, prep["lwo_wo0"])
                    if oc >= 2:
                        sy.wait_ge(S["po"].h, nPO + (oc - 1) * 8)
                    dmaw(sy, S["w"], wobf[:, oc % 2, :, :],
                         prep["c_wo0"].rearrange("(c p) o -> p c o", p=128)[:, :, oc * 512:(oc + 1) * 512])
                else:
                    dmaw(sy, S["w"], wof32[:, :, :],
                         woT.rearrange("(c p) o -> p c o", p=128)[:, :, oc * 512:(oc + 1) * 512])
            load_wo(0)
            for oc in range(8):
                for tc in range(TC):
                    i = oc * 8 + tc
                    if i >= 2:
                        sy.wait_ge(S["resv"].h, bRESV + i - 1)
                    dmaw(sy, S["xr"], x_res[:, i % 2, :], x_ap[tc * 128:(tc + 1) * 128, oc * 512:(oc + 1) * 512])
                    if tc == 1 and oc + 1 < 8 and prep is not None:
                        load_wo(oc + 1)
                    if tc == 1 and oc + 1 < 8 and prep is None:
                        sy.wait_ge(S["wc"].h, nWC3 + oc + 1)
                        load_wo(oc + 1)

        @block.vector
        def _(v):
            def cast(oc):
                v.wait_ge(S["w"].h, nW3 + (oc + 1) * 16)
                if oc >= 2:
                    v.wait_ge(S["po"].h, nPO + (oc - 1) * 8)
                S["wc"].inc(v.tensor_copy(wobf[:, oc % 2, :, :], wof32[:, :, :]), 1)
            if prep is None:
                cast(0)
                cast(1)
            for oc in range(8):
                for tc in range(TC):
                    i = oc * 8 + tc
                    v.wait_ge(S["po"].h, nPO + i + 1)
                    v.wait_ge(S["xr"].h, bXR + (i + 1) * 16)
                    if i >= 2:
                        v.wait_ge(S["out2"].h, bOUT + (i - 1) * 16)
                    S["resv"].inc(v.scalar_tensor_tensor(
                        out=xo_sb[:, i % 2, :], in0=PS[:, i % 4, :], scalar=1.0,
                        in1=x_res[:, i % 2, :], op0=ALU.mult, op1=ALU.add), 1)
                    if prep is not None:
                        drain_casts_on(v, 1)
                if oc + 2 < 8 and prep is None:
                    cast(oc + 2)
            if prep is not None:
                drain_casts_on(v, 10 ** 9)

        @block.tensor
        def _(t):
            for oc in range(8):
                if prep is not None:
                    t.wait_ge(S["w"].h, nW3 + (oc + 1) * 16)
                else:
                    t.wait_ge(S["wc"].h, nWC3 + oc + 1)
                for tc in range(TC):
                    i = oc * 8 + tc
                    b = i % 4
                    if i >= 4:
                        t.wait_ge(S["resv"].h, bRESV + i - 3)
                    for jj in range(4):
                        t.matmul(PS[:, b, :], attn_T[:, jj, tc * 128:(tc + 1) * 128],
                                 wobf[:, oc % 2, jj, :], start=(jj == 0), stop=False)
                    t.matmul(PS[:, b, :], low_s[:, tc * 128:(tc + 1) * 128],
                             BT_sb[:, oc * 512:(oc + 1) * 512], start=False, stop=False)
                    S["po"].inc(t.matmul(PS[:, b, :], consts["ones_bf"][0:1, :],
                                         borow_bf[0:1, oc * 512:(oc + 1) * 512],
                                         start=False, stop=True), 1)

        @block.scalar
        def _(sc):
            for oc in range(8):
                for tc in range(TC):
                    i = oc * 8 + tc
                    sc.wait_ge(S["resv"].h, bRESV + i + 1)
                    dmaw(sc, S["out2"], xout[tc * 128:(tc + 1) * 128, oc * 512:(oc + 1) * 512],
                         xo_sb[:, i % 2, :])
            sc.wait_ge(S["out2"].h, S["out2"].n)

        @block.gpsimd
        def _(g):
            g.wait_ge(S["out2"].h, S["out2"].n)
            if dbg:
                g.wait_ge(S["aux"].h, S["aux"].n)

        @block.sync
        def _(sy):
            sy.wait_ge(S["out"].h, S["out"].n)

    ctx.close()
    return nc


def _host_layout_l0(inputs, bslice):
    Wqk = np.ascontiguousarray(inputs["L0_attn_Wqk_w"])
    rows = np.arange(1024)
    h = rows // 128
    qrows = np.empty(512, np.int64)
    krows = np.empty(512, np.int64)
    for j in range(4):
        for hh in range(2):
            hlin = 2 * j + hh
            e = np.arange(64)
            qrows[j * 128 + hh * 64 + e] = hlin * 128 + e * 2 + 0
            krows[j * 128 + hh * 64 + e] = hlin * 128 + e * 2 + 1
    wqkT_q = np.ascontiguousarray(Wqk[qrows, :].T)
    wqkT_k = np.ascontiguousarray(Wqk[krows, :].T)
    bq_h = np.ascontiguousarray(inputs["L0_attn_Wqk_b"][qrows].reshape(4, 128))
    bk_h = np.ascontiguousarray(inputs["L0_attn_Wqk_b"][krows].reshape(4, 128))
    wvT = np.ascontiguousarray(inputs["L0_attn_Wv_w"].T)
    woT = np.ascontiguousarray(inputs["L0_attn_Wo_w"].T)
    A = inputs["L0_ffn_A"]
    AT = np.ascontiguousarray(A.reshape(128, 4096).T)
    B = inputs["L0_ffn_B"]
    BT = np.ascontiguousarray(B.transpose(0, 2, 1).reshape(128, 4096))
    Wr = inputs["L0_ffn_Wr"]
    WrT = np.ascontiguousarray(Wr.T.reshape(32, 128, 8).transpose(1, 0, 2).reshape(128, 256))
    return {
        "x": np.ascontiguousarray(inputs["L0_attn_x"].reshape(2, 4, 1024, 4096)[bslice[0], bslice[1]]),
        "wqkT_q": wqkT_q, "wqkT_k": wqkT_k, "wvT": wvT, "woT": woT,
        "bq_h": bq_h, "bk_h": bk_h,
        "bv": np.ascontiguousarray(inputs["L0_attn_Wv_b"]),
        "bo": np.ascontiguousarray(inputs["L0_attn_Wo_b"]),
        "AT0": AT, "BT0": BT, "WrT": WrT,
        "br": np.ascontiguousarray(inputs["L0_ffn_br"]),
    }




def build_l1(dbg=False, shell=None, io=None, caches=None, skip_w=False):
    if shell is None:
        nc = bass.Bass("TRN2", target_bir_lowering=False, debug=False, detect_race_conditions=False)
        S0 = None
    else:
        nc, S0 = shell
    if io is None:
        x_ap = _dram(nc, "x", [TOK, D], F32, "ExternalInput")
        xout = _dram(nc, "xout", [TOK, D], F32, "ExternalOutput")
    else:
        x_ap, xout = io
    if not skip_w:
        wq_ap = _dram(nc, "wq", [D, D], F32, "ExternalInput")
        wk_ap = _dram(nc, "wk", [D, 1024], F32, "ExternalInput")
        wv_ap = _dram(nc, "wv", [D, 1024], F32, "ExternalInput")
        wo_ap = _dram(nc, "wo", [1024, D], F32, "ExternalInput")
    AT_ap = _dram(nc, "AT1", [D, 128], F32, "ExternalInput")
    BT_ap = _dram(nc, "BT1", [128, D], F32, "ExternalInput")
    RT_ap = _dram(nc, "RT", [128, 256], F32, "ExternalInput")

    kt_own = _dram(nc, "kt_own1", [1024, TOK], BF16)
    v_own = _dram(nc, "v_own1", [TOK, 1032], BF16)
    kt_all = _dram(nc, "kt_all1", [4 * 1024, TOK], BF16)
    v_all = _dram(nc, "v_all1", [4 * TOK, 1032], BF16)
    if caches is None:
        caches = {"c_wqe": _dram(nc, "c_wqe", [D, 1024], BF16, "ExternalInput"),
                  "c_wk": _dram(nc, "c_wk", [D, 1024], BF16, "ExternalInput"),
                  "c_wv": _dram(nc, "c_wv", [D, 1024], BF16, "ExternalInput"),
                  "c_wo": _dram(nc, "c_wo", [1024, D], BF16, "ExternalInput")}
    c_wqe, c_wk, c_wv, c_wo = caches["c_wqe"], caches["c_wk"], caches["c_wv"], caches["c_wo"]

    if dbg:
        d_xnT = _dram(nc, "d_xnT", [128, DMC, TOK], BF16, "ExternalOutput")
        d_QT = _dram(nc, "d_QT", [128, 8, TOK], BF16, "ExternalOutput")
        d_KT = _dram(nc, "d_KT", [128, 8, TOK], BF16, "ExternalOutput")
        d_V = _dram(nc, "d_V", [128, TC, 1032], BF16, "ExternalOutput")
        d_logits = _dram(nc, "d_logits", [128, TC, 8], F32, "ExternalOutput")
        d_lows = _dram(nc, "d_lows", [128, TOK], BF16, "ExternalOutput")
        d_attnT = _dram(nc, "d_attnT", [128, 8, TOK], BF16, "ExternalOutput")

    ctx = ExitStack()
    sb, S = _alloc_common(nc, ctx, S0, sfx="_b")
    ps = lambda n, shp, dt: ctx.enter_context(nc.psum_tensor(n + "_b", shp, dt))

    x_sb = sb("x_sb1", [128, 2, 4224], F32)
    xn_tm = sb("xn_tm", [128, 2, D], BF16)
    wfstage = xn_tm.ap().rearrange("p b n -> p (b n)").bitcast(F32)
    xn_T = sb("xn_T", [128, DMC, TOK], BF16)
    ln = {"stats": sb("ln_stats", [128, 8, 6], F32), "mv": sb("ln_mv", [128, 2], F32),
          "vtmp": sb("ln_vtmp", [128, 1], F32), "negmean": sb("ln_negmean", [128, 1], F32),
          "rstd": sb("ln_rstd", [128, 1], F32), "sd": sb("ln_sd", [128, 1], F32)}
    consts = {"ident_bf": sb("ident_bf", [128, 128], BF16),
              "eps": sb("eps", [128, 1], F32),
              "ones_bf": sb("ones_bf", [1, 128], BF16),
              "sc128": sb("sc128", [128, 1], F32)}
    QT = sb("QT", [128, 8, TOK], BF16)
    KT = sb("KT", [128, 8, TOK], BF16)
    v_full = sb("v_full", [128, TC, 1032], BF16)
    AT_sb = sb("AT_sb", [128, DMC, 128], BF16)
    BT_sb = sb("BT_sb", [128, D], BF16)
    RT_sb = sb("RT_sb", [128, DMC, 8], BF16)
    low_T = sb("low_T", [128, TOK], F32)
    low_s = sb("low_s", [128, TOK], BF16)
    logits = sb("logits", [128, TC, 8], F32)
    stok_bf = sb("stok_bf", [128, TC, 8], BF16)
    E_exp = sb("E_exp", [8, 128], BF16)
    rmax = sb("rmax", [128, 1], F32)
    P_sb = sb("P_sb", [128, 4, 512], BF16)
    sT = P_sb.ap().rearrange("p b n -> p (b n)")[0:8, 0:1024]
    attn_stage = sb("attn_stage", [128, 8, 128], BF16)
    attn_T = KT
    lrecip = sb("lrecip", [128, 1], F32)

    AR = x_sb.ap().rearrange("p b n -> p (b n)")
    ARb = AR.bitcast(BF16)
    wqf32 = AR[:, 0:4096]
    wqbf = ARb[:, 8192:12288]
    K_h = ARb[:, 0:8192].rearrange("p (b r t) -> p b r t", b=2, r=4)
    V_h = ARb[:, 8192:16448].rearrange("p (b k c) -> p b k c", b=2, k=32)
    x_res = AR[:, 4608:5632].rearrange("p (b n) -> p b n", b=2)
    xo_sb = AR[:, 5632:6656].rearrange("p (b n) -> p b n", b=2)

    PS = ps("PS1", [128, 8, 512], F32)
    ps_tp = PS[:, 0:2, :].rearrange("p b n -> p (b n)").bitcast(BF16).rearrange(
        "p (b n) -> p b n", b=2)
    ps_atp = PS[:, 0, :].bitcast(BF16)
    ps_sb8 = PS[:, 3, :].bitcast(BF16)

    with nc.Block() as block:
        @block.gpsimd
        def _(g):
            aux = S["aux"]

            def step(inst):
                aux.inc(inst, 1)
                g.wait_ge(aux.h, aux.n)
            step(g.memset(consts["ident_bf"][:, :], 0.0))
            step(g.affine_select(out=consts["ident_bf"][:, :], in_=consts["ident_bf"][:, :],
                                 compare_op=ALU.not_equal, fill=1.0, base=0,
                                 pattern=[[-1, 128]], channel_multiplier=1))
            step(g.memset(consts["eps"][:, :], 1e-5))
            step(g.memset(consts["ones_bf"][:, :], 1.0))
            step(g.memset(consts["sc128"][:, :], INV_SQRT128))
            step(g.memset(E_exp[:, :], 1.0))
            step(g.affine_select(out=E_exp[:, :], in_=E_exp[:, :],
                                 compare_op=ALU.is_ge, fill=0.0, base=0,
                                 pattern=[[1, 128]], channel_multiplier=-16))
            step(g.affine_select(out=E_exp[:, :], in_=E_exp[:, :],
                                 compare_op=ALU.is_ge, fill=0.0, base=15,
                                 pattern=[[-1, 128]], channel_multiplier=16))
            step(g.memset(v_full[:, :, :].rearrange("p t (h c) -> p t h c", c=129)[:, :, :, 128:129], 1.0))
            S["const"].inc(g.nop(), 1)

        _emit_ln(nc, block, S, x_ap, x_sb, xn_tm, xn_T, ln, ps_tp, consts, cN=S["const"].n)

        barrier(block, S, waits=[(S["tpev"], S["tpev"].n), (S["const"], S["const"].n),
                                 (S["lwo"], S["lwo"].n)])

        bWCl1 = S["wc"].n

        @block.sync
        def _(sy):
            dmaw(sy, S["w"], wfstage.rearrange("p (c n) -> p c n", c=DMC),
                 AT_ap.rearrange("(c p) n -> p c n", p=128))

        @block.vector
        def _(v):
            v.wait_ge(S["w"].h, S["w"].n)
            S["wc"].inc(v.tensor_copy(AT_sb[:, :, :],
                                      wfstage.rearrange("p (c n) -> p c n", c=DMC)), 1)

        @block.sync
        def _(sy):
            sy.wait_ge(S["wc"].h, bWCl1 + 1)
            dmaw(sy, S["w"], wfstage, BT_ap)

        @block.vector
        def _(v):
            v.wait_ge(S["w"].h, S["w"].n)
            S["wc"].inc(v.tensor_copy(BT_sb[:, :], wfstage), 1)

        @block.sync
        def _(sy):
            sy.wait_ge(S["wc"].h, bWCl1 + 2)
            dmaw(sy, S["w"], wfstage[:, 0:256], RT_ap[:, :])

        @block.vector
        def _(v):
            v.wait_ge(S["w"].h, S["w"].n)
            S["wc"].inc(v.tensor_copy(RT_sb[:, :, :],
                                      wfstage[:, 0:256].rearrange("p (c e) -> p c e", c=DMC)), 1)

        barrier(block, S, waits=[(S["wc"], S["wc"].n)])

        wstr = ARb[:, 0:4096].rearrange("p (b n) -> p b n", b=8)
        wsem = {0: S["w"], 1: S["w2"]}

        def proj_bf(src_ap, OUT_sb):
            nMM2 = S["mm"].n
            nEV2 = S["ev"].n
            bW2 = {0: S["w"].n, 1: S["w2"].n}

            def ld(eng, i):
                og, dmc = divmod(i, DMC)
                if i >= 8:
                    eng.wait_ge(S["mm"].h, nMM2 + i - 7)
                dmaw(eng, wsem[i % 2], wstr[:, i % 8, :],
                     src_ap[dmc * 128:(dmc + 1) * 128, og * 512:(og + 1) * 512])

            @block.sync
            def _(sy):
                for i in range(0, 2 * DMC, 2):
                    ld(sy, i)

            @block.tensor
            def _(t):
                nld = {0: 0, 1: 0}
                for i in range(2 * DMC):
                    og, dmc = divmod(i, DMC)
                    nld[i % 2] += 16
                    t.wait_ge(wsem[i % 2].h, bW2[i % 2] + nld[i % 2])
                    if dmc == 0 and og == 1:
                        t.wait_ge(S["ev"].h, nEV2 + 8)
                    last = None
                    for oc in range(4):
                        for half in range(2):
                            last = t.matmul(PS[:, oc * 2 + half, :],
                                            wstr[:, i % 8, oc * 128:(oc + 1) * 128],
                                            xn_T[:, dmc, half * 512:(half + 1) * 512],
                                            start=(dmc == 0), stop=(dmc == DMC - 1))
                    S["mm"].inc(last, 1)

            @block.scalar
            def _(s):
                for og in range(2):
                    for i in range(og * DMC + 1, (og + 1) * DMC, 2):
                        ld(s, i)
                    s.wait_ge(S["mm"].h, nMM2 + (og + 1) * DMC)
                    last = None
                    for oc in range(4):
                        for half in range(2):
                            last = s.copy(OUT_sb[:, og * 4 + oc, half * 512:(half + 1) * 512],
                                          PS[:, oc * 2 + half, :])
                    S["ev"].inc(last, 8)

        proj_bf(c_wk, KT)
        barrier(block, S, waits=[(S["ev"], S["ev"].n)])

        nEVk1 = S["ev"].n

        @block.gpsimd
        def _(g):
            g.wait_ge(S["ev"].h, nEVk1)
            dmaw(g, S["kv"], kt_own.rearrange("(c p) t -> p c t", p=128), KT[:, :, :])
            g.wait_ge(S["kv"].h, S["kv"].n)
            S["cc"].inc(g.collective_compute(
                "AllGather", ALU.bypass,
                replica_groups=[[0, 1, 2, 3], [4, 5, 6, 7]],
                ins=[kt_own.opt()], outs=[kt_all.opt()]), 1)

        nMM3 = S["mm"].n
        nEVv = S["ev"].n
        bWv = {0: S["w"].n, 1: S["w2"].n}

        def vld(eng, i):
            dv, dmc = divmod(i, DMC)
            if i >= 8:
                eng.wait_ge(S["mm"].h, nMM3 + i - 7)
            dmaw(eng, wsem[i % 2], wstr[:, i % 8, :],
                 c_wv[dmc * 128:(dmc + 1) * 128, dv * 512:(dv + 1) * 512])

        @block.sync
        def _(sy):
            for i in range(0, 2 * DMC, 2):
                vld(sy, i)

        @block.tensor
        def _(t):
            nld = {0: 0, 1: 0}
            for dv in range(2):
                for dmc in range(DMC):
                    i = dv * DMC + dmc
                    nld[i % 2] += 16
                    t.wait_ge(wsem[i % 2].h, bWv[i % 2] + nld[i % 2])
                    if dmc == 0 and dv == 1:
                        t.wait_ge(S["ev"].h, nEVv + 1)
                    last = None
                    for tc in range(TC):
                        last = t.matmul(PS[:, tc, :], xn_T[:, dmc, tc * 128:(tc + 1) * 128],
                                        wstr[:, i % 8, :],
                                        start=(dmc == 0), stop=(dmc == DMC - 1))
                    S["mm"].inc(last, 1)

        @block.scalar
        def _(s):
            for dv in range(2):
                for i in range(dv * DMC + 1, (dv + 1) * DMC, 2):
                    vld(s, i)
                s.wait_ge(S["mm"].h, nMM3 + (dv + 1) * DMC)
                last = None
                for tc in range(TC):
                    dst = v_full[:, tc, :].rearrange("p (h c) -> p h c", c=129)[:, dv * 4:(dv + 1) * 4, 0:128]
                    last = s.copy(dst, PS[:, tc, :].rearrange("p (h e) -> p h e", e=128))
                S["ev"].inc(last, 1)

        @block.gpsimd
        def _(g):
            g.wait_ge(S["ev"].h, S["ev"].n)
            dmaw(g, S["kv"], v_own.rearrange("(c p) n -> p c n", p=128), v_full[:, :, :])
            g.wait_ge(S["kv"].h, S["kv"].n)
            S["cc"].inc(g.collective_compute(
                "AllGather", ALU.bypass,
                replica_groups=[[0, 1, 2, 3], [4, 5, 6, 7]],
                ins=[v_own.opt()], outs=[v_all.opt()]), 1)

        barrier(block, S, waits=[(S["ev"], S["ev"].n)])

        proj_bf(c_wqe, QT)
        barrier(block, S, waits=[(S["ev"], S["ev"].n)])

        nMM5 = S["mm"].n
        nR1 = S["r1"].n

        @block.tensor
        def _(t):
            last = None
            for dmc in range(DMC):
                for tc in range(TC):
                    last = t.matmul(PS[:, tc, 0:8], xn_T[:, dmc, tc * 128:(tc + 1) * 128],
                                    RT_sb[:, dmc, :],
                                    start=(dmc == 0), stop=(dmc == DMC - 1))
            S["mm"].inc(last, 1)

        @block.vector
        def _(v):
            v.wait_ge(S["mm"].h, nMM5 + 1)
            last = None
            for tc in range(TC):
                last = v.tensor_copy(logits[:, tc, :], PS[:, tc, 0:8])
            S["r1"].inc(last, 1)

        @block.tensor
        def _(t):
            t.wait_ge(S["r1"].h, nR1 + 1)
            for half in range(2):
                last = None
                for dmc in range(DMC):
                    last = t.matmul(PS[:, 1 + half, :], AT_sb[:, dmc, :],
                                    xn_T[:, dmc, half * 512:(half + 1) * 512],
                                    start=(dmc == 0), stop=(dmc == DMC - 1))
                S["mm"].inc(last, 1)

        @block.scalar
        def _(s):
            s.wait_ge(S["mm"].h, nMM5 + 3)
            for half in range(2):
                S["ev"].inc(s.copy(low_T[:, half * 512:(half + 1) * 512], PS[:, 1 + half, :]), 1)

        @block.vector
        def _(v):
            vch = S["vch"]
            v.wait_ge(S["r1"].h, S["r1"].n)
            for tc in range(TC):
                vch.inc(v.reduce_max(rmax[:, :], logits[:, tc, :], axis=mybir.AxisListType.X), 1)
                v.wait_ge(vch.h, vch.n)
                S["r1"].inc(v.tensor_scalar(stok_bf[:, tc, :], logits[:, tc, :],
                                            rmax[:, :], rmax[:, :], ALU.is_ge, ALU.mult), 1)

        @block.tensor
        def _(t):
            t.wait_ge(S["r1"].h, S["r1"].n)
            last = None
            for tc in range(TC):
                last = t.transpose(ps_sb8[0:8, tc * 128:(tc + 1) * 128],
                                   stok_bf[:, tc, :], consts["ident_bf"][:, :])
            S["att"].inc(last, 1)

        @block.scalar
        def _(s):
            s.wait_ge(S["att"].h, S["att"].n)
            S["ev"].inc(s.copy(sT[0:8, :], ps_sb8[0:8, :]), 1)

        @block.tensor
        def _(t):
            t.wait_ge(S["ev"].h, S["ev"].n)
            for half in range(2):
                S["mm"].inc(t.matmul(PS[:, 4 + half, :], E_exp[0:8, :],
                                     sT[0:8, half * 512:(half + 1) * 512],
                                     start=True, stop=True), 1)

        @block.vector
        def _(v):
            v.wait_ge(S["mm"].h, S["mm"].n)
            v.wait_ge(S["ev"].h, S["ev"].n)
            last = None
            for half in range(2):
                last = v.tensor_tensor(low_s[:, half * 512:(half + 1) * 512],
                                       low_T[:, half * 512:(half + 1) * 512],
                                       PS[:, 4 + half, :], ALU.mult)
            S["r1"].inc(last, 1)

        barrier(block, S, waits=[(S["r1"], S["r1"].n), (S["cc"], S["cc"].n)])

        if dbg:
            @block.sync
            def _(sy):
                dmaw(sy, S["aux"], d_xnT, xn_T[:, :, :])
                dmaw(sy, S["aux"], d_QT, QT[:, :, :])
                dmaw(sy, S["aux"], d_KT, KT[:, :, :])
                dmaw(sy, S["aux"], d_V, v_full[:, :, :])
                dmaw(sy, S["aux"], d_logits, logits[:, :, :])
                dmaw(sy, S["aux"], d_lows, low_s[:, :])

        kt_view = kt_all.rearrange("(r c p) t -> p r c t", r=4, p=128)
        v_view = v_all.rearrange("(kc p) n -> p kc n", p=128)
        kldD = S["kld"].n
        attD0 = S["att"].n
        bsS1, bpexp1, bpvm1, bep1, batev1 = (S["sS"].n, S["pexp"].n, S["pvm"].n,
                                             S["ep"].n, S["atev"].n)
        s_idx = [0]

        @block.sync
        def _(sy):
            for h in range(8):
                if h >= 2:
                    sy.wait_ge(S["ep"].h, bep1 + (h - 1) * 2)
                dmaw(sy, S["kld"], K_h[:, h % 2, :, :], kt_view[:, :, h, :])
                for q in range(4):
                    dmaw(sy, S["kld"], V_h[:, h % 2, q * 8:(q + 1) * 8, :],
                         v_view[:, q * 8:(q + 1) * 8, 129 * h:129 * (h + 1)])

        @block.tensor
        def _(t):
            def emit_S(h, half, kc, si):
                bk = si % 4
                if si >= 4:
                    t.wait_ge(S["pexp"].h, bpexp1 + si - 3)
                last = t.matmul(PS[:, bk, :],
                                K_h[:, h % 2, kc // 8, (kc % 8) * 128:(kc % 8 + 1) * 128],
                                QT[:, h, half * 512:(half + 1) * 512],
                                start=True, stop=True)
                S["sS"].inc(last, 1)

            def emit_PV(h, half, kc, si):
                if kc == 0:
                    e = h * 2 + half
                    if e > 0:
                        t.wait_ge(S["ep"].h, bep1 + e)
                t.wait_ge(S["pexp"].h, bpexp1 + si + 1)
                for qc in range(4):
                    lastpv = t.matmul(PS[:, 4 + qc, 0:129],
                                      P_sb[:, si % 4, qc * 128:(qc + 1) * 128],
                                      V_h[:, h % 2, kc, :],
                                      start=(kc == 0), stop=(kc == 31))
                S["pvm"].inc(lastpv, 1)
            for h in range(8):
                t.wait_ge(S["kld"].h, kldD + (h + 1) * 80)
                if h >= 1:
                    t.wait_ge(S["atev"].h, batev1 + h)
                pend = []
                for half in range(2):
                    for kc in range(32):
                        si = s_idx[0]
                        emit_S(h, half, kc, si)
                        if si % 2 == 1 and len(pend) >= 2:
                            emit_PV(*pend.pop(0))
                            emit_PV(*pend.pop(0))
                        pend.append((h, half, kc, si))
                        s_idx[0] += 1
                for d in pend:
                    emit_PV(*d)
                t.wait_ge(S["ep"].h, bep1 + (h + 1) * 2)
                t.wait_ge(S["pexp"].h, bpexp1 + (h + 1) * 64)
                last = None
                for i in range(8):
                    last = t.transpose(ps_atp[:, i * 128:(i + 1) * 128],
                                       attn_stage[:, i, :], consts["ident_bf"][:, :])
                S["att"].inc(last, 1)

        @block.scalar
        def _(s):
            si = 0

            def atev_copy(hh):
                s.wait_ge(S["att"].h, attD0 + hh + 1)
                S["atev"].inc(s.copy(attn_T[:, hh, :], ps_atp[:, :]), 1)
            for h in range(8):
                if h >= 1:
                    atev_copy(h - 1)
                for half in range(2):
                    for kc in range(32):
                        bk = si % 4
                        s.wait_ge(S["sS"].h, bsS1 + si + 1)
                        if si >= 4:
                            s.wait_ge(S["pvm"].h, bpvm1 + si - 3)
                        S["pexp"].inc(
                            s.activation(out=P_sb[:, bk, :], in_=PS[:, bk, :],
                                         func=AF.Exp, scale=consts["sc128"][:, :]), 1)
                        si += 1
            atev_copy(7)

        @block.vector
        def _(v):
            vch = S["vch"]
            for h in range(8):
                for half in range(2):
                    e = h * 2 + half
                    v.wait_ge(S["pvm"].h, bpvm1 + (e + 1) * 32)
                    if h >= 1 and half == 0:
                        v.wait_ge(S["att"].h, attD0 + h)
                    last = None
                    for qc in range(4):
                        vch.inc(v.reciprocal(lrecip[:, :], PS[:, 4 + qc, 128:129]), 1)
                        v.wait_ge(vch.h, vch.n)
                        last = v.tensor_scalar(attn_stage[:, half * 4 + qc, :],
                                               PS[:, 4 + qc, 0:128],
                                               lrecip[:, :], None, ALU.mult)
                    S["ep"].inc(last, 1)

        barrier(block, S, waits=[(S["atev"], S["atev"].n)])

        if dbg:
            @block.sync
            def _(sy):
                dmaw(sy, S["aux"], d_attnT, attn_T[:, :, :])

        nW4 = S["w"].n
        nWC4 = S["wc"].n
        nPO = S["po"].n
        bXR1 = S["w2"].n
        bOUT1 = S["out"].n
        bRESV1 = S["resv"].n
        OCN, OW = 8, 512
        wobf = ARb[:, 0:8192].rearrange("p (b c n) -> p b c n", b=2, c=8)

        @block.sync
        def _(sy):
            for oc in range(OCN):
                if oc >= 2:
                    sy.wait_ge(S["po"].h, nPO + (oc - 1) * 8)
                dmaw(sy, S["w"], wobf[:, oc % 2, :, :],
                     c_wo.rearrange("(c p) o -> p c o", p=128)[:, :, oc * OW:(oc + 1) * OW])

        @block.scalar
        def _(s):
            for oc in range(OCN):
                for tc in range(TC):
                    i = oc * 8 + tc
                    if i >= 2:
                        s.wait_ge(S["resv"].h, bRESV1 + i - 1)
                    dmaw(s, S["w2"], x_res[:, i % 2, :],
                         x_ap[tc * 128:(tc + 1) * 128, oc * OW:(oc + 1) * OW])

        @block.vector
        def _(v):
            for oc in range(OCN):
                for tc in range(TC):
                    i = oc * 8 + tc
                    v.wait_ge(S["po"].h, nPO + i + 1)
                    v.wait_ge(S["w2"].h, bXR1 + (i + 1) * 16)
                    if i >= 2:
                        v.wait_ge(S["out"].h, bOUT1 + (i - 1) * 16)
                    S["resv"].inc(v.scalar_tensor_tensor(
                        out=xo_sb[:, i % 2, :], in0=PS[:, i % 4, 0:OW], scalar=1.0,
                        in1=x_res[:, i % 2, :], op0=ALU.mult, op1=ALU.add), 1)

        @block.tensor
        def _(t):
            for oc in range(OCN):
                t.wait_ge(S["w"].h, nW4 + (oc + 1) * 16)
                for tc in range(TC):
                    i = oc * 8 + tc
                    b = i % 4
                    if i >= 4:
                        t.wait_ge(S["resv"].h, bRESV1 + i - 3)
                    for h in range(8):
                        t.matmul(PS[:, b, 0:OW], attn_T[:, h, tc * 128:(tc + 1) * 128],
                                 wobf[:, oc % 2, h, :], start=(h == 0), stop=False)
                    S["po"].inc(t.matmul(PS[:, b, 0:OW], low_s[:, tc * 128:(tc + 1) * 128],
                                         BT_sb[:, oc * OW:(oc + 1) * OW],
                                         start=False, stop=True), 1)

        @block.gpsimd
        def _(g):
            for oc in range(OCN):
                for tc in range(TC):
                    i = oc * 8 + tc
                    g.wait_ge(S["resv"].h, bRESV1 + i + 1)
                    dmaw(g, S["out"], xout[tc * 128:(tc + 1) * 128, oc * OW:(oc + 1) * OW],
                         xo_sb[:, i % 2, :])
            g.wait_ge(S["out"].h, S["out"].n)
            if dbg:
                g.wait_ge(S["aux"].h, S["aux"].n)

        @block.sync
        def _(sy):
            sy.wait_ge(S["out"].h, S["out"].n)

    ctx.close()
    return nc


def _host_layout_l1(inputs, bslice, x1=None):
    A = inputs["L1_ffn_A"]
    AT = np.ascontiguousarray(A.reshape(128, 4096).T)
    Bw = inputs["L1_ffn_Bw"]
    BT = np.ascontiguousarray(Bw.transpose(0, 2, 1).reshape(128, 4096))
    Rw = inputs["L1_ffn_route_w"]
    RT = np.ascontiguousarray(Rw.T.reshape(32, 128, 8).transpose(1, 0, 2).reshape(128, 256))
    d = {
        "wq": np.ascontiguousarray(inputs["L1_attn_wq"]),
        "wk": np.ascontiguousarray(inputs["L1_attn_wk"]),
        "wv": np.ascontiguousarray(inputs["L1_attn_wv"]),
        "wo": np.ascontiguousarray(inputs["L1_attn_wo"]),
        "AT1": AT, "BT1": BT, "RT": RT,
    }
    if x1 is not None:
        d["x"] = np.ascontiguousarray(x1.reshape(2, 4, 1024, 4096)[bslice[0], bslice[1]])
    return d


def build_fused():
    nc = bass.Bass("TRN2", target_bir_lowering=False, debug=False, detect_race_conditions=False)
    octx = ExitStack()
    S = _mk_sems(nc, octx)
    x_in = _dram(nc, "x", [TOK, D], F32, "ExternalInput")
    x1 = _dram(nc, "x1", [TOK, D], F32, "Internal")
    xout = _dram(nc, "xout", [TOK, D], F32, "ExternalOutput")
    prep = {
        "wq": _dram(nc, "wq", [D, D], F32, "ExternalInput"),
        "wk": _dram(nc, "wk", [D, 1024], F32, "ExternalInput"),
        "wv": _dram(nc, "wv", [D, 1024], F32, "ExternalInput"),
        "wo": _dram(nc, "wo", [1024, D], F32, "ExternalInput"),
        "c_wqe": _dram(nc, "c_wqe", [D, 1024], BF16),
        "c_wk": _dram(nc, "c_wk", [D, 1024], BF16),
        "c_wv": _dram(nc, "c_wv", [D, 1024], BF16),
        "c_wo": _dram(nc, "c_wo", [1024, D], BF16),
        "c_wo0": _dram(nc, "c_wo0", [512, D], BF16),
    }
    build_l0(shell=(nc, S), io=(x_in, x1), prep=prep)
    caches = {k: prep[k] for k in ("c_wqe", "c_wk", "c_wv", "c_wo")}
    build_l1(shell=(nc, S), io=(x1, xout), caches=caches, skip_w=True)
    octx.close()
    return nc


def kernel(**inputs):
    nc = build_fused()
    in_maps = []
    for c in range(8):
        m = _host_layout_l0(inputs, divmod(c, 4))
        m.update(_host_layout_l1(inputs, divmod(c, 4)))
        in_maps.append(m)
    res = run_bass_kernel_spmd(nc, in_maps, core_ids=list(range(8)))
    return np.stack([r["xout"] for r in res.results]).reshape(2, 4096, 4096)


# version 17 (final)
# speedup vs baseline: 1.1089x; 1.1089x over previous
import numpy as np
import ml_dtypes
from contextlib import ExitStack
import concourse.bass as bass
import concourse.mybir as mybir
from concourse.bass_utils import run_bass_kernel_spmd

F32 = mybir.dt.float32
BF16 = mybir.dt.bfloat16
AF = mybir.ActivationFunctionType
ALU = mybir.AluOpType

TOK = 1024
D = 4096
DMC = 32
TC = 8
INV_SQRT512 = float(np.float32(1.0 / np.sqrt(512.0)))
INV_SQRT128 = float(np.float32(1.0 / np.sqrt(128.0)))
L0_SHIFT = -40.0


class Sem:
    def __init__(self, nc, ctx, name):
        self.h = ctx.enter_context(nc.semaphore(name))
        self.n = 0

    def inc(self, inst, k):
        inst.then_inc(self.h, k)
        self.n += k
        return inst


def _dram(nc, name, shape, dtype, kind="Internal"):
    return nc.dram_tensor(name, list(shape), dtype, kind=kind).ap()


ENGS = ["sync", "gpsimd", "vector", "scalar", "tensor"]


ENGS_NOG = ["sync", "vector", "scalar", "tensor"]


def barrier(block, S, waits=(), engines=None):
    engines = ENGS_NOG if engines is None else engines
    b = S["bar"]
    target = b.n + len(engines)

    for ename in engines:
        def mk(e):
            b.inc(e.nop(), 1)
            e.wait_ge(b.h, target)
            for (sem, tgt) in waits:
                e.wait_ge(sem.h, tgt)
        getattr(block, ename)(mk)



def dmaw(eng, sem, out, in_):
    eng.wait_ge(sem.h, sem.n)
    return sem.inc(eng.dma_start(out=out, in_=in_), 16)

SEM_NAMES = ["lw", "lwc", "lwo", "out2", "w2", "lbr", "const", "bar", "x", "v1", "sc", "xn", "tp", "tpev",
             "vch", "w", "wg", "wc", "mm", "ev", "kv", "cc", "kld", "sS", "pexp", "pvm",
             "ep", "att", "atev", "r1", "r2", "xr", "po", "resv", "out", "aux"]


def _mk_sems(nc, ctx):
    return {k: Sem(nc, ctx, "s_" + k) for k in SEM_NAMES}


def _alloc_common(nc, ctx, S=None, sfx=""):
    sb = lambda n, shp, dt: ctx.enter_context(nc.sbuf_tensor(n + sfx, shp, dt))
    if S is None:
        S = _mk_sems(nc, ctx)
    return sb, S


def _emit_ln(nc, block, S, x_ap, x_sb, xn_tm, xn_T, ln, ps_tp, consts, cN=1):
    bx, bxn, bv1, bsc, btp, btpev = (S["x"].n, S["xn"].n, S["v1"].n,
                                     S["sc"].n, S["tp"].n, S["tpev"].n)

    @block.sync
    def _(sy):
        for tc in range(TC):
            if tc >= 2:
                sy.wait_ge(S["xn"].h, bxn + tc - 1)
            dmaw(sy, S["x"], x_sb[:, tc % 2, 0:D], x_ap[tc * 128:(tc + 1) * 128, :])

    @block.vector
    def _(v):
        vch = S["vch"]

        def edge(inst):
            vch.inc(inst, 1)
            v.wait_ge(vch.h, vch.n)
        v.wait_ge(S["const"].h, cN)
        for tc in range(TC):
            v.wait_ge(S["x"].h, bx + (tc + 1) * 16)
            for i in range(8):
                last = v.bn_stats(ln["stats"][:, i, :], x_sb[:, tc % 2, i * 512:(i + 1) * 512])
            edge(last)
            edge(v.bn_aggr(ln["mv"][:, :], ln["stats"][:, :, :]))
            v.tensor_tensor(ln["vtmp"][:, :], ln["mv"][:, 1:2], consts["eps"][:, :], ALU.add)
            S["v1"].inc(v.tensor_scalar_mul(ln["negmean"][:, :], ln["mv"][:, 0:1], -1.0), 1)
            v.wait_ge(S["sc"].h, bsc + tc + 1)
            edge(v.reciprocal(ln["rstd"][:, :], ln["sd"][:, :]))
            if tc >= 2:
                v.wait_ge(S["tp"].h, btp + (tc - 1) * 4)
            S["xn"].inc(v.tensor_scalar(xn_tm[:, tc % 2, 0:D], x_sb[:, tc % 2, 0:D],
                                        ln["negmean"][:, :], ln["rstd"][:, :],
                                        ALU.add, ALU.mult), 1)

    @block.tensor
    def _(t):
        t.wait_ge(S["const"].h, cN)
        for tc in range(TC):
            t.wait_ge(S["xn"].h, bxn + tc + 1)
            for grp in range(4):
                gi = tc * 4 + grp
                if gi >= 2:
                    t.wait_ge(S["tpev"].h, btpev + gi - 1)
                last = None
                for i in range(8):
                    dmc = grp * 8 + i
                    last = t.transpose(ps_tp[:, gi % 2, i * 128:(i + 1) * 128],
                                       xn_tm[:, tc % 2, dmc * 128:(dmc + 1) * 128],
                                       consts["ident_bf"][:, :])
                S["tp"].inc(last, 1)

    @block.scalar
    def _(s):
        def do_sqrt(tc):
            s.wait_ge(S["v1"].h, bv1 + tc + 1)
            S["sc"].inc(s.sqrt(ln["sd"][:, :], ln["vtmp"][:, :]), 1)
        do_sqrt(0)
        do_sqrt(1)
        for tc in range(TC):
            for grp in range(4):
                gi = tc * 4 + grp
                s.wait_ge(S["tp"].h, btp + gi + 1)
                dst = xn_T[:, grp * 8:(grp + 1) * 8, tc * 128:(tc + 1) * 128]
                S["tpev"].inc(
                    s.copy(dst, ps_tp[:, gi % 2, :].rearrange("p (c n) -> p c n", c=8)), 1)
            if tc + 2 < TC:
                do_sqrt(tc + 2)


def build_l0(dbg=False, shell=None, io=None, prep=None):
    if shell is None:
        nc = bass.Bass("TRN2", target_bir_lowering=False, debug=False, detect_race_conditions=False)
        S0 = None
    else:
        nc, S0 = shell
    if io is None:
        x_ap = _dram(nc, "x", [TOK, D], F32, "ExternalInput")
        xout = _dram(nc, "xout", [TOK, D], F32, "ExternalOutput")
    else:
        x_ap, xout = io
    wqkT_q = _dram(nc, "wqkT_q", [D, 512], F32, "ExternalInput")
    wqkT_k = _dram(nc, "wqkT_k", [D, 512], F32, "ExternalInput")
    wvT = _dram(nc, "wvT", [D, 512], F32, "ExternalInput")
    woT = _dram(nc, "woT", [512, D], F32, "ExternalInput")
    bq_h = _dram(nc, "bq_h", [4, 128], F32, "ExternalInput")
    bk_h = _dram(nc, "bk_h", [4, 128], F32, "ExternalInput")
    bv_ap = _dram(nc, "bv", [512], F32, "ExternalInput")
    bo_ap = _dram(nc, "bo", [D], F32, "ExternalInput")
    AT_ap = _dram(nc, "AT0", [D, 128], F32, "ExternalInput")
    BT_ap = _dram(nc, "BT0", [128, D], F32, "ExternalInput")
    WrT_ap = _dram(nc, "WrT", [128, 256], F32, "ExternalInput")
    br_ap = _dram(nc, "br", [8], F32, "ExternalInput")

    kt_own_p = [_dram(nc, "kt_own0_%d" % j, [128, TOK], BF16) for j in range(4)]
    kt_all_p = [_dram(nc, "kt_all0_%d" % j, [4 * 128, TOK], BF16) for j in range(4)]
    v_own_p = [_dram(nc, "v_own0_%d" % j, [TOK, 130], BF16) for j in range(4)]
    v_all_p = [_dram(nc, "v_all0_%d" % j, [4 * TOK, 130], BF16) for j in range(4)]

    if dbg:
        d_xnT = _dram(nc, "d_xnT", [128, DMC, TOK], BF16, "ExternalOutput")
        d_QT = _dram(nc, "d_QT", [128, 4, TOK], BF16, "ExternalOutput")
        d_KT = _dram(nc, "d_KT", [128, 4, TOK], BF16, "ExternalOutput")
        d_V = _dram(nc, "d_V", [128, TC, 520], BF16, "ExternalOutput")
        d_logits = _dram(nc, "d_logits", [128, TC, 8], F32, "ExternalOutput")
        d_lows = _dram(nc, "d_lows", [128, TOK], BF16, "ExternalOutput")
        d_attnT = _dram(nc, "d_attnT", [128, 4, TOK], BF16, "ExternalOutput")
        d_xsb = _dram(nc, "d_xsb", [128, D], F32, "ExternalOutput")
        d_mv = _dram(nc, "d_mv", [128, 2], F32, "ExternalOutput")
        d_rstd = _dram(nc, "d_rstd", [128, 1], F32, "ExternalOutput")
        d_xnl = _dram(nc, "d_xnl", [128, D], BF16, "ExternalOutput")
        d_xnT_early = _dram(nc, "d_xnT_early", [128, DMC, TOK], BF16, "ExternalOutput")

    ctx = ExitStack()
    sb, S = _alloc_common(nc, ctx, S0, sfx="_a")
    ps = lambda n, shp, dt: ctx.enter_context(nc.psum_tensor(n + "_a", shp, dt))

    x_sb = sb("x_sb0", [128, 2, 4224], F32)
    xn_tm = sb("xn_tm", [128, 2, D], BF16)
    wfstage = xn_tm.ap().rearrange("p b n -> p (b n)").bitcast(F32)
    xn_T = sb("xn_T", [128, DMC, TOK], BF16)
    ln = {"stats": sb("ln_stats", [128, 8, 6], F32), "mv": sb("ln_mv", [128, 2], F32),
          "vtmp": sb("ln_vtmp", [128, 1], F32), "negmean": sb("ln_negmean", [128, 1], F32),
          "rstd": sb("ln_rstd", [128, 1], F32), "sd": sb("ln_sd", [128, 1], F32)}
    consts = {"ident_bf": sb("ident_bf", [128, 128], BF16),
              "eps": sb("eps", [128, 1], F32),
              "ones_bf": sb("ones_bf", [1, 128], BF16),
              "ones_f32": sb("ones_f32", [1, 128], F32),
              "shift": sb("shift", [128, 1], F32),
              "maskAB": sb("maskAB", [1, 2, 128], BF16)}
    wf32 = sb("wf32", [128, 2, 1024], F32)
    wbf = sb("wbf", [128, 2, 1024], BF16)
    QT = sb("QT", [128, 4, TOK], BF16)
    KT = sb("KT", [128, 4, TOK], BF16)
    v_full = sb("v_full", [128, TC, 520], BF16)
    AT_sb = sb("AT_sb", [128, DMC, 128], BF16)
    BT_sb = sb("BT_sb", [128, D], BF16)
    WrT_sb = sb("WrT_sb", [128, DMC, 8], BF16)
    low_T = sb("low_T", [128, TOK], F32)
    low_s = sb("low_s", [128, TOK], BF16)
    logits = sb("logits", [128, TC, 8], F32)
    sroute = sb("sroute", [128, TC, 8], F32)
    sroute_bf = sb("sroute_bf", [128, TC, 8], BF16)
    sT = sb("sT", [8, TOK], BF16)
    E_exp = sb("E_exp", [8, 128], BF16)
    b_pair_q = sb("b_pair_q", [128, 4], F32)
    b_pair_k = sb("b_pair_k", [128, 4], F32)
    bvrow = sb("bvrow", [1, 512], F32)
    bias_bcast = sb("bias_bcast", [128, 512], F32)
    borow_bf = sb("borow_bf", [1, D], BF16)
    brrow = sb("brrow", [1, 8], F32)
    br_bcast = sb("br_bcast", [128, 8], F32)
    rmax = sb("rmax", [128, 1], F32)
    rnegmax = sb("rnegmax", [128, 1], F32)
    rsum = sb("rsum", [128, 1], F32)
    rrecip = sb("rrecip", [128, 1], F32)
    KVar = x_sb.ap().rearrange("p b n -> p (b n)").bitcast(BF16)
    K_pair = KVar[:, 0:8192].rearrange("p (b r t) -> p b r t", b=2, r=4)
    V_pair = KVar[:, 8192:16512].rearrange("p (b k c) -> p b k c", b=2, k=32)
    P_sb = sb("P_sb", [128, 4, 512], BF16)
    attn_T = sb("attn_T", [128, 4, TOK], BF16)
    lbf_pair = sb("lbf_pair", [1, 4, 512], BF16)
    rcpD = sb("rcpD", [128, 512], F32)
    if prep is not None:
        _pxt = xn_tm.ap().rearrange("p b n -> p (b n)").bitcast(F32).rearrange(
            "p (b n) -> p b n", b=2)
        pf32s = [_pxt[:, 0, :], _pxt[:, 1, :],
                 KT.ap().rearrange("p b n -> p (b n)").bitcast(F32),
                 v_full.ap().rearrange("p b n -> p (b n)").bitcast(F32)[:, 0:2048]]
        pbf = wf32.ap().rearrange("p b n -> p (b n)").bitcast(BF16).rearrange(
            "p (b n) -> p b n", b=2)
        pscr = wbf.ap().rearrange("p b n -> p (b n)").bitcast(F32)[:, 0:512]
    wof32 = x_sb[:, 0, 0:2048].rearrange("p (c n) -> p c n", c=4)
    wobf = x_sb[:, 0, 2048:4096].bitcast(BF16).rearrange("p (b c n) -> p b c n", b=2, c=4)
    x_res = x_sb[:, 1, 0:1024].rearrange("p (b n) -> p b n", b=2)
    xo_sb = x_sb[:, 1, 1024:2048].rearrange("p (b n) -> p b n", b=2)

    PS = ps("PS0", [128, 8, 512], F32)
    ps_tp = PS[:, 0:2, :].rearrange("p b n -> p (b n)").bitcast(BF16).rearrange(
        "p (b n) -> p b n", b=2)
    pvflat = PS[:, 4:6, :].rearrange("p b n -> p (b n)")
    ps_sb8 = PS[:, 7, :].bitcast(BF16)


    with nc.Block() as block:
        @block.gpsimd
        def _(g):
            aux = S["aux"]

            def step(inst):
                aux.inc(inst, 1)
                g.wait_ge(aux.h, aux.n)
            step(g.memset(consts["ident_bf"][:, :], 0.0))
            step(g.affine_select(out=consts["ident_bf"][:, :], in_=consts["ident_bf"][:, :],
                                 compare_op=ALU.not_equal, fill=1.0, base=0,
                                 pattern=[[-1, 128]], channel_multiplier=1))
            step(g.memset(consts["eps"][:, :], 1e-5))
            step(g.memset(consts["ones_bf"][:, :], 1.0))
            step(g.memset(consts["ones_f32"][:, :], 1.0))
            step(g.memset(consts["shift"][:, :], L0_SHIFT))
            step(g.memset(consts["maskAB"][:, :, :], 0.0))
            step(g.memset(consts["maskAB"][:, 0, 0:64], 1.0))
            step(g.memset(consts["maskAB"][:, 1, 64:128], 1.0))
            step(g.memset(E_exp[:, :], 1.0))
            step(g.affine_select(out=E_exp[:, :], in_=E_exp[:, :],
                                 compare_op=ALU.is_ge, fill=0.0, base=0,
                                 pattern=[[1, 128]], channel_multiplier=-16))
            step(g.affine_select(out=E_exp[:, :], in_=E_exp[:, :],
                                 compare_op=ALU.is_ge, fill=0.0, base=15,
                                 pattern=[[-1, 128]], channel_multiplier=16))
            step(g.memset(v_full[:, :, :].rearrange("p t (h c) -> p t h c", c=65)[:, :, :, 64:65], 1.0))
            S["const"].inc(g.nop(), 1)

        _emit_ln(nc, block, S, x_ap, x_sb, xn_tm, xn_T, ln, ps_tp, consts, cN=S["const"].n)

        @block.gpsimd
        def _(g):
            for j in range(4):
                dmaw(g, S["wg"], b_pair_q[:, j:j + 1], bq_h[j:j + 1, :].rearrange("a p -> p a"))
                dmaw(g, S["wg"], b_pair_k[:, j:j + 1], bk_h[j:j + 1, :].rearrange("a p -> p a"))
            dmaw(g, S["wg"], bvrow[:, :], bv_ap.rearrange("(a n) -> a n", a=1))
            dmaw(g, S["wg"], brrow[:, :], br_ap.rearrange("(a n) -> a n", a=1))

        prep_casts = []
        PRE = 32
        if prep is not None:
            blw, blwc, blwo = S["lw"].n, S["lwc"].n, S["lwo"].n
            prep["wo0"] = woT
            plan = []
            for dmc in range(DMC):
                plan.append(("wk", dmc * 128, 0, 1024, "cast"))
            for dmc in range(DMC):
                plan.append(("wv", dmc * 128, 0, 1024, "cast"))
            for c4 in range(4):
                for ch in range(2):
                    plan.append(("wo0", c4 * 128, ch * 2048, 2048, "cast"))
            prep["lwo_wo0"] = blwo + 72 * 16
            prep["lwo_wk"] = blwo + 32 * 16
            prep["lwo_wv"] = blwo + 64 * 16
            prep["lwo_wq"] = blwo + 136 * 16
            prep["lwo_all"] = blwo + 152 * 16
            for dmc in range(DMC):
                for ch in range(2):
                    plan.append(("wq", dmc * 128, ch * 2048, 2048, "presum"))
            for c8 in range(8):
                for ch in range(2):
                    plan.append(("wo", c8 * 128, ch * 2048, 2048, "cast"))

            pre_f32 = [attn_T.ap().rearrange("p b n -> p (b n)").bitcast(F32)[:, 0:1024],
                       attn_T.ap().rearrange("p b n -> p (b n)").bitcast(F32)[:, 1024:2048],
                       P_sb.ap().rearrange("p b n -> p (b n)").bitcast(F32)]
            pre_bf = rcpD.ap().bitcast(BF16)

            def _dst(w, r0, c0, ncols, kind):
                if kind == "presum":
                    return prep["c_wqe"][r0:r0 + 128, (c0 // 2048) * 512:(c0 // 2048) * 512 + 512], 512
                return prep["c_" + w][r0:r0 + 128, c0:c0 + ncols], ncols

            @block.gpsimd
            def _(g):
                for k in range(PRE):
                    w, r0, c0, ncols, kind = plan[k]
                    if k >= 3:
                        g.wait_ge(S["lwc"].h, blwc + k - 2)
                    dmaw(g, S["lw"], pre_f32[k % 3][:, 0:ncols], prep[w][r0:r0 + 128, c0:c0 + ncols])
                    if k >= 1:
                        dst, dn = _dst(*plan[k - 1])
                        g.wait_ge(S["lwc"].h, blwc + k)
                        dmaw(g, S["lwo"], dst, pre_bf[:, 0:dn])
                dst, dn = _dst(*plan[PRE - 1])
                g.wait_ge(S["lwc"].h, blwc + PRE)
                dmaw(g, S["lwo"], dst, pre_bf[:, 0:dn])

            def mk_cast(k, w, r0, c0, ncols, kind):
                def go(v):
                    vch = S["vch"]
                    v.wait_ge(S["lw"].h, blw + (k + 1) * 16)
                    if k < PRE:
                        if k >= 1:
                            v.wait_ge(S["lwo"].h, blwo + k * 16)
                        S["lwc"].inc(v.tensor_copy(pre_bf[:, 0:ncols],
                                                   pre_f32[k % 3][:, 0:ncols]), 1)
                        return
                    if k >= 2:
                        v.wait_ge(S["lwo"].h, blwo + (k - 1) * 16)
                    if kind == "presum":
                        wv4 = pf32s[k % 4].rearrange("p (kv g c) -> p kv g c", kv=4, g=4)
                        scr = pscr.rearrange("p (kv c) -> p kv c", kv=4)
                        vch.inc(v.tensor_tensor(scr, wv4[:, :, 0, :], wv4[:, :, 1, :], ALU.add), 1)
                        v.wait_ge(vch.h, vch.n)
                        vch.inc(v.tensor_tensor(scr, scr, wv4[:, :, 2, :], ALU.add), 1)
                        v.wait_ge(vch.h, vch.n)
                        S["lwc"].inc(v.tensor_tensor(
                            pbf[:, k % 2, 0:512].rearrange("p (kv c) -> p kv c", kv=4),
                            scr, wv4[:, :, 3, :], ALU.add), 1)
                    else:
                        S["lwc"].inc(v.tensor_copy(pbf[:, k % 2, 0:ncols], pf32s[k % 4][:, 0:ncols]), 1)
                return go
            prep_casts = [mk_cast(k, *cfg) for k, cfg in enumerate(plan)]

        _ci = [0]

        def drain_casts_on(v, n):
            while _ci[0] < len(prep_casts) and n > 0:
                prep_casts[_ci[0]](v)
                _ci[0] += 1
                n -= 1

        barrier(block, S, waits=[(S["tpev"], S["tpev"].n), (S["const"], S["const"].n)])

        if dbg:
            @block.gpsimd
            def _(g):
                dmaw(g, S["kv"], d_xnT_early, xn_T[:, :, :])
                g.wait_ge(S["kv"].h, S["kv"].n)

        bWC2 = S["wc"].n
        bMM2 = S["mm"].n

        @block.sync
        def _(sy):
            dmaw(sy, S["w"], wfstage.rearrange("p (c n) -> p c n", c=DMC), AT_ap.rearrange("(c p) n -> p c n", p=128))

        @block.vector
        def _(v):
            v.wait_ge(S["w"].h, S["w"].n)
            S["wc"].inc(v.tensor_copy(AT_sb[:, :, :],
                                      wfstage.rearrange("p (c n) -> p c n", c=DMC)), 1)

        @block.sync
        def _(sy):
            sy.wait_ge(S["wc"].h, bWC2 + 1)
            dmaw(sy, S["w"], wfstage, BT_ap)

        @block.vector
        def _(v):
            v.wait_ge(S["w"].h, S["w"].n)
            S["wc"].inc(v.tensor_copy(BT_sb[:, :], wfstage), 1)

        @block.sync
        def _(sy):
            sy.wait_ge(S["wc"].h, bWC2 + 2)
            dmaw(sy, S["w"], wfstage[:, 0:256], WrT_ap[:, :])
            sy.wait_ge(S["wc"].h, bWC2 + 3)
            dmaw(sy, S["w"], wfstage[0:1, 0:D], bo_ap.rearrange("(a n) -> a n", a=1))

        @block.vector
        def _(v):
            v.wait_ge(S["w"].h, S["w"].n - 16)
            S["wc"].inc(v.tensor_copy(WrT_sb[:, :, :],
                                      wfstage[:, 0:256].rearrange("p (c e) -> p c e", c=DMC)), 1)
            v.wait_ge(S["w"].h, S["w"].n)
            S["wc"].inc(v.tensor_copy(borow_bf[:, :], wfstage[0:1, 0:D]), 1)

        @block.tensor
        def _(t):
            t.wait_ge(S["w"].h, S["w"].n)
            t.wait_ge(S["wg"].h, S["wg"].n)
            t.matmul(PS[:, 3, 0:512], consts["ones_f32"][0:1, :], bvrow[0:1, :],
                     start=True, stop=True)
            S["mm"].inc(t.matmul(PS[:, 2, 0:8], consts["ones_f32"][0:1, :], brrow[0:1, :],
                                 start=True, stop=True), 1)

        @block.scalar
        def _(s):
            s.wait_ge(S["mm"].h, bMM2 + 1)
            s.copy(bias_bcast[:, :], PS[:, 3, 0:512])
            S["ev"].inc(s.copy(br_bcast[:, :], PS[:, 2, 0:8]), 1)

        barrier(block, S, waits=[(S["wc"], S["wc"].n), (S["ev"], S["ev"].n), (S["wg"], S["wg"].n)])

        nMM = S["mm"].n
        nEV = S["ev"].n
        bWC1 = S["wc"].n
        wf32s = wf32.ap().rearrange("p b n -> p (b n)").rearrange("p (b n) -> p b n", b=4)
        wbfs = wbf.ap().rearrange("p b n -> p (b n)").rearrange("p (b n) -> p b n", b=4)
        wsem = {0: S["w"], 1: S["w2"]}
        bW1 = {0: S["w"].n, 1: S["w2"].n}

        def c1_load(eng, i):
            og, dmc = divmod(i, DMC)
            if i >= 4:
                eng.wait_ge(S["wc"].h, bWC1 + i - 3)
            q = wsem[i % 2]
            dmaw(eng, q, wf32s[:, i % 4, 0:256],
                 wqkT_q[dmc * 128:(dmc + 1) * 128, og * 256:(og + 1) * 256])
            dmaw(eng, q, wf32s[:, i % 4, 256:512],
                 wqkT_k[dmc * 128:(dmc + 1) * 128, og * 256:(og + 1) * 256])

        @block.sync
        def _(sy):
            for i in range(0, 2 * DMC, 2):
                c1_load(sy, i)

        @block.vector
        def _(v):
            nld = {0: 0, 1: 0}
            for i in range(2 * DMC):
                nld[i % 2] += 32
                v.wait_ge(wsem[i % 2].h, bW1[i % 2] + nld[i % 2])
                if i >= 4:
                    v.wait_ge(S["mm"].h, nMM + i - 3)
                S["wc"].inc(v.tensor_copy(wbfs[:, i % 4, :], wf32s[:, i % 4, :]), 1)
                if prep is not None and i % 2 == 1:
                    drain_casts_on(v, 1)

        @block.tensor
        def _(t):
            for i in range(2 * DMC):
                og, dmc = divmod(i, DMC)
                t.wait_ge(S["wc"].h, bWC1 + i + 1)
                if dmc == 0 and og == 1:
                    t.wait_ge(S["ev"].h, nEV + 8)
                last = None
                for j2 in range(2):
                    for half in range(2):
                        t.matmul(PS[:, j2 * 2 + half, :], wbfs[:, i % 4, j2 * 128:(j2 + 1) * 128],
                                 xn_T[:, dmc, half * 512:(half + 1) * 512],
                                 start=(dmc == 0), stop=(dmc == DMC - 1))
                        last = t.matmul(PS[:, 4 + j2 * 2 + half, :],
                                        wbfs[:, i % 4, 256 + j2 * 128:256 + (j2 + 1) * 128],
                                        xn_T[:, dmc, half * 512:(half + 1) * 512],
                                        start=(dmc == 0), stop=(dmc == DMC - 1))
                S["mm"].inc(last, 1)

        @block.scalar
        def _(s):
            for og in range(2):
                for i in range(og * DMC + 1, (og + 1) * DMC, 2):
                    c1_load(s, i)
                s.wait_ge(S["mm"].h, nMM + (og + 1) * DMC)
                last = None
                for j2 in range(2):
                    j = og * 2 + j2
                    for half in range(2):
                        s.activation(out=QT[:, j, half * 512:(half + 1) * 512],
                                     in_=PS[:, j2 * 2 + half, :],
                                     func=AF.Identity, bias=b_pair_q[:, j:j + 1], scale=1.0)
                        last = s.activation(out=KT[:, j, half * 512:(half + 1) * 512],
                                            in_=PS[:, 4 + j2 * 2 + half, :],
                                            func=AF.Identity, bias=b_pair_k[:, j:j + 1], scale=1.0)
                S["ev"].inc(last, 8)

        barrier(block, S, waits=[(S["ev"], S["ev"].n)])

        nEVk0 = S["ev"].n

        ncc0 = S["cc"].n
        CC_K = [ncc0 + 1, ncc0 + 2, ncc0 + 4, ncc0 + 5]
        CC_V = [ncc0 + 3, ncc0 + 6, ncc0 + 7, ncc0 + 8]

        @block.gpsimd
        def _(g):
            for j in range(4):
                g.wait_ge(S["ev"].h, nEVk0 - 16 + (8 if j < 2 else 16))
                dmaw(g, S["kv"], kt_own_p[j], KT[:, j, :])
            g.wait_ge(S["kv"].h, S["kv"].n)
            for j in range(2):
                S["cc"].inc(g.collective_compute(
                    "AllGather", ALU.bypass,
                    replica_groups=[[0, 1, 2, 3], [4, 5, 6, 7]],
                    ins=[kt_own_p[j].opt()], outs=[kt_all_p[j].opt()]), 1)

        nWC2 = S["wc"].n
        nMM2 = S["mm"].n
        bW3 = {0: S["w"].n, 1: S["w2"].n}

        def c3_load(eng, dmc):
            if dmc >= 4:
                eng.wait_ge(S["wc"].h, nWC2 + dmc - 3)
            dmaw(eng, wsem[dmc % 2], wf32s[:, dmc % 4, 0:512],
                 wvT[dmc * 128:(dmc + 1) * 128, :])

        @block.sync
        def _(sy):
            for dmc in range(0, DMC, 2):
                c3_load(sy, dmc)

        @block.scalar
        def _(s):
            for dmc in range(1, DMC, 2):
                c3_load(s, dmc)

        @block.vector
        def _(v):
            nld = {0: 0, 1: 0}
            for dmc in range(DMC):
                nld[dmc % 2] += 16
                v.wait_ge(wsem[dmc % 2].h, bW3[dmc % 2] + nld[dmc % 2])
                if dmc >= 4:
                    v.wait_ge(S["mm"].h, nMM2 + dmc - 3)
                S["wc"].inc(v.tensor_copy(wbfs[:, dmc % 4, :], wf32s[:, dmc % 4, :]), 1)

        @block.tensor
        def _(t):
            for dmc in range(DMC):
                t.wait_ge(S["wc"].h, nWC2 + dmc + 1)
                last = None
                for tc in range(TC):
                    last = t.matmul(PS[:, tc, :], xn_T[:, dmc, tc * 128:(tc + 1) * 128],
                                    wbfs[:, dmc % 4, :],
                                    start=(dmc == 0), stop=(dmc == DMC - 1))
                S["mm"].inc(last, 1)

        @block.vector
        def _(v):
            v.wait_ge(S["mm"].h, nMM2 + DMC)
            last = None
            for tc in range(TC):
                dst = v_full[:, tc, :].rearrange("p (h c) -> p h c", c=65)[:, :, 0:64]
                last = v.scalar_tensor_tensor(
                    out=dst, in0=PS[:, tc, :].rearrange("p (h e) -> p h e", e=64),
                    scalar=1.0, in1=bias_bcast[:, :].rearrange("p (h e) -> p h e", e=64),
                    op0=ALU.mult, op1=ALU.add)
            S["ev"].inc(last, 1)

        @block.gpsimd
        def _(g):
            g.wait_ge(S["ev"].h, S["ev"].n)
            for j in range(4):
                dmaw(g, S["kv"], v_own_p[j].rearrange("(c p) n -> p c n", p=128),
                     v_full[:, :, 130 * j:130 * (j + 1)])
            g.wait_ge(S["kv"].h, S["kv"].n)

            def ag(src, dst):
                S["cc"].inc(g.collective_compute(
                    "AllGather", ALU.bypass,
                    replica_groups=[[0, 1, 2, 3], [4, 5, 6, 7]],
                    ins=[src.opt()], outs=[dst.opt()]), 1)
            ag(v_own_p[0], v_all_p[0])
            ag(kt_own_p[2], kt_all_p[2])
            ag(kt_own_p[3], kt_all_p[3])
            for j in range(1, 4):
                ag(v_own_p[j], v_all_p[j])

        barrier(block, S, waits=[(S["ev"], S["ev"].n)])

        if prep is not None:
            @block.gpsimd
            def _(g):
                for k in range(PRE, len(plan)):
                    w, r0, c0, ncols, kind = plan[k]
                    if k >= PRE + 4:
                        g.wait_ge(S["lwc"].h, blwc + k - 3)
                    dmaw(g, S["lw"], pf32s[k % 4][:, 0:ncols], prep[w][r0:r0 + 128, c0:c0 + ncols])
                    if k >= PRE + 1:
                        pw = k - 1
                        dst, dn = _dst(*plan[pw])
                        g.wait_ge(S["lwc"].h, blwc + pw + 1)
                        dmaw(g, S["lwo"], dst, pbf[:, pw % 2, 0:dn])
                pw = len(plan) - 1
                dst, dn = _dst(*plan[pw])
                g.wait_ge(S["lwc"].h, blwc + pw + 1)
                dmaw(g, S["lwo"], dst, pbf[:, pw % 2, 0:dn])

        nMM3 = S["mm"].n
        nR1c = S["r1"].n

        @block.tensor
        def _(t):
            last = None
            for dmc in range(DMC):
                for tc in range(TC):
                    last = t.matmul(PS[:, tc, 0:8], xn_T[:, dmc, tc * 128:(tc + 1) * 128],
                                    WrT_sb[:, dmc, :],
                                    start=(dmc == 0), stop=(dmc == DMC - 1))
            S["mm"].inc(last, 1)

        @block.vector
        def _(v):
            v.wait_ge(S["mm"].h, nMM3 + 1)
            last = None
            for tc in range(TC):
                last = v.tensor_tensor(logits[:, tc, :], PS[:, tc, 0:8],
                                       br_bcast[:, :], ALU.add)
            S["r1"].inc(last, 1)

        @block.tensor
        def _(t):
            t.wait_ge(S["r1"].h, nR1c + 1)
            for half in range(2):
                last = None
                for dmc in range(DMC):
                    last = t.matmul(PS[:, 1 + half, :], AT_sb[:, dmc, :],
                                    xn_T[:, dmc, half * 512:(half + 1) * 512],
                                    start=(dmc == 0), stop=(dmc == DMC - 1))
                S["mm"].inc(last, 1)

        @block.scalar
        def _(s):
            s.wait_ge(S["mm"].h, nMM3 + 3)
            for half in range(2):
                S["ev"].inc(s.copy(low_T[:, half * 512:(half + 1) * 512], PS[:, 1 + half, :]), 1)

        @block.vector
        def _(v):
            v.wait_ge(S["r1"].h, S["r1"].n)
            vch = S["vch"]
            bR2 = S["r2"].n
            for tc in range(TC):
                vch.inc(v.reduce_max(rmax[:, :], logits[:, tc, :], axis=mybir.AxisListType.X), 1)
                v.wait_ge(vch.h, vch.n)
                S["r1"].inc(v.tensor_scalar_mul(rnegmax[:, :], rmax[:, :], -1.0), 1)
                v.wait_ge(S["r2"].h, bR2 + tc + 1)
                vch.inc(v.reciprocal(rrecip[:, :], rsum[:, :]), 1)
                v.wait_ge(vch.h, vch.n)
                S["r1"].inc(v.tensor_scalar(sroute_bf[:, tc, :], sroute[:, tc, :],
                                            rrecip[:, :], 16.0, ALU.mult, ALU.mult), 1)

        @block.scalar
        def _(s):
            base = S["r1"].n - 2 * TC
            for tc in range(TC):
                s.wait_ge(S["r1"].h, base + 2 * tc + 1)
                S["r2"].inc(s.activation(out=sroute[:, tc, :], in_=logits[:, tc, :],
                                         func=AF.Exp, bias=rnegmax[:, :], scale=1.0,
                                         accum_out=rsum[:, :]), 1)

        @block.tensor
        def _(t):
            t.wait_ge(S["r1"].h, S["r1"].n)
            last = None
            for tc in range(TC):
                last = t.transpose(ps_sb8[0:8, tc * 128:(tc + 1) * 128],
                                   sroute_bf[:, tc, :], consts["ident_bf"][:, :])
            S["att"].inc(last, 1)

        @block.scalar
        def _(s):
            s.wait_ge(S["att"].h, S["att"].n)
            S["ev"].inc(s.copy(sT[0:8, :], ps_sb8[0:8, :]), 1)

        @block.tensor
        def _(t):
            t.wait_ge(S["ev"].h, S["ev"].n)
            for half in range(2):
                S["mm"].inc(t.matmul(PS[:, 4 + half, :], E_exp[0:8, :],
                                     sT[0:8, half * 512:(half + 1) * 512],
                                     start=True, stop=True), 1)

        @block.vector
        def _(v):
            v.wait_ge(S["mm"].h, S["mm"].n)
            v.wait_ge(S["ev"].h, S["ev"].n)
            last = None
            for half in range(2):
                last = v.tensor_tensor(low_s[:, half * 512:(half + 1) * 512],
                                       low_T[:, half * 512:(half + 1) * 512],
                                       PS[:, 4 + half, :], ALU.mult)
            S["r1"].inc(last, 1)

        barrier(block, S, waits=[(S["r1"], S["r1"].n)])

        if dbg:
            @block.sync
            def _(sy):
                dmaw(sy, S["aux"], d_xnT, xn_T[:, :, :])
                dmaw(sy, S["aux"], d_QT, QT[:, :, :])
                dmaw(sy, S["aux"], d_KT, KT[:, :, :])
                dmaw(sy, S["aux"], d_V, v_full[:, :, :])
                dmaw(sy, S["aux"], d_logits, logits[:, :, :])
                dmaw(sy, S["aux"], d_lows, low_s[:, :])
                dmaw(sy, S["aux"], d_xsb, x_sb[:, 1, 0:D])
                dmaw(sy, S["aux"], d_mv, ln["mv"][:, :])
                dmaw(sy, S["aux"], d_rstd, ln["rstd"][:, :])
                dmaw(sy, S["aux"], d_xnl, xn_tm[:, 1, 0:D])

        kt_views = [ka.rearrange("(r p) t -> p r t", p=128) for ka in kt_all_p]
        v_views = [va.rearrange("(kc p) n -> p kc n", p=128) for va in v_all_p]
        cnt = {"kld": S["kld"].n}
        bxrD = S["xr"].n
        bsS, bpexp, bpvm, bep, batev, blbr, batt = (S["sS"].n, S["pexp"].n, S["pvm"].n,
                                                    S["ep"].n, S["atev"].n, S["lbr"].n,
                                                    S["att"].n)
        s_idx = [0]

        @block.sync
        def _(sy):
            for j in range(4):
                if j >= 2:
                    sy.wait_ge(S["pvm"].h, bpvm + (j - 1) * 128)
                sy.wait_ge(S["cc"].h, CC_K[j])
                dmaw(sy, S["kld"], K_pair[:, j % 2, :, :], kt_views[j][:, :, :])
                sy.wait_ge(S["cc"].h, CC_V[j])
                for q in range(4):
                    dmaw(sy, S["xr"], V_pair[:, j % 2, q * 8:(q + 1) * 8, :],
                         v_views[j][:, q * 8:(q + 1) * 8, :])

        @block.tensor
        def _(t):
            def emit_S(j, kc, u, si):
                hh, qh = divmod(u, 2)
                S["sS"].inc(t.matmul(
                    PS[:, si % 4, :],
                    K_pair[hh * 64:(hh + 1) * 64, j % 2, kc // 8, (kc % 8) * 128:(kc % 8 + 1) * 128],
                    QT[hh * 64:(hh + 1) * 64, j, qh * 512:(qh + 1) * 512],
                    start=True, stop=True, tile_position=(hh * 64, 0)), 1)

            def emit_PV(j, kc, u, si):
                hh = u // 2
                S["pvm"].inc(t.matmul(
                    PS[0:65, 4 + u, :],
                    V_pair[:, j % 2, kc, hh * 65:(hh + 1) * 65],
                    P_sb[:, si % 4, :],
                    start=(kc == 0), stop=(kc == 31)), 1)

            for j in range(4):
                t.wait_ge(S["kld"].h, cnt["kld"] + (j + 1) * 16)
                base_si = s_idx[0]
                for kc in range(32):
                    if s_idx[0] >= 4:
                        t.wait_ge(S["pexp"].h, bpexp + s_idx[0])
                    for u in range(4):
                        emit_S(j, kc, u, s_idx[0])
                        s_idx[0] += 1
                    if kc >= 1:
                        if kc == 1:
                            t.wait_ge(S["ep"].h, bep + j)
                            t.wait_ge(S["lbr"].h, blbr + j * 2)
                            t.wait_ge(S["xr"].h, bxrD + (j + 1) * 64)
                        t.wait_ge(S["pexp"].h, bpexp + base_si + 4 * kc)
                        for u in range(4):
                            emit_PV(j, kc - 1, u, base_si + 4 * (kc - 1) + u)
                t.wait_ge(S["pexp"].h, bpexp + base_si + 128)
                for u in range(4):
                    emit_PV(j, 31, u, base_si + 4 * 31 + u)
                for qh in range(2):
                    t.wait_ge(S["ep"].h, bep + j + 1)
                    t.matmul(PS[:, 4 + qh, :], consts["maskAB"][:, 0, :], lbf_pair[:, qh, :],
                             start=True, stop=False)
                    S["atev"].inc(t.matmul(PS[:, 4 + qh, :], consts["maskAB"][:, 1, :],
                                           lbf_pair[:, 2 + qh, :],
                                           start=False, stop=True), 1)

        @block.scalar
        def _(s):
            si = 0
            for j in range(4):
                for kc in range(32):
                    for u in range(4):
                        s.wait_ge(S["sS"].h, bsS + si + 1)
                        if si >= 4:
                            s.wait_ge(S["pvm"].h, bpvm + si - 3)
                        S["pexp"].inc(s.activation(
                            out=P_sb[:, si % 4, :], in_=PS[:, si % 4, :],
                            func=AF.Exp, bias=consts["shift"][:, :], scale=1.0), 1)
                        si += 1

        @block.vector
        def _(v):
            for j in range(4):
                drain_casts_on(v, 16)
                v.wait_ge(S["pvm"].h, bpvm + (j + 1) * 128)
                v.tensor_copy(attn_T[0:64, j, 0:512], PS[0:64, 4, :])
                v.tensor_copy(attn_T[0:64, j, 512:1024], PS[0:64, 5, :])
                v.tensor_copy(attn_T[64:128, j, 0:512], PS[0:64, 6, :])
                v.tensor_copy(attn_T[64:128, j, 512:1024], PS[0:64, 7, :])
                last = None
                for u in range(4):
                    last = v.tensor_copy(lbf_pair[:, u, :], PS[64:65, 4 + u, :])
                S["ep"].inc(last, 1)
                v.wait_ge(S["ep"].h, S["ep"].n)
                for qh in range(2):
                    v.wait_ge(S["atev"].h, batev + j * 2 + qh + 1)
                    S["lbr"].inc(v.reciprocal(rcpD[:, :], PS[:, 4 + qh, :]), 1)
                    v.wait_ge(S["lbr"].h, S["lbr"].n)
                    inst = v.scalar_tensor_tensor(
                        out=attn_T[:, j, qh * 512:(qh + 1) * 512], in0=rcpD[:, :],
                        scalar=INV_SQRT512, in1=attn_T[:, j, qh * 512:(qh + 1) * 512],
                        op0=ALU.mult, op1=ALU.mult)
                    if qh == 1:
                        S["att"].inc(inst, 1)

        barrier(block, S, waits=[(S["att"], S["att"].n)])

        if dbg:
            @block.sync
            def _(sy):
                dmaw(sy, S["aux"], d_attnT, attn_T[:, :, :])

        nW3 = S["w"].n
        nWC3 = S["wc"].n
        nPO = S["po"].n
        bXR = S["xr"].n
        bOUT = S["out2"].n
        bRESV = S["resv"].n

        @block.sync
        def _(sy):
            def load_wo(oc):
                if prep is not None:
                    if oc == 0:
                        sy.wait_ge(S["lwo"].h, prep["lwo_wo0"])
                    if oc >= 2:
                        sy.wait_ge(S["po"].h, nPO + (oc - 1) * 8)
                    dmaw(sy, S["w"], wobf[:, oc % 2, :, :],
                         prep["c_wo0"].rearrange("(c p) o -> p c o", p=128)[:, :, oc * 512:(oc + 1) * 512])
                else:
                    dmaw(sy, S["w"], wof32[:, :, :],
                         woT.rearrange("(c p) o -> p c o", p=128)[:, :, oc * 512:(oc + 1) * 512])
            load_wo(0)
            for oc in range(8):
                for tc in range(TC):
                    i = oc * 8 + tc
                    if i >= 2:
                        sy.wait_ge(S["resv"].h, bRESV + i - 1)
                    dmaw(sy, S["xr"], x_res[:, i % 2, :], x_ap[tc * 128:(tc + 1) * 128, oc * 512:(oc + 1) * 512])
                    if tc == 1 and oc + 1 < 8 and prep is not None:
                        load_wo(oc + 1)
                    if tc == 1 and oc + 1 < 8 and prep is None:
                        sy.wait_ge(S["wc"].h, nWC3 + oc + 1)
                        load_wo(oc + 1)

        @block.vector
        def _(v):
            def cast(oc):
                v.wait_ge(S["w"].h, nW3 + (oc + 1) * 16)
                if oc >= 2:
                    v.wait_ge(S["po"].h, nPO + (oc - 1) * 8)
                S["wc"].inc(v.tensor_copy(wobf[:, oc % 2, :, :], wof32[:, :, :]), 1)
            if prep is None:
                cast(0)
                cast(1)
            for oc in range(8):
                for tc in range(TC):
                    i = oc * 8 + tc
                    v.wait_ge(S["po"].h, nPO + i + 1)
                    v.wait_ge(S["xr"].h, bXR + (i + 1) * 16)
                    if i >= 2:
                        v.wait_ge(S["out2"].h, bOUT + (i - 1) * 16)
                    S["resv"].inc(v.scalar_tensor_tensor(
                        out=xo_sb[:, i % 2, :], in0=PS[:, i % 4, :], scalar=1.0,
                        in1=x_res[:, i % 2, :], op0=ALU.mult, op1=ALU.add), 1)
                    if prep is not None:
                        drain_casts_on(v, 1)
                if oc + 2 < 8 and prep is None:
                    cast(oc + 2)
            if prep is not None:
                drain_casts_on(v, 10 ** 9)

        @block.tensor
        def _(t):
            for oc in range(8):
                if prep is not None:
                    t.wait_ge(S["w"].h, nW3 + (oc + 1) * 16)
                else:
                    t.wait_ge(S["wc"].h, nWC3 + oc + 1)
                for tc in range(TC):
                    i = oc * 8 + tc
                    b = i % 4
                    if i >= 4:
                        t.wait_ge(S["resv"].h, bRESV + i - 3)
                    for jj in range(4):
                        t.matmul(PS[:, b, :], attn_T[:, jj, tc * 128:(tc + 1) * 128],
                                 wobf[:, oc % 2, jj, :], start=(jj == 0), stop=False)
                    t.matmul(PS[:, b, :], low_s[:, tc * 128:(tc + 1) * 128],
                             BT_sb[:, oc * 512:(oc + 1) * 512], start=False, stop=False)
                    S["po"].inc(t.matmul(PS[:, b, :], consts["ones_bf"][0:1, :],
                                         borow_bf[0:1, oc * 512:(oc + 1) * 512],
                                         start=False, stop=True), 1)

        @block.scalar
        def _(sc):
            for oc in range(8):
                for tc in range(TC):
                    i = oc * 8 + tc
                    sc.wait_ge(S["resv"].h, bRESV + i + 1)
                    dmaw(sc, S["out2"], xout[tc * 128:(tc + 1) * 128, oc * 512:(oc + 1) * 512],
                         xo_sb[:, i % 2, :])
            sc.wait_ge(S["out2"].h, S["out2"].n)

        @block.gpsimd
        def _(g):
            g.wait_ge(S["out2"].h, S["out2"].n)
            if dbg:
                g.wait_ge(S["aux"].h, S["aux"].n)

        @block.sync
        def _(sy):
            sy.wait_ge(S["out"].h, S["out"].n)

    ctx.close()
    return nc


def _host_layout_l0(inputs, bslice):
    Wqk = np.ascontiguousarray(inputs["L0_attn_Wqk_w"])
    rows = np.arange(1024)
    h = rows // 128
    qrows = np.empty(512, np.int64)
    krows = np.empty(512, np.int64)
    for j in range(4):
        for hh in range(2):
            hlin = 2 * j + hh
            e = np.arange(64)
            qrows[j * 128 + hh * 64 + e] = hlin * 128 + e * 2 + 0
            krows[j * 128 + hh * 64 + e] = hlin * 128 + e * 2 + 1
    wqkT_q = np.ascontiguousarray(Wqk[qrows, :].T)
    wqkT_k = np.ascontiguousarray(Wqk[krows, :].T)
    bq_h = np.ascontiguousarray(inputs["L0_attn_Wqk_b"][qrows].reshape(4, 128))
    bk_h = np.ascontiguousarray(inputs["L0_attn_Wqk_b"][krows].reshape(4, 128))
    wvT = np.ascontiguousarray(inputs["L0_attn_Wv_w"].T)
    woT = np.ascontiguousarray(inputs["L0_attn_Wo_w"].T)
    A = inputs["L0_ffn_A"]
    AT = np.ascontiguousarray(A.reshape(128, 4096).T)
    B = inputs["L0_ffn_B"]
    BT = np.ascontiguousarray(B.transpose(0, 2, 1).reshape(128, 4096))
    Wr = inputs["L0_ffn_Wr"]
    WrT = np.ascontiguousarray(Wr.T.reshape(32, 128, 8).transpose(1, 0, 2).reshape(128, 256))
    return {
        "x": np.ascontiguousarray(inputs["L0_attn_x"].reshape(2, 4, 1024, 4096)[bslice[0], bslice[1]]),
        "wqkT_q": wqkT_q, "wqkT_k": wqkT_k, "wvT": wvT, "woT": woT,
        "bq_h": bq_h, "bk_h": bk_h,
        "bv": np.ascontiguousarray(inputs["L0_attn_Wv_b"]),
        "bo": np.ascontiguousarray(inputs["L0_attn_Wo_b"]),
        "AT0": AT, "BT0": BT, "WrT": WrT,
        "br": np.ascontiguousarray(inputs["L0_ffn_br"]),
    }




def build_l1(dbg=False, shell=None, io=None, caches=None, skip_w=False):
    if shell is None:
        nc = bass.Bass("TRN2", target_bir_lowering=False, debug=False, detect_race_conditions=False)
        S0 = None
    else:
        nc, S0 = shell
    if io is None:
        x_ap = _dram(nc, "x", [TOK, D], F32, "ExternalInput")
        xout = _dram(nc, "xout", [TOK, D], F32, "ExternalOutput")
    else:
        x_ap, xout = io
    if not skip_w:
        wq_ap = _dram(nc, "wq", [D, D], F32, "ExternalInput")
        wk_ap = _dram(nc, "wk", [D, 1024], F32, "ExternalInput")
        wv_ap = _dram(nc, "wv", [D, 1024], F32, "ExternalInput")
        wo_ap = _dram(nc, "wo", [1024, D], F32, "ExternalInput")
    AT_ap = _dram(nc, "AT1", [D, 128], F32, "ExternalInput")
    BT_ap = _dram(nc, "BT1", [128, D], F32, "ExternalInput")
    RT_ap = _dram(nc, "RT", [128, 256], F32, "ExternalInput")

    kt_own = _dram(nc, "kt_own1", [1024, TOK], BF16)
    v_own = _dram(nc, "v_own1", [TOK, 1032], BF16)
    kt_all = _dram(nc, "kt_all1", [4 * 1024, TOK], BF16)
    v_all = _dram(nc, "v_all1", [4 * TOK, 1032], BF16)
    if caches is None:
        caches = {"c_wqe": _dram(nc, "c_wqe", [D, 1024], BF16, "ExternalInput"),
                  "c_wk": _dram(nc, "c_wk", [D, 1024], BF16, "ExternalInput"),
                  "c_wv": _dram(nc, "c_wv", [D, 1024], BF16, "ExternalInput"),
                  "c_wo": _dram(nc, "c_wo", [1024, D], BF16, "ExternalInput")}
    c_wqe, c_wk, c_wv, c_wo = caches["c_wqe"], caches["c_wk"], caches["c_wv"], caches["c_wo"]

    if dbg:
        d_xnT = _dram(nc, "d_xnT", [128, DMC, TOK], BF16, "ExternalOutput")
        d_QT = _dram(nc, "d_QT", [128, 8, TOK], BF16, "ExternalOutput")
        d_KT = _dram(nc, "d_KT", [128, 8, TOK], BF16, "ExternalOutput")
        d_V = _dram(nc, "d_V", [128, TC, 1032], BF16, "ExternalOutput")
        d_logits = _dram(nc, "d_logits", [128, TC, 8], F32, "ExternalOutput")
        d_lows = _dram(nc, "d_lows", [128, TOK], BF16, "ExternalOutput")
        d_attnT = _dram(nc, "d_attnT", [128, 8, TOK], BF16, "ExternalOutput")

    ctx = ExitStack()
    sb, S = _alloc_common(nc, ctx, S0, sfx="_b")
    ps = lambda n, shp, dt: ctx.enter_context(nc.psum_tensor(n + "_b", shp, dt))

    x_sb = sb("x_sb1", [128, 2, 4224], F32)
    xn_tm = sb("xn_tm", [128, 2, D], BF16)
    wfstage = xn_tm.ap().rearrange("p b n -> p (b n)").bitcast(F32)
    xn_T = sb("xn_T", [128, DMC, TOK], BF16)
    ln = {"stats": sb("ln_stats", [128, 8, 6], F32), "mv": sb("ln_mv", [128, 2], F32),
          "vtmp": sb("ln_vtmp", [128, 1], F32), "negmean": sb("ln_negmean", [128, 1], F32),
          "rstd": sb("ln_rstd", [128, 1], F32), "sd": sb("ln_sd", [128, 1], F32)}
    consts = {"ident_bf": sb("ident_bf", [128, 128], BF16),
              "eps": sb("eps", [128, 1], F32),
              "ones_bf": sb("ones_bf", [1, 128], BF16),
              "sc128": sb("sc128", [128, 1], F32)}
    QT = sb("QT", [128, 8, TOK], BF16)
    KT = sb("KT", [128, 8, TOK], BF16)
    v_full = sb("v_full", [128, TC, 1032], BF16)
    AT_sb = sb("AT_sb", [128, DMC, 128], BF16)
    BT_sb = sb("BT_sb", [128, D], BF16)
    RT_sb = sb("RT_sb", [128, DMC, 8], BF16)
    low_T = sb("low_T", [128, TOK], F32)
    low_s = sb("low_s", [128, TOK], BF16)
    logits = sb("logits", [128, TC, 8], F32)
    stok_bf = sb("stok_bf", [128, TC, 8], BF16)
    E_exp = sb("E_exp", [8, 128], BF16)
    rmax = sb("rmax", [128, 1], F32)
    P_sb = sb("P_sb", [128, 4, 512], BF16)
    sT = P_sb.ap().rearrange("p b n -> p (b n)")[0:8, 0:1024]
    attn_stage = sb("attn_stage", [128, 8, 128], BF16)
    attn_T = KT
    lrecip = sb("lrecip", [128, 1], F32)

    AR = x_sb.ap().rearrange("p b n -> p (b n)")
    ARb = AR.bitcast(BF16)
    wqf32 = AR[:, 0:4096]
    wqbf = ARb[:, 8192:12288]
    K_h = ARb[:, 0:8192].rearrange("p (b r t) -> p b r t", b=2, r=4)
    V_h = ARb[:, 8192:16448].rearrange("p (b k c) -> p b k c", b=2, k=32)
    x_res = AR[:, 4608:5632].rearrange("p (b n) -> p b n", b=2)
    xo_sb = AR[:, 5632:6656].rearrange("p (b n) -> p b n", b=2)

    PS = ps("PS1", [128, 8, 512], F32)
    ps_tp = PS[:, 0:2, :].rearrange("p b n -> p (b n)").bitcast(BF16).rearrange(
        "p (b n) -> p b n", b=2)
    ps_atp = PS[:, 0, :].bitcast(BF16)
    ps_sb8 = PS[:, 3, :].bitcast(BF16)

    with nc.Block() as block:
        @block.gpsimd
        def _(g):
            aux = S["aux"]

            def step(inst):
                aux.inc(inst, 1)
                g.wait_ge(aux.h, aux.n)
            step(g.memset(consts["ident_bf"][:, :], 0.0))
            step(g.affine_select(out=consts["ident_bf"][:, :], in_=consts["ident_bf"][:, :],
                                 compare_op=ALU.not_equal, fill=1.0, base=0,
                                 pattern=[[-1, 128]], channel_multiplier=1))
            step(g.memset(consts["eps"][:, :], 1e-5))
            step(g.memset(consts["ones_bf"][:, :], 1.0))
            step(g.memset(consts["sc128"][:, :], INV_SQRT128))
            step(g.memset(E_exp[:, :], 1.0))
            step(g.affine_select(out=E_exp[:, :], in_=E_exp[:, :],
                                 compare_op=ALU.is_ge, fill=0.0, base=0,
                                 pattern=[[1, 128]], channel_multiplier=-16))
            step(g.affine_select(out=E_exp[:, :], in_=E_exp[:, :],
                                 compare_op=ALU.is_ge, fill=0.0, base=15,
                                 pattern=[[-1, 128]], channel_multiplier=16))
            step(g.memset(v_full[:, :, :].rearrange("p t (h c) -> p t h c", c=129)[:, :, :, 128:129], 1.0))
            S["const"].inc(g.nop(), 1)

        _emit_ln(nc, block, S, x_ap, x_sb, xn_tm, xn_T, ln, ps_tp, consts, cN=S["const"].n)

        barrier(block, S, waits=[(S["tpev"], S["tpev"].n), (S["const"], S["const"].n)])

        bWCl1 = S["wc"].n

        @block.sync
        def _(sy):
            dmaw(sy, S["w"], wfstage.rearrange("p (c n) -> p c n", c=DMC),
                 AT_ap.rearrange("(c p) n -> p c n", p=128))

        @block.vector
        def _(v):
            v.wait_ge(S["w"].h, S["w"].n)
            S["wc"].inc(v.tensor_copy(AT_sb[:, :, :],
                                      wfstage.rearrange("p (c n) -> p c n", c=DMC)), 1)

        @block.sync
        def _(sy):
            sy.wait_ge(S["wc"].h, bWCl1 + 1)
            dmaw(sy, S["w"], wfstage, BT_ap)

        @block.vector
        def _(v):
            v.wait_ge(S["w"].h, S["w"].n)
            S["wc"].inc(v.tensor_copy(BT_sb[:, :], wfstage), 1)

        @block.sync
        def _(sy):
            sy.wait_ge(S["wc"].h, bWCl1 + 2)
            dmaw(sy, S["w"], wfstage[:, 0:256], RT_ap[:, :])

        @block.vector
        def _(v):
            v.wait_ge(S["w"].h, S["w"].n)
            S["wc"].inc(v.tensor_copy(RT_sb[:, :, :],
                                      wfstage[:, 0:256].rearrange("p (c e) -> p c e", c=DMC)), 1)

        barrier(block, S, waits=[(S["wc"], S["wc"].n)])

        wstr = ARb[:, 0:4096].rearrange("p (b n) -> p b n", b=8)
        wsem = {0: S["w"], 1: S["w2"]}

        def proj_bf(src_ap, OUT_sb, gate=None):
            nMM2 = S["mm"].n
            nEV2 = S["ev"].n
            bW2 = {0: S["w"].n, 1: S["w2"].n}

            def ld(eng, i):
                og, dmc = divmod(i, DMC)
                if i < 2 and gate is not None:
                    eng.wait_ge(S["lwo"].h, gate)
                if i >= 8:
                    eng.wait_ge(S["mm"].h, nMM2 + i - 7)
                dmaw(eng, wsem[i % 2], wstr[:, i % 8, :],
                     src_ap[dmc * 128:(dmc + 1) * 128, og * 512:(og + 1) * 512])

            @block.sync
            def _(sy):
                for i in range(0, 2 * DMC, 2):
                    ld(sy, i)

            @block.tensor
            def _(t):
                nld = {0: 0, 1: 0}
                for i in range(2 * DMC):
                    og, dmc = divmod(i, DMC)
                    nld[i % 2] += 16
                    t.wait_ge(wsem[i % 2].h, bW2[i % 2] + nld[i % 2])
                    if dmc == 0 and og == 1:
                        t.wait_ge(S["ev"].h, nEV2 + 8)
                    last = None
                    for oc in range(4):
                        for half in range(2):
                            last = t.matmul(PS[:, oc * 2 + half, :],
                                            wstr[:, i % 8, oc * 128:(oc + 1) * 128],
                                            xn_T[:, dmc, half * 512:(half + 1) * 512],
                                            start=(dmc == 0), stop=(dmc == DMC - 1))
                    S["mm"].inc(last, 1)

            @block.scalar
            def _(s):
                for og in range(2):
                    for i in range(og * DMC + 1, (og + 1) * DMC, 2):
                        ld(s, i)
                    s.wait_ge(S["mm"].h, nMM2 + (og + 1) * DMC)
                    last = None
                    for oc in range(4):
                        for half in range(2):
                            last = s.copy(OUT_sb[:, og * 4 + oc, half * 512:(half + 1) * 512],
                                          PS[:, oc * 2 + half, :])
                    S["ev"].inc(last, 8)

        proj_bf(c_wk, KT, gate=caches.get("lwo_wk"))
        barrier(block, S, waits=[(S["ev"], S["ev"].n)])

        nEVk1 = S["ev"].n

        @block.gpsimd
        def _(g):
            g.wait_ge(S["ev"].h, nEVk1)
            dmaw(g, S["kv"], kt_own.rearrange("(c p) t -> p c t", p=128), KT[:, :, :])
            g.wait_ge(S["kv"].h, S["kv"].n)
            S["cc"].inc(g.collective_compute(
                "AllGather", ALU.bypass,
                replica_groups=[[0, 1, 2, 3], [4, 5, 6, 7]],
                ins=[kt_own.opt()], outs=[kt_all.opt()]), 1)

        nMM3 = S["mm"].n
        nEVv = S["ev"].n
        bWv = {0: S["w"].n, 1: S["w2"].n}

        def vld(eng, i):
            dv, dmc = divmod(i, DMC)
            if i < 2 and caches.get("lwo_wv") is not None:
                eng.wait_ge(S["lwo"].h, caches["lwo_wv"])
            if i >= 8:
                eng.wait_ge(S["mm"].h, nMM3 + i - 7)
            dmaw(eng, wsem[i % 2], wstr[:, i % 8, :],
                 c_wv[dmc * 128:(dmc + 1) * 128, dv * 512:(dv + 1) * 512])

        @block.sync
        def _(sy):
            for i in range(0, 2 * DMC, 2):
                vld(sy, i)

        @block.tensor
        def _(t):
            nld = {0: 0, 1: 0}
            for dv in range(2):
                for dmc in range(DMC):
                    i = dv * DMC + dmc
                    nld[i % 2] += 16
                    t.wait_ge(wsem[i % 2].h, bWv[i % 2] + nld[i % 2])
                    if dmc == 0 and dv == 1:
                        t.wait_ge(S["ev"].h, nEVv + 1)
                    last = None
                    for tc in range(TC):
                        last = t.matmul(PS[:, tc, :], xn_T[:, dmc, tc * 128:(tc + 1) * 128],
                                        wstr[:, i % 8, :],
                                        start=(dmc == 0), stop=(dmc == DMC - 1))
                    S["mm"].inc(last, 1)

        @block.scalar
        def _(s):
            for dv in range(2):
                for i in range(dv * DMC + 1, (dv + 1) * DMC, 2):
                    vld(s, i)
                s.wait_ge(S["mm"].h, nMM3 + (dv + 1) * DMC)
                last = None
                for tc in range(TC):
                    dst = v_full[:, tc, :].rearrange("p (h c) -> p h c", c=129)[:, dv * 4:(dv + 1) * 4, 0:128]
                    last = s.copy(dst, PS[:, tc, :].rearrange("p (h e) -> p h e", e=128))
                S["ev"].inc(last, 1)

        @block.gpsimd
        def _(g):
            g.wait_ge(S["ev"].h, S["ev"].n)
            dmaw(g, S["kv"], v_own.rearrange("(c p) n -> p c n", p=128), v_full[:, :, :])
            g.wait_ge(S["kv"].h, S["kv"].n)
            S["cc"].inc(g.collective_compute(
                "AllGather", ALU.bypass,
                replica_groups=[[0, 1, 2, 3], [4, 5, 6, 7]],
                ins=[v_own.opt()], outs=[v_all.opt()]), 1)

        barrier(block, S, waits=[(S["ev"], S["ev"].n)])

        proj_bf(c_wqe, QT, gate=caches.get("lwo_wq"))
        barrier(block, S, waits=[(S["ev"], S["ev"].n)])

        nMM5 = S["mm"].n
        nR1 = S["r1"].n

        @block.tensor
        def _(t):
            last = None
            for dmc in range(DMC):
                for tc in range(TC):
                    last = t.matmul(PS[:, tc, 0:8], xn_T[:, dmc, tc * 128:(tc + 1) * 128],
                                    RT_sb[:, dmc, :],
                                    start=(dmc == 0), stop=(dmc == DMC - 1))
            S["mm"].inc(last, 1)

        @block.vector
        def _(v):
            v.wait_ge(S["mm"].h, nMM5 + 1)
            last = None
            for tc in range(TC):
                last = v.tensor_copy(logits[:, tc, :], PS[:, tc, 0:8])
            S["r1"].inc(last, 1)

        @block.tensor
        def _(t):
            t.wait_ge(S["r1"].h, nR1 + 1)
            for half in range(2):
                last = None
                for dmc in range(DMC):
                    last = t.matmul(PS[:, 1 + half, :], AT_sb[:, dmc, :],
                                    xn_T[:, dmc, half * 512:(half + 1) * 512],
                                    start=(dmc == 0), stop=(dmc == DMC - 1))
                S["mm"].inc(last, 1)

        @block.scalar
        def _(s):
            s.wait_ge(S["mm"].h, nMM5 + 3)
            for half in range(2):
                S["ev"].inc(s.copy(low_T[:, half * 512:(half + 1) * 512], PS[:, 1 + half, :]), 1)

        @block.vector
        def _(v):
            vch = S["vch"]
            v.wait_ge(S["r1"].h, S["r1"].n)
            for tc in range(TC):
                vch.inc(v.reduce_max(rmax[:, :], logits[:, tc, :], axis=mybir.AxisListType.X), 1)
                v.wait_ge(vch.h, vch.n)
                S["r1"].inc(v.tensor_scalar(stok_bf[:, tc, :], logits[:, tc, :],
                                            rmax[:, :], rmax[:, :], ALU.is_ge, ALU.mult), 1)

        @block.tensor
        def _(t):
            t.wait_ge(S["r1"].h, S["r1"].n)
            last = None
            for tc in range(TC):
                last = t.transpose(ps_sb8[0:8, tc * 128:(tc + 1) * 128],
                                   stok_bf[:, tc, :], consts["ident_bf"][:, :])
            S["att"].inc(last, 1)

        @block.scalar
        def _(s):
            s.wait_ge(S["att"].h, S["att"].n)
            S["ev"].inc(s.copy(sT[0:8, :], ps_sb8[0:8, :]), 1)

        @block.tensor
        def _(t):
            t.wait_ge(S["ev"].h, S["ev"].n)
            for half in range(2):
                S["mm"].inc(t.matmul(PS[:, 4 + half, :], E_exp[0:8, :],
                                     sT[0:8, half * 512:(half + 1) * 512],
                                     start=True, stop=True), 1)

        @block.vector
        def _(v):
            v.wait_ge(S["mm"].h, S["mm"].n)
            v.wait_ge(S["ev"].h, S["ev"].n)
            last = None
            for half in range(2):
                last = v.tensor_tensor(low_s[:, half * 512:(half + 1) * 512],
                                       low_T[:, half * 512:(half + 1) * 512],
                                       PS[:, 4 + half, :], ALU.mult)
            S["r1"].inc(last, 1)

        barrier(block, S, waits=[(S["r1"], S["r1"].n), (S["cc"], S["cc"].n)])

        if dbg:
            @block.sync
            def _(sy):
                dmaw(sy, S["aux"], d_xnT, xn_T[:, :, :])
                dmaw(sy, S["aux"], d_QT, QT[:, :, :])
                dmaw(sy, S["aux"], d_KT, KT[:, :, :])
                dmaw(sy, S["aux"], d_V, v_full[:, :, :])
                dmaw(sy, S["aux"], d_logits, logits[:, :, :])
                dmaw(sy, S["aux"], d_lows, low_s[:, :])

        kt_view = kt_all.rearrange("(r c p) t -> p r c t", r=4, p=128)
        v_view = v_all.rearrange("(kc p) n -> p kc n", p=128)
        kldD = S["kld"].n
        attD0 = S["att"].n
        bsS1, bpexp1, bpvm1, bep1, batev1 = (S["sS"].n, S["pexp"].n, S["pvm"].n,
                                             S["ep"].n, S["atev"].n)
        s_idx = [0]

        @block.sync
        def _(sy):
            for h in range(8):
                if h >= 2:
                    sy.wait_ge(S["ep"].h, bep1 + (h - 1) * 2)
                dmaw(sy, S["kld"], K_h[:, h % 2, :, :], kt_view[:, :, h, :])
                for q in range(4):
                    dmaw(sy, S["kld"], V_h[:, h % 2, q * 8:(q + 1) * 8, :],
                         v_view[:, q * 8:(q + 1) * 8, 129 * h:129 * (h + 1)])

        @block.tensor
        def _(t):
            def emit_S(h, half, kc, si):
                bk = si % 4
                if si >= 4:
                    t.wait_ge(S["pexp"].h, bpexp1 + si - 3)
                last = t.matmul(PS[:, bk, :],
                                K_h[:, h % 2, kc // 8, (kc % 8) * 128:(kc % 8 + 1) * 128],
                                QT[:, h, half * 512:(half + 1) * 512],
                                start=True, stop=True)
                S["sS"].inc(last, 1)

            def emit_PV(h, half, kc, si):
                if kc == 0:
                    e = h * 2 + half
                    if e > 0:
                        t.wait_ge(S["ep"].h, bep1 + e)
                t.wait_ge(S["pexp"].h, bpexp1 + si + 1)
                for qc in range(4):
                    lastpv = t.matmul(PS[:, 4 + qc, 0:129],
                                      P_sb[:, si % 4, qc * 128:(qc + 1) * 128],
                                      V_h[:, h % 2, kc, :],
                                      start=(kc == 0), stop=(kc == 31))
                S["pvm"].inc(lastpv, 1)
            for h in range(8):
                t.wait_ge(S["kld"].h, kldD + (h + 1) * 80)
                if h >= 1:
                    t.wait_ge(S["atev"].h, batev1 + h)
                pend = []
                for half in range(2):
                    for kc in range(32):
                        si = s_idx[0]
                        emit_S(h, half, kc, si)
                        if si % 2 == 1 and len(pend) >= 2:
                            emit_PV(*pend.pop(0))
                            emit_PV(*pend.pop(0))
                        pend.append((h, half, kc, si))
                        s_idx[0] += 1
                for d in pend:
                    emit_PV(*d)
                t.wait_ge(S["ep"].h, bep1 + (h + 1) * 2)
                t.wait_ge(S["pexp"].h, bpexp1 + (h + 1) * 64)
                last = None
                for i in range(8):
                    last = t.transpose(ps_atp[:, i * 128:(i + 1) * 128],
                                       attn_stage[:, i, :], consts["ident_bf"][:, :])
                S["att"].inc(last, 1)

        @block.scalar
        def _(s):
            si = 0

            def atev_copy(hh):
                s.wait_ge(S["att"].h, attD0 + hh + 1)
                S["atev"].inc(s.copy(attn_T[:, hh, :], ps_atp[:, :]), 1)
            for h in range(8):
                if h >= 1:
                    atev_copy(h - 1)
                for half in range(2):
                    for kc in range(32):
                        bk = si % 4
                        s.wait_ge(S["sS"].h, bsS1 + si + 1)
                        if si >= 4:
                            s.wait_ge(S["pvm"].h, bpvm1 + si - 3)
                        S["pexp"].inc(
                            s.activation(out=P_sb[:, bk, :], in_=PS[:, bk, :],
                                         func=AF.Exp, scale=consts["sc128"][:, :]), 1)
                        si += 1
            atev_copy(7)

        @block.vector
        def _(v):
            vch = S["vch"]
            for h in range(8):
                for half in range(2):
                    e = h * 2 + half
                    v.wait_ge(S["pvm"].h, bpvm1 + (e + 1) * 32)
                    if h >= 1 and half == 0:
                        v.wait_ge(S["att"].h, attD0 + h)
                    last = None
                    for qc in range(4):
                        vch.inc(v.reciprocal(lrecip[:, :], PS[:, 4 + qc, 128:129]), 1)
                        v.wait_ge(vch.h, vch.n)
                        last = v.tensor_scalar(attn_stage[:, half * 4 + qc, :],
                                               PS[:, 4 + qc, 0:128],
                                               lrecip[:, :], None, ALU.mult)
                    S["ep"].inc(last, 1)

        barrier(block, S, waits=[(S["atev"], S["atev"].n)])

        if dbg:
            @block.sync
            def _(sy):
                dmaw(sy, S["aux"], d_attnT, attn_T[:, :, :])

        nW4 = S["w"].n
        nWC4 = S["wc"].n
        nPO = S["po"].n
        bXR1 = S["w2"].n
        bOUT1 = S["out"].n
        bRESV1 = S["resv"].n
        OCN, OW = 8, 512
        wobf = ARb[:, 0:8192].rearrange("p (b c n) -> p b c n", b=2, c=8)

        @block.sync
        def _(sy):
            for oc in range(OCN):
                if oc == 0 and caches.get("lwo_all") is not None:
                    sy.wait_ge(S["lwo"].h, caches["lwo_all"])
                if oc >= 2:
                    sy.wait_ge(S["po"].h, nPO + (oc - 1) * 8)
                dmaw(sy, S["w"], wobf[:, oc % 2, :, :],
                     c_wo.rearrange("(c p) o -> p c o", p=128)[:, :, oc * OW:(oc + 1) * OW])

        @block.scalar
        def _(s):
            for oc in range(OCN):
                for tc in range(TC):
                    i = oc * 8 + tc
                    if i >= 2:
                        s.wait_ge(S["resv"].h, bRESV1 + i - 1)
                    dmaw(s, S["w2"], x_res[:, i % 2, :],
                         x_ap[tc * 128:(tc + 1) * 128, oc * OW:(oc + 1) * OW])

        @block.vector
        def _(v):
            for oc in range(OCN):
                for tc in range(TC):
                    i = oc * 8 + tc
                    v.wait_ge(S["po"].h, nPO + i + 1)
                    v.wait_ge(S["w2"].h, bXR1 + (i + 1) * 16)
                    if i >= 2:
                        v.wait_ge(S["out"].h, bOUT1 + (i - 1) * 16)
                    S["resv"].inc(v.scalar_tensor_tensor(
                        out=xo_sb[:, i % 2, :], in0=PS[:, i % 4, 0:OW], scalar=1.0,
                        in1=x_res[:, i % 2, :], op0=ALU.mult, op1=ALU.add), 1)

        @block.tensor
        def _(t):
            for oc in range(OCN):
                t.wait_ge(S["w"].h, nW4 + (oc + 1) * 16)
                for tc in range(TC):
                    i = oc * 8 + tc
                    b = i % 4
                    if i >= 4:
                        t.wait_ge(S["resv"].h, bRESV1 + i - 3)
                    for h in range(8):
                        t.matmul(PS[:, b, 0:OW], attn_T[:, h, tc * 128:(tc + 1) * 128],
                                 wobf[:, oc % 2, h, :], start=(h == 0), stop=False)
                    S["po"].inc(t.matmul(PS[:, b, 0:OW], low_s[:, tc * 128:(tc + 1) * 128],
                                         BT_sb[:, oc * OW:(oc + 1) * OW],
                                         start=False, stop=True), 1)

        @block.gpsimd
        def _(g):
            for oc in range(OCN):
                for tc in range(TC):
                    i = oc * 8 + tc
                    g.wait_ge(S["resv"].h, bRESV1 + i + 1)
                    dmaw(g, S["out"], xout[tc * 128:(tc + 1) * 128, oc * OW:(oc + 1) * OW],
                         xo_sb[:, i % 2, :])
            g.wait_ge(S["out"].h, S["out"].n)
            if dbg:
                g.wait_ge(S["aux"].h, S["aux"].n)

        @block.sync
        def _(sy):
            sy.wait_ge(S["out"].h, S["out"].n)

    ctx.close()
    return nc


def _host_layout_l1(inputs, bslice, x1=None):
    A = inputs["L1_ffn_A"]
    AT = np.ascontiguousarray(A.reshape(128, 4096).T)
    Bw = inputs["L1_ffn_Bw"]
    BT = np.ascontiguousarray(Bw.transpose(0, 2, 1).reshape(128, 4096))
    Rw = inputs["L1_ffn_route_w"]
    RT = np.ascontiguousarray(Rw.T.reshape(32, 128, 8).transpose(1, 0, 2).reshape(128, 256))
    d = {
        "wq": np.ascontiguousarray(inputs["L1_attn_wq"]),
        "wk": np.ascontiguousarray(inputs["L1_attn_wk"]),
        "wv": np.ascontiguousarray(inputs["L1_attn_wv"]),
        "wo": np.ascontiguousarray(inputs["L1_attn_wo"]),
        "AT1": AT, "BT1": BT, "RT": RT,
    }
    if x1 is not None:
        d["x"] = np.ascontiguousarray(x1.reshape(2, 4, 1024, 4096)[bslice[0], bslice[1]])
    return d


def build_fused():
    nc = bass.Bass("TRN2", target_bir_lowering=False, debug=False, detect_race_conditions=False)
    octx = ExitStack()
    S = _mk_sems(nc, octx)
    x_in = _dram(nc, "x", [TOK, D], F32, "ExternalInput")
    x1 = _dram(nc, "x1", [TOK, D], F32, "Internal")
    xout = _dram(nc, "xout", [TOK, D], F32, "ExternalOutput")
    prep = {
        "wq": _dram(nc, "wq", [D, D], F32, "ExternalInput"),
        "wk": _dram(nc, "wk", [D, 1024], F32, "ExternalInput"),
        "wv": _dram(nc, "wv", [D, 1024], F32, "ExternalInput"),
        "wo": _dram(nc, "wo", [1024, D], F32, "ExternalInput"),
        "c_wqe": _dram(nc, "c_wqe", [D, 1024], BF16),
        "c_wk": _dram(nc, "c_wk", [D, 1024], BF16),
        "c_wv": _dram(nc, "c_wv", [D, 1024], BF16),
        "c_wo": _dram(nc, "c_wo", [1024, D], BF16),
        "c_wo0": _dram(nc, "c_wo0", [512, D], BF16),
    }
    build_l0(shell=(nc, S), io=(x_in, x1), prep=prep)
    caches = {k: prep[k] for k in ("c_wqe", "c_wk", "c_wv", "c_wo",
                               "lwo_wk", "lwo_wv", "lwo_wq", "lwo_all")}
    build_l1(shell=(nc, S), io=(x1, xout), caches=caches, skip_w=True)
    octx.close()
    return nc


def kernel(**inputs):
    nc = build_fused()
    in_maps = []
    for c in range(8):
        m = _host_layout_l0(inputs, divmod(c, 4))
        m.update(_host_layout_l1(inputs, divmod(c, 4)))
        in_maps.append(m)
    res = run_bass_kernel_spmd(nc, in_maps, core_ids=list(range(8)))
    return np.stack([r["xout"] for r in res.results]).reshape(2, 4096, 4096)


# version 18
# speedup vs baseline: 1.1089x; 1.0000x over previous
import numpy as np
import ml_dtypes
from contextlib import ExitStack
import concourse.bass as bass
import concourse.mybir as mybir
from concourse.bass_utils import run_bass_kernel_spmd

F32 = mybir.dt.float32
BF16 = mybir.dt.bfloat16
AF = mybir.ActivationFunctionType
ALU = mybir.AluOpType

TOK = 1024
D = 4096
DMC = 32
TC = 8
INV_SQRT512 = float(np.float32(1.0 / np.sqrt(512.0)))
INV_SQRT128 = float(np.float32(1.0 / np.sqrt(128.0)))
L0_SHIFT = -40.0


class Sem:
    def __init__(self, nc, ctx, name):
        self.h = ctx.enter_context(nc.semaphore(name))
        self.n = 0

    def inc(self, inst, k):
        inst.then_inc(self.h, k)
        self.n += k
        return inst


def _dram(nc, name, shape, dtype, kind="Internal"):
    return nc.dram_tensor(name, list(shape), dtype, kind=kind).ap()


ENGS = ["sync", "gpsimd", "vector", "scalar", "tensor"]


ENGS_NOG = ["sync", "vector", "scalar", "tensor"]


def barrier(block, S, waits=(), engines=None):
    engines = ENGS_NOG if engines is None else engines
    b = S["bar"]
    target = b.n + len(engines)

    for ename in engines:
        def mk(e):
            b.inc(e.nop(), 1)
            e.wait_ge(b.h, target)
            for (sem, tgt) in waits:
                e.wait_ge(sem.h, tgt)
        getattr(block, ename)(mk)



def dmaw(eng, sem, out, in_):
    eng.wait_ge(sem.h, sem.n)
    return sem.inc(eng.dma_start(out=out, in_=in_), 16)

SEM_NAMES = ["lw", "lwc", "lwo", "out2", "w2", "lbr", "const", "bar", "x", "v1", "sc", "xn", "tp", "tpev",
             "vch", "w", "wg", "wc", "mm", "ev", "kv", "cc", "kld", "sS", "pexp", "pvm",
             "ep", "att", "atev", "r1", "r2", "xr", "po", "resv", "out", "aux"]


def _mk_sems(nc, ctx):
    return {k: Sem(nc, ctx, "s_" + k) for k in SEM_NAMES}


def _alloc_common(nc, ctx, S=None, sfx=""):
    sb = lambda n, shp, dt: ctx.enter_context(nc.sbuf_tensor(n + sfx, shp, dt))
    if S is None:
        S = _mk_sems(nc, ctx)
    return sb, S


def _emit_ln(nc, block, S, x_ap, x_sb, xn_tm, xn_T, ln, ps_tp, consts, cN=1):
    bx, bxn, bv1, bsc, btp, btpev = (S["x"].n, S["xn"].n, S["v1"].n,
                                     S["sc"].n, S["tp"].n, S["tpev"].n)

    @block.sync
    def _(sy):
        for tc in range(TC):
            if tc >= 2:
                sy.wait_ge(S["xn"].h, bxn + tc - 1)
            dmaw(sy, S["x"], x_sb[:, tc % 2, 0:D], x_ap[tc * 128:(tc + 1) * 128, :])

    @block.vector
    def _(v):
        vch = S["vch"]

        def edge(inst):
            vch.inc(inst, 1)
            v.wait_ge(vch.h, vch.n)
        v.wait_ge(S["const"].h, cN)
        for tc in range(TC):
            v.wait_ge(S["x"].h, bx + (tc + 1) * 16)
            for i in range(8):
                last = v.bn_stats(ln["stats"][:, i, :], x_sb[:, tc % 2, i * 512:(i + 1) * 512])
            edge(last)
            edge(v.bn_aggr(ln["mv"][:, :], ln["stats"][:, :, :]))
            v.tensor_tensor(ln["vtmp"][:, :], ln["mv"][:, 1:2], consts["eps"][:, :], ALU.add)
            S["v1"].inc(v.tensor_scalar_mul(ln["negmean"][:, :], ln["mv"][:, 0:1], -1.0), 1)
            v.wait_ge(S["sc"].h, bsc + tc + 1)
            edge(v.reciprocal(ln["rstd"][:, :], ln["sd"][:, :]))
            if tc >= 2:
                v.wait_ge(S["tp"].h, btp + (tc - 1) * 4)
            S["xn"].inc(v.tensor_scalar(xn_tm[:, tc % 2, 0:D], x_sb[:, tc % 2, 0:D],
                                        ln["negmean"][:, :], ln["rstd"][:, :],
                                        ALU.add, ALU.mult), 1)

    @block.tensor
    def _(t):
        t.wait_ge(S["const"].h, cN)
        for tc in range(TC):
            t.wait_ge(S["xn"].h, bxn + tc + 1)
            for grp in range(4):
                gi = tc * 4 + grp
                if gi >= 2:
                    t.wait_ge(S["tpev"].h, btpev + gi - 1)
                last = None
                for i in range(8):
                    dmc = grp * 8 + i
                    last = t.transpose(ps_tp[:, gi % 2, i * 128:(i + 1) * 128],
                                       xn_tm[:, tc % 2, dmc * 128:(dmc + 1) * 128],
                                       consts["ident_bf"][:, :])
                S["tp"].inc(last, 1)

    @block.scalar
    def _(s):
        def do_sqrt(tc):
            s.wait_ge(S["v1"].h, bv1 + tc + 1)
            S["sc"].inc(s.sqrt(ln["sd"][:, :], ln["vtmp"][:, :]), 1)
        do_sqrt(0)
        do_sqrt(1)
        for tc in range(TC):
            for grp in range(4):
                gi = tc * 4 + grp
                s.wait_ge(S["tp"].h, btp + gi + 1)
                dst = xn_T[:, grp * 8:(grp + 1) * 8, tc * 128:(tc + 1) * 128]
                S["tpev"].inc(
                    s.copy(dst, ps_tp[:, gi % 2, :].rearrange("p (c n) -> p c n", c=8)), 1)
            if tc + 2 < TC:
                do_sqrt(tc + 2)


def build_l0(dbg=False, shell=None, io=None, prep=None):
    if shell is None:
        nc = bass.Bass("TRN2", target_bir_lowering=False, debug=False, detect_race_conditions=False)
        S0 = None
    else:
        nc, S0 = shell
    if io is None:
        x_ap = _dram(nc, "x", [TOK, D], F32, "ExternalInput")
        xout = _dram(nc, "xout", [TOK, D], F32, "ExternalOutput")
    else:
        x_ap, xout = io
    wqkT_q = _dram(nc, "wqkT_q", [D, 512], F32, "ExternalInput")
    wqkT_k = _dram(nc, "wqkT_k", [D, 512], F32, "ExternalInput")
    wvT = _dram(nc, "wvT", [D, 512], F32, "ExternalInput")
    woT = _dram(nc, "woT", [512, D], F32, "ExternalInput")
    bq_h = _dram(nc, "bq_h", [4, 128], F32, "ExternalInput")
    bk_h = _dram(nc, "bk_h", [4, 128], F32, "ExternalInput")
    bv_ap = _dram(nc, "bv", [512], F32, "ExternalInput")
    bo_ap = _dram(nc, "bo", [D], F32, "ExternalInput")
    AT_ap = _dram(nc, "AT0", [D, 128], F32, "ExternalInput")
    BT_ap = _dram(nc, "BT0", [128, D], F32, "ExternalInput")
    WrT_ap = _dram(nc, "WrT", [128, 256], F32, "ExternalInput")
    br_ap = _dram(nc, "br", [8], F32, "ExternalInput")

    kt_own_p = [_dram(nc, "kt_own0_%d" % j, [128, TOK], BF16) for j in range(4)]
    kt_all_p = [_dram(nc, "kt_all0_%d" % j, [4 * 128, TOK], BF16) for j in range(4)]
    v_own_p = [_dram(nc, "v_own0_%d" % j, [TOK, 130], BF16) for j in range(4)]
    v_all_p = [_dram(nc, "v_all0_%d" % j, [4 * TOK, 130], BF16) for j in range(4)]

    if dbg:
        d_xnT = _dram(nc, "d_xnT", [128, DMC, TOK], BF16, "ExternalOutput")
        d_QT = _dram(nc, "d_QT", [128, 4, TOK], BF16, "ExternalOutput")
        d_KT = _dram(nc, "d_KT", [128, 4, TOK], BF16, "ExternalOutput")
        d_V = _dram(nc, "d_V", [128, TC, 520], BF16, "ExternalOutput")
        d_logits = _dram(nc, "d_logits", [128, TC, 8], F32, "ExternalOutput")
        d_lows = _dram(nc, "d_lows", [128, TOK], BF16, "ExternalOutput")
        d_attnT = _dram(nc, "d_attnT", [128, 4, TOK], BF16, "ExternalOutput")
        d_xsb = _dram(nc, "d_xsb", [128, D], F32, "ExternalOutput")
        d_mv = _dram(nc, "d_mv", [128, 2], F32, "ExternalOutput")
        d_rstd = _dram(nc, "d_rstd", [128, 1], F32, "ExternalOutput")
        d_xnl = _dram(nc, "d_xnl", [128, D], BF16, "ExternalOutput")
        d_xnT_early = _dram(nc, "d_xnT_early", [128, DMC, TOK], BF16, "ExternalOutput")

    ctx = ExitStack()
    sb, S = _alloc_common(nc, ctx, S0, sfx="_a")
    ps = lambda n, shp, dt: ctx.enter_context(nc.psum_tensor(n + "_a", shp, dt))

    x_sb = sb("x_sb0", [128, 2, 4224], F32)
    xn_tm = sb("xn_tm", [128, 2, D], BF16)
    wfstage = xn_tm.ap().rearrange("p b n -> p (b n)").bitcast(F32)
    xn_T = sb("xn_T", [128, DMC, TOK], BF16)
    ln = {"stats": sb("ln_stats", [128, 8, 6], F32), "mv": sb("ln_mv", [128, 2], F32),
          "vtmp": sb("ln_vtmp", [128, 1], F32), "negmean": sb("ln_negmean", [128, 1], F32),
          "rstd": sb("ln_rstd", [128, 1], F32), "sd": sb("ln_sd", [128, 1], F32)}
    consts = {"ident_bf": sb("ident_bf", [128, 128], BF16),
              "eps": sb("eps", [128, 1], F32),
              "ones_bf": sb("ones_bf", [1, 128], BF16),
              "ones_f32": sb("ones_f32", [1, 128], F32),
              "shift": sb("shift", [128, 1], F32),
              "maskAB": sb("maskAB", [1, 2, 128], BF16)}
    wf32 = sb("wf32", [128, 2, 1024], F32)
    wbf = sb("wbf", [128, 2, 1024], BF16)
    QT = sb("QT", [128, 4, TOK], BF16)
    KT = sb("KT", [128, 4, TOK], BF16)
    v_full = sb("v_full", [128, TC, 520], BF16)
    AT_sb = sb("AT_sb", [128, DMC, 128], BF16)
    BT_sb = sb("BT_sb", [128, D], BF16)
    WrT_sb = sb("WrT_sb", [128, DMC, 8], BF16)
    low_T = sb("low_T", [128, TOK], F32)
    low_s = sb("low_s", [128, TOK], BF16)
    logits = sb("logits", [128, TC, 8], F32)
    sroute = sb("sroute", [128, TC, 8], F32)
    sroute_bf = sb("sroute_bf", [128, TC, 8], BF16)
    sT = sb("sT", [8, TOK], BF16)
    E_exp = sb("E_exp", [8, 128], BF16)
    b_pair_q = sb("b_pair_q", [128, 4], F32)
    b_pair_k = sb("b_pair_k", [128, 4], F32)
    bvrow = sb("bvrow", [1, 512], F32)
    bias_bcast = sb("bias_bcast", [128, 512], F32)
    borow_bf = sb("borow_bf", [1, D], BF16)
    brrow = sb("brrow", [1, 8], F32)
    br_bcast = sb("br_bcast", [128, 8], F32)
    rmax = sb("rmax", [128, 1], F32)
    rnegmax = sb("rnegmax", [128, 1], F32)
    rsum = sb("rsum", [128, 1], F32)
    rrecip = sb("rrecip", [128, 1], F32)
    KVar = x_sb.ap().rearrange("p b n -> p (b n)").bitcast(BF16)
    K_pair = KVar[:, 0:8192].rearrange("p (b r t) -> p b r t", b=2, r=4)
    V_pair = KVar[:, 8192:16512].rearrange("p (b k c) -> p b k c", b=2, k=32)
    P_sb = sb("P_sb", [128, 4, 512], BF16)
    attn_T = sb("attn_T", [128, 4, TOK], BF16)
    lbf_pair = sb("lbf_pair", [1, 4, 512], BF16)
    rcpD = sb("rcpD", [128, 512], F32)
    if prep is not None:
        _pxt = xn_tm.ap().rearrange("p b n -> p (b n)").bitcast(F32).rearrange(
            "p (b n) -> p b n", b=2)
        pf32s = [_pxt[:, 0, :], _pxt[:, 1, :],
                 KT.ap().rearrange("p b n -> p (b n)").bitcast(F32),
                 v_full.ap().rearrange("p b n -> p (b n)").bitcast(F32)[:, 0:2048]]
        pbf = wf32.ap().rearrange("p b n -> p (b n)").bitcast(BF16).rearrange(
            "p (b n) -> p b n", b=2)
        pscr = wbf.ap().rearrange("p b n -> p (b n)").bitcast(F32)[:, 0:512]
    wof32 = x_sb[:, 0, 0:2048].rearrange("p (c n) -> p c n", c=4)
    wobf = x_sb[:, 0, 2048:4096].bitcast(BF16).rearrange("p (b c n) -> p b c n", b=2, c=4)
    x_res = x_sb[:, 1, 0:1024].rearrange("p (b n) -> p b n", b=2)
    xo_sb = x_sb[:, 1, 1024:2048].rearrange("p (b n) -> p b n", b=2)

    PS = ps("PS0", [128, 8, 512], F32)
    ps_tp = PS[:, 0:2, :].rearrange("p b n -> p (b n)").bitcast(BF16).rearrange(
        "p (b n) -> p b n", b=2)
    pvflat = PS[:, 4:6, :].rearrange("p b n -> p (b n)")
    ps_sb8 = PS[:, 7, :].bitcast(BF16)


    with nc.Block() as block:
        @block.gpsimd
        def _(g):
            aux = S["aux"]

            def step(inst):
                aux.inc(inst, 1)
                g.wait_ge(aux.h, aux.n)
            step(g.memset(consts["ident_bf"][:, :], 0.0))
            step(g.affine_select(out=consts["ident_bf"][:, :], in_=consts["ident_bf"][:, :],
                                 compare_op=ALU.not_equal, fill=1.0, base=0,
                                 pattern=[[-1, 128]], channel_multiplier=1))
            step(g.memset(consts["eps"][:, :], 1e-5))
            step(g.memset(consts["ones_bf"][:, :], 1.0))
            step(g.memset(consts["ones_f32"][:, :], 1.0))
            step(g.memset(consts["shift"][:, :], L0_SHIFT))
            step(g.memset(consts["maskAB"][:, :, :], 0.0))
            step(g.memset(consts["maskAB"][:, 0, 0:64], 1.0))
            step(g.memset(consts["maskAB"][:, 1, 64:128], 1.0))
            step(g.memset(E_exp[:, :], 1.0))
            step(g.affine_select(out=E_exp[:, :], in_=E_exp[:, :],
                                 compare_op=ALU.is_ge, fill=0.0, base=0,
                                 pattern=[[1, 128]], channel_multiplier=-16))
            step(g.affine_select(out=E_exp[:, :], in_=E_exp[:, :],
                                 compare_op=ALU.is_ge, fill=0.0, base=15,
                                 pattern=[[-1, 128]], channel_multiplier=16))
            step(g.memset(v_full[:, :, :].rearrange("p t (h c) -> p t h c", c=65)[:, :, :, 64:65], 1.0))
            S["const"].inc(g.nop(), 1)

        _emit_ln(nc, block, S, x_ap, x_sb, xn_tm, xn_T, ln, ps_tp, consts, cN=S["const"].n)

        @block.gpsimd
        def _(g):
            for j in range(4):
                dmaw(g, S["wg"], b_pair_q[:, j:j + 1], bq_h[j:j + 1, :].rearrange("a p -> p a"))
                dmaw(g, S["wg"], b_pair_k[:, j:j + 1], bk_h[j:j + 1, :].rearrange("a p -> p a"))
            dmaw(g, S["wg"], bvrow[:, :], bv_ap.rearrange("(a n) -> a n", a=1))
            dmaw(g, S["wg"], brrow[:, :], br_ap.rearrange("(a n) -> a n", a=1))

        prep_casts = []
        PRE = 32
        if prep is not None:
            blw, blwc, blwo = S["lw"].n, S["lwc"].n, S["lwo"].n
            prep["wo0"] = woT
            plan = []
            for dmc in range(DMC):
                plan.append(("wk", dmc * 128, 0, 1024, "cast"))
            for dmc in range(DMC):
                plan.append(("wv", dmc * 128, 0, 1024, "cast"))
            for c4 in range(4):
                for ch in range(2):
                    plan.append(("wo0", c4 * 128, ch * 2048, 2048, "cast"))
            prep["lwo_wo0"] = blwo + 72 * 16
            prep["lwo_wk"] = blwo + 32 * 16
            prep["lwo_wv"] = blwo + 64 * 16
            prep["lwo_wq"] = blwo + 136 * 16
            prep["lwo_all"] = blwo + 152 * 16
            for dmc in range(DMC):
                for ch in range(2):
                    plan.append(("wq", dmc * 128, ch * 2048, 2048, "presum"))
            for c8 in range(8):
                for ch in range(2):
                    plan.append(("wo", c8 * 128, ch * 2048, 2048, "cast"))

            pre_f32 = [attn_T.ap().rearrange("p b n -> p (b n)").bitcast(F32)[:, 0:1024],
                       attn_T.ap().rearrange("p b n -> p (b n)").bitcast(F32)[:, 1024:2048],
                       P_sb.ap().rearrange("p b n -> p (b n)").bitcast(F32)]
            pre_bf = rcpD.ap().bitcast(BF16)

            def _dst(w, r0, c0, ncols, kind):
                if kind == "presum":
                    return prep["c_wqe"][r0:r0 + 128, (c0 // 2048) * 512:(c0 // 2048) * 512 + 512], 512
                return prep["c_" + w][r0:r0 + 128, c0:c0 + ncols], ncols

            @block.gpsimd
            def _(g):
                for k in range(PRE):
                    w, r0, c0, ncols, kind = plan[k]
                    if k >= 3:
                        g.wait_ge(S["lwc"].h, blwc + k - 2)
                    dmaw(g, S["lw"], pre_f32[k % 3][:, 0:ncols], prep[w][r0:r0 + 128, c0:c0 + ncols])
                    if k >= 1:
                        dst, dn = _dst(*plan[k - 1])
                        g.wait_ge(S["lwc"].h, blwc + k)
                        dmaw(g, S["lwo"], dst, pre_bf[:, 0:dn])
                dst, dn = _dst(*plan[PRE - 1])
                g.wait_ge(S["lwc"].h, blwc + PRE)
                dmaw(g, S["lwo"], dst, pre_bf[:, 0:dn])

            def mk_cast(k, w, r0, c0, ncols, kind):
                def go(v):
                    vch = S["vch"]
                    v.wait_ge(S["lw"].h, blw + (k + 1) * 16)
                    if k < PRE:
                        if k >= 1:
                            v.wait_ge(S["lwo"].h, blwo + k * 16)
                        S["lwc"].inc(v.tensor_copy(pre_bf[:, 0:ncols],
                                                   pre_f32[k % 3][:, 0:ncols]), 1)
                        return
                    if k >= 2:
                        v.wait_ge(S["lwo"].h, blwo + (k - 1) * 16)
                    if kind == "presum":
                        wv4 = pf32s[k % 4].rearrange("p (kv g c) -> p kv g c", kv=4, g=4)
                        scr = pscr.rearrange("p (kv c) -> p kv c", kv=4)
                        vch.inc(v.tensor_tensor(scr, wv4[:, :, 0, :], wv4[:, :, 1, :], ALU.add), 1)
                        v.wait_ge(vch.h, vch.n)
                        vch.inc(v.tensor_tensor(scr, scr, wv4[:, :, 2, :], ALU.add), 1)
                        v.wait_ge(vch.h, vch.n)
                        S["lwc"].inc(v.tensor_tensor(
                            pbf[:, k % 2, 0:512].rearrange("p (kv c) -> p kv c", kv=4),
                            scr, wv4[:, :, 3, :], ALU.add), 1)
                    else:
                        S["lwc"].inc(v.tensor_copy(pbf[:, k % 2, 0:ncols], pf32s[k % 4][:, 0:ncols]), 1)
                return go
            prep_casts = [mk_cast(k, *cfg) for k, cfg in enumerate(plan)]

        _ci = [0]

        def drain_casts_on(v, n):
            while _ci[0] < len(prep_casts) and n > 0:
                prep_casts[_ci[0]](v)
                _ci[0] += 1
                n -= 1

        barrier(block, S, waits=[(S["tpev"], S["tpev"].n), (S["const"], S["const"].n)])

        if dbg:
            @block.gpsimd
            def _(g):
                dmaw(g, S["kv"], d_xnT_early, xn_T[:, :, :])
                g.wait_ge(S["kv"].h, S["kv"].n)

        bWC2 = S["wc"].n
        bMM2 = S["mm"].n

        @block.sync
        def _(sy):
            dmaw(sy, S["w"], wfstage.rearrange("p (c n) -> p c n", c=DMC), AT_ap.rearrange("(c p) n -> p c n", p=128))

        @block.vector
        def _(v):
            v.wait_ge(S["w"].h, S["w"].n)
            S["wc"].inc(v.tensor_copy(AT_sb[:, :, :],
                                      wfstage.rearrange("p (c n) -> p c n", c=DMC)), 1)

        @block.sync
        def _(sy):
            sy.wait_ge(S["wc"].h, bWC2 + 1)
            dmaw(sy, S["w"], wfstage, BT_ap)

        @block.vector
        def _(v):
            v.wait_ge(S["w"].h, S["w"].n)
            S["wc"].inc(v.tensor_copy(BT_sb[:, :], wfstage), 1)

        @block.sync
        def _(sy):
            sy.wait_ge(S["wc"].h, bWC2 + 2)
            dmaw(sy, S["w"], wfstage[:, 0:256], WrT_ap[:, :])
            sy.wait_ge(S["wc"].h, bWC2 + 3)
            dmaw(sy, S["w"], wfstage[0:1, 0:D], bo_ap.rearrange("(a n) -> a n", a=1))

        @block.vector
        def _(v):
            v.wait_ge(S["w"].h, S["w"].n - 16)
            S["wc"].inc(v.tensor_copy(WrT_sb[:, :, :],
                                      wfstage[:, 0:256].rearrange("p (c e) -> p c e", c=DMC)), 1)
            v.wait_ge(S["w"].h, S["w"].n)
            S["wc"].inc(v.tensor_copy(borow_bf[:, :], wfstage[0:1, 0:D]), 1)

        @block.tensor
        def _(t):
            t.wait_ge(S["w"].h, S["w"].n)
            t.wait_ge(S["wg"].h, S["wg"].n)
            t.matmul(PS[:, 3, 0:512], consts["ones_f32"][0:1, :], bvrow[0:1, :],
                     start=True, stop=True)
            S["mm"].inc(t.matmul(PS[:, 2, 0:8], consts["ones_f32"][0:1, :], brrow[0:1, :],
                                 start=True, stop=True), 1)

        @block.scalar
        def _(s):
            s.wait_ge(S["mm"].h, bMM2 + 1)
            s.copy(bias_bcast[:, :], PS[:, 3, 0:512])
            S["ev"].inc(s.copy(br_bcast[:, :], PS[:, 2, 0:8]), 1)

        barrier(block, S, waits=[(S["wc"], S["wc"].n), (S["ev"], S["ev"].n), (S["wg"], S["wg"].n)])

        nMM = S["mm"].n
        nEV = S["ev"].n
        bWC1 = S["wc"].n
        wf32s = wf32.ap().rearrange("p b n -> p (b n)").rearrange("p (b n) -> p b n", b=4)
        wbfs = wbf.ap().rearrange("p b n -> p (b n)").rearrange("p (b n) -> p b n", b=4)
        wsem = {0: S["w"], 1: S["w2"]}
        bW1 = {0: S["w"].n, 1: S["w2"].n}

        def c1_load(eng, i):
            og, dmc = divmod(i, DMC)
            if i >= 4:
                eng.wait_ge(S["wc"].h, bWC1 + i - 3)
            q = wsem[i % 2]
            dmaw(eng, q, wf32s[:, i % 4, 0:256],
                 wqkT_q[dmc * 128:(dmc + 1) * 128, og * 256:(og + 1) * 256])
            dmaw(eng, q, wf32s[:, i % 4, 256:512],
                 wqkT_k[dmc * 128:(dmc + 1) * 128, og * 256:(og + 1) * 256])

        @block.sync
        def _(sy):
            for i in range(0, 2 * DMC, 2):
                c1_load(sy, i)

        @block.vector
        def _(v):
            nld = {0: 0, 1: 0}
            for i in range(2 * DMC):
                nld[i % 2] += 32
                v.wait_ge(wsem[i % 2].h, bW1[i % 2] + nld[i % 2])
                if i >= 4:
                    v.wait_ge(S["mm"].h, nMM + i - 3)
                S["wc"].inc(v.tensor_copy(wbfs[:, i % 4, :], wf32s[:, i % 4, :]), 1)
                if prep is not None and i % 2 == 1:
                    drain_casts_on(v, 1)

        @block.tensor
        def _(t):
            for i in range(2 * DMC):
                og, dmc = divmod(i, DMC)
                t.wait_ge(S["wc"].h, bWC1 + i + 1)
                if dmc == 0 and og == 1:
                    t.wait_ge(S["ev"].h, nEV + 8)
                last = None
                for j2 in range(2):
                    for half in range(2):
                        t.matmul(PS[:, j2 * 2 + half, :], wbfs[:, i % 4, j2 * 128:(j2 + 1) * 128],
                                 xn_T[:, dmc, half * 512:(half + 1) * 512],
                                 start=(dmc == 0), stop=(dmc == DMC - 1))
                        last = t.matmul(PS[:, 4 + j2 * 2 + half, :],
                                        wbfs[:, i % 4, 256 + j2 * 128:256 + (j2 + 1) * 128],
                                        xn_T[:, dmc, half * 512:(half + 1) * 512],
                                        start=(dmc == 0), stop=(dmc == DMC - 1))
                S["mm"].inc(last, 1)

        @block.scalar
        def _(s):
            for og in range(2):
                for i in range(og * DMC + 1, (og + 1) * DMC, 2):
                    c1_load(s, i)
                s.wait_ge(S["mm"].h, nMM + (og + 1) * DMC)
                last = None
                for j2 in range(2):
                    j = og * 2 + j2
                    for half in range(2):
                        s.activation(out=QT[:, j, half * 512:(half + 1) * 512],
                                     in_=PS[:, j2 * 2 + half, :],
                                     func=AF.Identity, bias=b_pair_q[:, j:j + 1], scale=1.0)
                        last = s.activation(out=KT[:, j, half * 512:(half + 1) * 512],
                                            in_=PS[:, 4 + j2 * 2 + half, :],
                                            func=AF.Identity, bias=b_pair_k[:, j:j + 1], scale=1.0)
                S["ev"].inc(last, 8)

        barrier(block, S, waits=[(S["ev"], S["ev"].n)])

        nEVk0 = S["ev"].n

        ncc0 = S["cc"].n
        CC_K = [ncc0 + 1, ncc0 + 2, ncc0 + 4, ncc0 + 5]
        CC_V = [ncc0 + 3, ncc0 + 6, ncc0 + 7, ncc0 + 8]

        @block.gpsimd
        def _(g):
            for j in range(4):
                g.wait_ge(S["ev"].h, nEVk0 - 16 + (8 if j < 2 else 16))
                dmaw(g, S["kv"], kt_own_p[j], KT[:, j, :])
            g.wait_ge(S["kv"].h, S["kv"].n)
            for j in range(2):
                S["cc"].inc(g.collective_compute(
                    "AllGather", ALU.bypass,
                    replica_groups=[[0, 1, 2, 3], [4, 5, 6, 7]],
                    ins=[kt_own_p[j].opt()], outs=[kt_all_p[j].opt()]), 1)

        nWC2 = S["wc"].n
        nMM2 = S["mm"].n
        bW3 = {0: S["w"].n, 1: S["w2"].n}

        def c3_load(eng, dmc):
            if dmc >= 4:
                eng.wait_ge(S["wc"].h, nWC2 + dmc - 3)
            dmaw(eng, wsem[dmc % 2], wf32s[:, dmc % 4, 0:512],
                 wvT[dmc * 128:(dmc + 1) * 128, :])

        @block.sync
        def _(sy):
            for dmc in range(0, DMC, 2):
                c3_load(sy, dmc)

        @block.scalar
        def _(s):
            for dmc in range(1, DMC, 2):
                c3_load(s, dmc)

        @block.vector
        def _(v):
            nld = {0: 0, 1: 0}
            for dmc in range(DMC):
                nld[dmc % 2] += 16
                v.wait_ge(wsem[dmc % 2].h, bW3[dmc % 2] + nld[dmc % 2])
                if dmc >= 4:
                    v.wait_ge(S["mm"].h, nMM2 + dmc - 3)
                S["wc"].inc(v.tensor_copy(wbfs[:, dmc % 4, :], wf32s[:, dmc % 4, :]), 1)

        @block.tensor
        def _(t):
            for dmc in range(DMC):
                t.wait_ge(S["wc"].h, nWC2 + dmc + 1)
                last = None
                for tc in range(TC):
                    last = t.matmul(PS[:, tc, :], xn_T[:, dmc, tc * 128:(tc + 1) * 128],
                                    wbfs[:, dmc % 4, :],
                                    start=(dmc == 0), stop=(dmc == DMC - 1))
                S["mm"].inc(last, 1)

        @block.vector
        def _(v):
            v.wait_ge(S["mm"].h, nMM2 + DMC)
            last = None
            for tc in range(TC):
                dst = v_full[:, tc, :].rearrange("p (h c) -> p h c", c=65)[:, :, 0:64]
                last = v.scalar_tensor_tensor(
                    out=dst, in0=PS[:, tc, :].rearrange("p (h e) -> p h e", e=64),
                    scalar=1.0, in1=bias_bcast[:, :].rearrange("p (h e) -> p h e", e=64),
                    op0=ALU.mult, op1=ALU.add)
            S["ev"].inc(last, 1)

        @block.gpsimd
        def _(g):
            g.wait_ge(S["ev"].h, S["ev"].n)
            for j in range(4):
                dmaw(g, S["kv"], v_own_p[j].rearrange("(c p) n -> p c n", p=128),
                     v_full[:, :, 130 * j:130 * (j + 1)])
            g.wait_ge(S["kv"].h, S["kv"].n)

            def ag(src, dst):
                S["cc"].inc(g.collective_compute(
                    "AllGather", ALU.bypass,
                    replica_groups=[[0, 1, 2, 3], [4, 5, 6, 7]],
                    ins=[src.opt()], outs=[dst.opt()]), 1)
            ag(v_own_p[0], v_all_p[0])
            ag(kt_own_p[2], kt_all_p[2])
            ag(kt_own_p[3], kt_all_p[3])
            for j in range(1, 4):
                ag(v_own_p[j], v_all_p[j])

        barrier(block, S, waits=[(S["ev"], S["ev"].n)])

        if prep is not None:
            @block.gpsimd
            def _(g):
                for k in range(PRE, len(plan)):
                    w, r0, c0, ncols, kind = plan[k]
                    if k >= PRE + 4:
                        g.wait_ge(S["lwc"].h, blwc + k - 3)
                    dmaw(g, S["lw"], pf32s[k % 4][:, 0:ncols], prep[w][r0:r0 + 128, c0:c0 + ncols])
                    if k >= PRE + 1:
                        pw = k - 1
                        dst, dn = _dst(*plan[pw])
                        g.wait_ge(S["lwc"].h, blwc + pw + 1)
                        dmaw(g, S["lwo"], dst, pbf[:, pw % 2, 0:dn])
                pw = len(plan) - 1
                dst, dn = _dst(*plan[pw])
                g.wait_ge(S["lwc"].h, blwc + pw + 1)
                dmaw(g, S["lwo"], dst, pbf[:, pw % 2, 0:dn])

        nMM3 = S["mm"].n
        nR1c = S["r1"].n

        @block.tensor
        def _(t):
            last = None
            for dmc in range(DMC):
                for tc in range(TC):
                    last = t.matmul(PS[:, tc, 0:8], xn_T[:, dmc, tc * 128:(tc + 1) * 128],
                                    WrT_sb[:, dmc, :],
                                    start=(dmc == 0), stop=(dmc == DMC - 1))
            S["mm"].inc(last, 1)

        @block.vector
        def _(v):
            v.wait_ge(S["mm"].h, nMM3 + 1)
            last = None
            for tc in range(TC):
                last = v.tensor_tensor(logits[:, tc, :], PS[:, tc, 0:8],
                                       br_bcast[:, :], ALU.add)
            S["r1"].inc(last, 1)

        @block.tensor
        def _(t):
            t.wait_ge(S["r1"].h, nR1c + 1)
            for half in range(2):
                last = None
                for dmc in range(DMC):
                    last = t.matmul(PS[:, 1 + half, :], AT_sb[:, dmc, :],
                                    xn_T[:, dmc, half * 512:(half + 1) * 512],
                                    start=(dmc == 0), stop=(dmc == DMC - 1))
                S["mm"].inc(last, 1)

        @block.scalar
        def _(s):
            s.wait_ge(S["mm"].h, nMM3 + 3)
            for half in range(2):
                S["ev"].inc(s.copy(low_T[:, half * 512:(half + 1) * 512], PS[:, 1 + half, :]), 1)

        @block.vector
        def _(v):
            v.wait_ge(S["r1"].h, S["r1"].n)
            vch = S["vch"]
            bR2 = S["r2"].n
            for tc in range(TC):
                vch.inc(v.reduce_max(rmax[:, :], logits[:, tc, :], axis=mybir.AxisListType.X), 1)
                v.wait_ge(vch.h, vch.n)
                S["r1"].inc(v.tensor_scalar_mul(rnegmax[:, :], rmax[:, :], -1.0), 1)
                v.wait_ge(S["r2"].h, bR2 + tc + 1)
                vch.inc(v.reciprocal(rrecip[:, :], rsum[:, :]), 1)
                v.wait_ge(vch.h, vch.n)
                S["r1"].inc(v.tensor_scalar(sroute_bf[:, tc, :], sroute[:, tc, :],
                                            rrecip[:, :], 16.0, ALU.mult, ALU.mult), 1)

        @block.scalar
        def _(s):
            base = S["r1"].n - 2 * TC
            for tc in range(TC):
                s.wait_ge(S["r1"].h, base + 2 * tc + 1)
                S["r2"].inc(s.activation(out=sroute[:, tc, :], in_=logits[:, tc, :],
                                         func=AF.Exp, bias=rnegmax[:, :], scale=1.0,
                                         accum_out=rsum[:, :]), 1)

        @block.tensor
        def _(t):
            t.wait_ge(S["r1"].h, S["r1"].n)
            last = None
            for tc in range(TC):
                last = t.transpose(ps_sb8[0:8, tc * 128:(tc + 1) * 128],
                                   sroute_bf[:, tc, :], consts["ident_bf"][:, :])
            S["att"].inc(last, 1)

        @block.scalar
        def _(s):
            s.wait_ge(S["att"].h, S["att"].n)
            S["ev"].inc(s.copy(sT[0:8, :], ps_sb8[0:8, :]), 1)

        @block.tensor
        def _(t):
            t.wait_ge(S["ev"].h, S["ev"].n)
            for half in range(2):
                S["mm"].inc(t.matmul(PS[:, 4 + half, :], E_exp[0:8, :],
                                     sT[0:8, half * 512:(half + 1) * 512],
                                     start=True, stop=True), 1)

        @block.vector
        def _(v):
            v.wait_ge(S["mm"].h, S["mm"].n)
            v.wait_ge(S["ev"].h, S["ev"].n)
            last = None
            for half in range(2):
                last = v.tensor_tensor(low_s[:, half * 512:(half + 1) * 512],
                                       low_T[:, half * 512:(half + 1) * 512],
                                       PS[:, 4 + half, :], ALU.mult)
            S["r1"].inc(last, 1)

        barrier(block, S, waits=[(S["r1"], S["r1"].n)])

        if dbg:
            @block.sync
            def _(sy):
                dmaw(sy, S["aux"], d_xnT, xn_T[:, :, :])
                dmaw(sy, S["aux"], d_QT, QT[:, :, :])
                dmaw(sy, S["aux"], d_KT, KT[:, :, :])
                dmaw(sy, S["aux"], d_V, v_full[:, :, :])
                dmaw(sy, S["aux"], d_logits, logits[:, :, :])
                dmaw(sy, S["aux"], d_lows, low_s[:, :])
                dmaw(sy, S["aux"], d_xsb, x_sb[:, 1, 0:D])
                dmaw(sy, S["aux"], d_mv, ln["mv"][:, :])
                dmaw(sy, S["aux"], d_rstd, ln["rstd"][:, :])
                dmaw(sy, S["aux"], d_xnl, xn_tm[:, 1, 0:D])

        kt_views = [ka.rearrange("(r p) t -> p r t", p=128) for ka in kt_all_p]
        v_views = [va.rearrange("(kc p) n -> p kc n", p=128) for va in v_all_p]
        cnt = {"kld": S["kld"].n}
        bxrD = S["xr"].n
        bsS, bpexp, bpvm, bep, batev, blbr, batt = (S["sS"].n, S["pexp"].n, S["pvm"].n,
                                                    S["ep"].n, S["atev"].n, S["lbr"].n,
                                                    S["att"].n)
        s_idx = [0]

        @block.sync
        def _(sy):
            for j in range(4):
                if j >= 2:
                    sy.wait_ge(S["pvm"].h, bpvm + (j - 1) * 128)
                sy.wait_ge(S["cc"].h, CC_K[j])
                dmaw(sy, S["kld"], K_pair[:, j % 2, :, :], kt_views[j][:, :, :])
                sy.wait_ge(S["cc"].h, CC_V[j])
                for q in range(4):
                    dmaw(sy, S["xr"], V_pair[:, j % 2, q * 8:(q + 1) * 8, :],
                         v_views[j][:, q * 8:(q + 1) * 8, :])

        @block.tensor
        def _(t):
            def emit_S(j, kc, u, si):
                hh, qh = divmod(u, 2)
                S["sS"].inc(t.matmul(
                    PS[:, si % 4, :],
                    K_pair[hh * 64:(hh + 1) * 64, j % 2, kc // 8, (kc % 8) * 128:(kc % 8 + 1) * 128],
                    QT[hh * 64:(hh + 1) * 64, j, qh * 512:(qh + 1) * 512],
                    start=True, stop=True, tile_position=(hh * 64, 0)), 1)

            def emit_PV(j, kc, u, si):
                hh = u // 2
                S["pvm"].inc(t.matmul(
                    PS[0:65, 4 + u, :],
                    V_pair[:, j % 2, kc, hh * 65:(hh + 1) * 65],
                    P_sb[:, si % 4, :],
                    start=(kc == 0), stop=(kc == 31)), 1)

            for j in range(4):
                t.wait_ge(S["kld"].h, cnt["kld"] + (j + 1) * 16)
                base_si = s_idx[0]
                for kc in range(32):
                    if s_idx[0] >= 4:
                        t.wait_ge(S["pexp"].h, bpexp + s_idx[0])
                    for u in range(4):
                        emit_S(j, kc, u, s_idx[0])
                        s_idx[0] += 1
                    if kc >= 1:
                        if kc == 1:
                            t.wait_ge(S["ep"].h, bep + j)
                            t.wait_ge(S["lbr"].h, blbr + j * 2)
                            t.wait_ge(S["xr"].h, bxrD + (j + 1) * 64)
                        t.wait_ge(S["pexp"].h, bpexp + base_si + 4 * kc)
                        for u in range(4):
                            emit_PV(j, kc - 1, u, base_si + 4 * (kc - 1) + u)
                t.wait_ge(S["pexp"].h, bpexp + base_si + 128)
                for u in range(4):
                    emit_PV(j, 31, u, base_si + 4 * 31 + u)
                for qh in range(2):
                    t.wait_ge(S["ep"].h, bep + j + 1)
                    t.matmul(PS[:, 4 + qh, :], consts["maskAB"][:, 0, :], lbf_pair[:, qh, :],
                             start=True, stop=False)
                    S["atev"].inc(t.matmul(PS[:, 4 + qh, :], consts["maskAB"][:, 1, :],
                                           lbf_pair[:, 2 + qh, :],
                                           start=False, stop=True), 1)

        @block.scalar
        def _(s):
            si = 0
            for j in range(4):
                for kc in range(32):
                    for u in range(4):
                        s.wait_ge(S["sS"].h, bsS + si + 1)
                        if si >= 4:
                            s.wait_ge(S["pvm"].h, bpvm + si - 3)
                        S["pexp"].inc(s.activation(
                            out=P_sb[:, si % 4, :], in_=PS[:, si % 4, :],
                            func=AF.Exp, bias=consts["shift"][:, :], scale=1.0), 1)
                        si += 1

        @block.vector
        def _(v):
            for j in range(4):
                drain_casts_on(v, 16)
                v.wait_ge(S["pvm"].h, bpvm + (j + 1) * 128)
                v.tensor_copy(attn_T[0:64, j, 0:512], PS[0:64, 4, :])
                v.tensor_copy(attn_T[0:64, j, 512:1024], PS[0:64, 5, :])
                v.tensor_copy(attn_T[64:128, j, 0:512], PS[0:64, 6, :])
                v.tensor_copy(attn_T[64:128, j, 512:1024], PS[0:64, 7, :])
                last = None
                for u in range(4):
                    last = v.tensor_copy(lbf_pair[:, u, :], PS[64:65, 4 + u, :])
                S["ep"].inc(last, 1)
                v.wait_ge(S["ep"].h, S["ep"].n)
                for qh in range(2):
                    v.wait_ge(S["atev"].h, batev + j * 2 + qh + 1)
                    S["lbr"].inc(v.reciprocal(rcpD[:, :], PS[:, 4 + qh, :]), 1)
                    v.wait_ge(S["lbr"].h, S["lbr"].n)
                    inst = v.scalar_tensor_tensor(
                        out=attn_T[:, j, qh * 512:(qh + 1) * 512], in0=rcpD[:, :],
                        scalar=INV_SQRT512, in1=attn_T[:, j, qh * 512:(qh + 1) * 512],
                        op0=ALU.mult, op1=ALU.mult)
                    if qh == 1:
                        S["att"].inc(inst, 1)

        barrier(block, S, waits=[(S["att"], S["att"].n)])

        if dbg:
            @block.sync
            def _(sy):
                dmaw(sy, S["aux"], d_attnT, attn_T[:, :, :])

        nW3 = S["w"].n
        nWC3 = S["wc"].n
        nPO = S["po"].n
        bXR = S["xr"].n
        bOUT = S["out2"].n
        bRESV = S["resv"].n

        @block.sync
        def _(sy):
            def load_wo(oc):
                if prep is not None:
                    if oc == 0:
                        sy.wait_ge(S["lwo"].h, prep["lwo_wo0"])
                    if oc >= 2:
                        sy.wait_ge(S["po"].h, nPO + (oc - 1) * 8)
                    dmaw(sy, S["w"], wobf[:, oc % 2, :, :],
                         prep["c_wo0"].rearrange("(c p) o -> p c o", p=128)[:, :, oc * 512:(oc + 1) * 512])
                else:
                    dmaw(sy, S["w"], wof32[:, :, :],
                         woT.rearrange("(c p) o -> p c o", p=128)[:, :, oc * 512:(oc + 1) * 512])
            load_wo(0)
            for oc in range(8):
                for tc in range(TC):
                    i = oc * 8 + tc
                    if i >= 2:
                        sy.wait_ge(S["resv"].h, bRESV + i - 1)
                    dmaw(sy, S["xr"], x_res[:, i % 2, :], x_ap[tc * 128:(tc + 1) * 128, oc * 512:(oc + 1) * 512])
                    if tc == 1 and oc + 1 < 8 and prep is not None:
                        load_wo(oc + 1)
                    if tc == 1 and oc + 1 < 8 and prep is None:
                        sy.wait_ge(S["wc"].h, nWC3 + oc + 1)
                        load_wo(oc + 1)

        @block.vector
        def _(v):
            def cast(oc):
                v.wait_ge(S["w"].h, nW3 + (oc + 1) * 16)
                if oc >= 2:
                    v.wait_ge(S["po"].h, nPO + (oc - 1) * 8)
                S["wc"].inc(v.tensor_copy(wobf[:, oc % 2, :, :], wof32[:, :, :]), 1)
            if prep is None:
                cast(0)
                cast(1)
            for oc in range(8):
                for tc in range(TC):
                    i = oc * 8 + tc
                    v.wait_ge(S["po"].h, nPO + i + 1)
                    v.wait_ge(S["xr"].h, bXR + (i + 1) * 16)
                    if i >= 2:
                        v.wait_ge(S["out2"].h, bOUT + (i - 1) * 16)
                    S["resv"].inc(v.scalar_tensor_tensor(
                        out=xo_sb[:, i % 2, :], in0=PS[:, i % 4, :], scalar=1.0,
                        in1=x_res[:, i % 2, :], op0=ALU.mult, op1=ALU.add), 1)
                    if prep is not None:
                        drain_casts_on(v, 1)
                if oc + 2 < 8 and prep is None:
                    cast(oc + 2)
            if prep is not None:
                drain_casts_on(v, 10 ** 9)

        @block.tensor
        def _(t):
            for oc in range(8):
                if prep is not None:
                    t.wait_ge(S["w"].h, nW3 + (oc + 1) * 16)
                else:
                    t.wait_ge(S["wc"].h, nWC3 + oc + 1)
                for tc in range(TC):
                    i = oc * 8 + tc
                    b = i % 4
                    if i >= 4:
                        t.wait_ge(S["resv"].h, bRESV + i - 3)
                    for jj in range(4):
                        t.matmul(PS[:, b, :], attn_T[:, jj, tc * 128:(tc + 1) * 128],
                                 wobf[:, oc % 2, jj, :], start=(jj == 0), stop=False)
                    t.matmul(PS[:, b, :], low_s[:, tc * 128:(tc + 1) * 128],
                             BT_sb[:, oc * 512:(oc + 1) * 512], start=False, stop=False)
                    S["po"].inc(t.matmul(PS[:, b, :], consts["ones_bf"][0:1, :],
                                         borow_bf[0:1, oc * 512:(oc + 1) * 512],
                                         start=False, stop=True), 1)

        @block.scalar
        def _(sc):
            for oc in range(8):
                for tc in range(TC):
                    i = oc * 8 + tc
                    sc.wait_ge(S["resv"].h, bRESV + i + 1)
                    dmaw(sc, S["out2"], xout[tc * 128:(tc + 1) * 128, oc * 512:(oc + 1) * 512],
                         xo_sb[:, i % 2, :])
            sc.wait_ge(S["out2"].h, S["out2"].n)

        @block.gpsimd
        def _(g):
            g.wait_ge(S["out2"].h, S["out2"].n)
            if dbg:
                g.wait_ge(S["aux"].h, S["aux"].n)

        @block.sync
        def _(sy):
            sy.wait_ge(S["out"].h, S["out"].n)

    ctx.close()
    return nc


def _host_layout_l0(inputs, bslice):
    Wqk = np.ascontiguousarray(inputs["L0_attn_Wqk_w"])
    rows = np.arange(1024)
    h = rows // 128
    qrows = np.empty(512, np.int64)
    krows = np.empty(512, np.int64)
    for j in range(4):
        for hh in range(2):
            hlin = 2 * j + hh
            e = np.arange(64)
            qrows[j * 128 + hh * 64 + e] = hlin * 128 + e * 2 + 0
            krows[j * 128 + hh * 64 + e] = hlin * 128 + e * 2 + 1
    wqkT_q = np.ascontiguousarray(Wqk[qrows, :].T)
    wqkT_k = np.ascontiguousarray(Wqk[krows, :].T)
    bq_h = np.ascontiguousarray(inputs["L0_attn_Wqk_b"][qrows].reshape(4, 128))
    bk_h = np.ascontiguousarray(inputs["L0_attn_Wqk_b"][krows].reshape(4, 128))
    wvT = np.ascontiguousarray(inputs["L0_attn_Wv_w"].T)
    woT = np.ascontiguousarray(inputs["L0_attn_Wo_w"].T)
    A = inputs["L0_ffn_A"]
    AT = np.ascontiguousarray(A.reshape(128, 4096).T)
    B = inputs["L0_ffn_B"]
    BT = np.ascontiguousarray(B.transpose(0, 2, 1).reshape(128, 4096))
    Wr = inputs["L0_ffn_Wr"]
    WrT = np.ascontiguousarray(Wr.T.reshape(32, 128, 8).transpose(1, 0, 2).reshape(128, 256))
    return {
        "x": np.ascontiguousarray(inputs["L0_attn_x"].reshape(2, 4, 1024, 4096)[bslice[0], bslice[1]]),
        "wqkT_q": wqkT_q, "wqkT_k": wqkT_k, "wvT": wvT, "woT": woT,
        "bq_h": bq_h, "bk_h": bk_h,
        "bv": np.ascontiguousarray(inputs["L0_attn_Wv_b"]),
        "bo": np.ascontiguousarray(inputs["L0_attn_Wo_b"]),
        "AT0": AT, "BT0": BT, "WrT": WrT,
        "br": np.ascontiguousarray(inputs["L0_ffn_br"]),
    }




def build_l1(dbg=False, shell=None, io=None, caches=None, skip_w=False):
    if shell is None:
        nc = bass.Bass("TRN2", target_bir_lowering=False, debug=False, detect_race_conditions=False)
        S0 = None
    else:
        nc, S0 = shell
    if io is None:
        x_ap = _dram(nc, "x", [TOK, D], F32, "ExternalInput")
        xout = _dram(nc, "xout", [TOK, D], F32, "ExternalOutput")
    else:
        x_ap, xout = io
    if not skip_w:
        wq_ap = _dram(nc, "wq", [D, D], F32, "ExternalInput")
        wk_ap = _dram(nc, "wk", [D, 1024], F32, "ExternalInput")
        wv_ap = _dram(nc, "wv", [D, 1024], F32, "ExternalInput")
        wo_ap = _dram(nc, "wo", [1024, D], F32, "ExternalInput")
    AT_ap = _dram(nc, "AT1", [D, 128], F32, "ExternalInput")
    BT_ap = _dram(nc, "BT1", [128, D], F32, "ExternalInput")
    RT_ap = _dram(nc, "RT", [128, 256], F32, "ExternalInput")

    kt_own = _dram(nc, "kt_own1", [1024, TOK], BF16)
    v_own = _dram(nc, "v_own1", [TOK, 1032], BF16)
    kt_all = _dram(nc, "kt_all1", [4 * 1024, TOK], BF16)
    v_all = _dram(nc, "v_all1", [4 * TOK, 1032], BF16)
    if caches is None:
        caches = {"c_wqe": _dram(nc, "c_wqe", [D, 1024], BF16, "ExternalInput"),
                  "c_wk": _dram(nc, "c_wk", [D, 1024], BF16, "ExternalInput"),
                  "c_wv": _dram(nc, "c_wv", [D, 1024], BF16, "ExternalInput"),
                  "c_wo": _dram(nc, "c_wo", [1024, D], BF16, "ExternalInput")}
    c_wqe, c_wk, c_wv, c_wo = caches["c_wqe"], caches["c_wk"], caches["c_wv"], caches["c_wo"]

    if dbg:
        d_xnT = _dram(nc, "d_xnT", [128, DMC, TOK], BF16, "ExternalOutput")
        d_QT = _dram(nc, "d_QT", [128, 8, TOK], BF16, "ExternalOutput")
        d_KT = _dram(nc, "d_KT", [128, 8, TOK], BF16, "ExternalOutput")
        d_V = _dram(nc, "d_V", [128, TC, 1032], BF16, "ExternalOutput")
        d_logits = _dram(nc, "d_logits", [128, TC, 8], F32, "ExternalOutput")
        d_lows = _dram(nc, "d_lows", [128, TOK], BF16, "ExternalOutput")
        d_attnT = _dram(nc, "d_attnT", [128, 8, TOK], BF16, "ExternalOutput")

    ctx = ExitStack()
    sb, S = _alloc_common(nc, ctx, S0, sfx="_b")
    ps = lambda n, shp, dt: ctx.enter_context(nc.psum_tensor(n + "_b", shp, dt))

    x_sb = sb("x_sb1", [128, 2, 4224], F32)
    xn_tm = sb("xn_tm", [128, 2, D], BF16)
    wfstage = xn_tm.ap().rearrange("p b n -> p (b n)").bitcast(F32)
    xn_T = sb("xn_T", [128, DMC, TOK], BF16)
    ln = {"stats": sb("ln_stats", [128, 8, 6], F32), "mv": sb("ln_mv", [128, 2], F32),
          "vtmp": sb("ln_vtmp", [128, 1], F32), "negmean": sb("ln_negmean", [128, 1], F32),
          "rstd": sb("ln_rstd", [128, 1], F32), "sd": sb("ln_sd", [128, 1], F32)}
    consts = {"ident_bf": sb("ident_bf", [128, 128], BF16),
              "eps": sb("eps", [128, 1], F32),
              "ones_bf": sb("ones_bf", [1, 128], BF16),
              "sc128": sb("sc128", [128, 1], F32)}
    QT = sb("QT", [128, 8, TOK], BF16)
    KT = sb("KT", [128, 8, TOK], BF16)
    v_full = sb("v_full", [128, TC, 1032], BF16)
    AT_sb = sb("AT_sb", [128, DMC, 128], BF16)
    BT_sb = sb("BT_sb", [128, D], BF16)
    RT_sb = sb("RT_sb", [128, DMC, 8], BF16)
    low_T = sb("low_T", [128, TOK], F32)
    low_s = sb("low_s", [128, TOK], BF16)
    logits = sb("logits", [128, TC, 8], F32)
    stok_bf = sb("stok_bf", [128, TC, 8], BF16)
    E_exp = sb("E_exp", [8, 128], BF16)
    rmax = sb("rmax", [128, 1], F32)
    P_sb = sb("P_sb", [128, 4, 512], BF16)
    sT = P_sb.ap().rearrange("p b n -> p (b n)")[0:8, 0:1024]
    attn_stage = sb("attn_stage", [128, 8, 128], BF16)
    attn_T = KT
    lrecip = sb("lrecip", [128, 1], F32)

    AR = x_sb.ap().rearrange("p b n -> p (b n)")
    ARb = AR.bitcast(BF16)
    wqf32 = AR[:, 0:4096]
    wqbf = ARb[:, 8192:12288]
    K_h = ARb[:, 0:8192].rearrange("p (b r t) -> p b r t", b=2, r=4)
    V_h = ARb[:, 8192:16448].rearrange("p (b k c) -> p b k c", b=2, k=32)
    x_res = AR[:, 4608:5632].rearrange("p (b n) -> p b n", b=2)
    xo_sb = AR[:, 5632:6656].rearrange("p (b n) -> p b n", b=2)

    PS = ps("PS1", [128, 8, 512], F32)
    ps_tp = PS[:, 0:2, :].rearrange("p b n -> p (b n)").bitcast(BF16).rearrange(
        "p (b n) -> p b n", b=2)
    ps_atp = PS[:, 0, :].bitcast(BF16)
    ps_sb8 = PS[:, 3, :].bitcast(BF16)

    with nc.Block() as block:
        @block.gpsimd
        def _(g):
            aux = S["aux"]

            def step(inst):
                aux.inc(inst, 1)
                g.wait_ge(aux.h, aux.n)
            step(g.memset(consts["ident_bf"][:, :], 0.0))
            step(g.affine_select(out=consts["ident_bf"][:, :], in_=consts["ident_bf"][:, :],
                                 compare_op=ALU.not_equal, fill=1.0, base=0,
                                 pattern=[[-1, 128]], channel_multiplier=1))
            step(g.memset(consts["eps"][:, :], 1e-5))
            step(g.memset(consts["ones_bf"][:, :], 1.0))
            step(g.memset(consts["sc128"][:, :], INV_SQRT128))
            step(g.memset(E_exp[:, :], 1.0))
            step(g.affine_select(out=E_exp[:, :], in_=E_exp[:, :],
                                 compare_op=ALU.is_ge, fill=0.0, base=0,
                                 pattern=[[1, 128]], channel_multiplier=-16))
            step(g.affine_select(out=E_exp[:, :], in_=E_exp[:, :],
                                 compare_op=ALU.is_ge, fill=0.0, base=15,
                                 pattern=[[-1, 128]], channel_multiplier=16))
            step(g.memset(v_full[:, :, :].rearrange("p t (h c) -> p t h c", c=129)[:, :, :, 128:129], 1.0))
            S["const"].inc(g.nop(), 1)

        _emit_ln(nc, block, S, x_ap, x_sb, xn_tm, xn_T, ln, ps_tp, consts, cN=S["const"].n)

        barrier(block, S, waits=[(S["tpev"], S["tpev"].n), (S["const"], S["const"].n)])

        bWCl1 = S["wc"].n

        @block.sync
        def _(sy):
            dmaw(sy, S["w"], wfstage.rearrange("p (c n) -> p c n", c=DMC),
                 AT_ap.rearrange("(c p) n -> p c n", p=128))

        @block.vector
        def _(v):
            v.wait_ge(S["w"].h, S["w"].n)
            S["wc"].inc(v.tensor_copy(AT_sb[:, :, :],
                                      wfstage.rearrange("p (c n) -> p c n", c=DMC)), 1)

        @block.sync
        def _(sy):
            sy.wait_ge(S["wc"].h, bWCl1 + 1)
            dmaw(sy, S["w"], wfstage, BT_ap)

        @block.vector
        def _(v):
            v.wait_ge(S["w"].h, S["w"].n)
            S["wc"].inc(v.tensor_copy(BT_sb[:, :], wfstage), 1)

        @block.sync
        def _(sy):
            sy.wait_ge(S["wc"].h, bWCl1 + 2)
            dmaw(sy, S["w"], wfstage[:, 0:256], RT_ap[:, :])

        @block.vector
        def _(v):
            v.wait_ge(S["w"].h, S["w"].n)
            S["wc"].inc(v.tensor_copy(RT_sb[:, :, :],
                                      wfstage[:, 0:256].rearrange("p (c e) -> p c e", c=DMC)), 1)

        barrier(block, S, waits=[(S["wc"], S["wc"].n)])

        wstr = ARb[:, 0:4096].rearrange("p (b n) -> p b n", b=8)
        wsem = {0: S["w"], 1: S["w2"]}

        def proj_bf(src_ap, OUT_sb, gate=None):
            nMM2 = S["mm"].n
            nEV2 = S["ev"].n
            bW2 = {0: S["w"].n, 1: S["w2"].n}

            def ld(eng, i):
                og, dmc = divmod(i, DMC)
                if i < 2 and gate is not None:
                    eng.wait_ge(S["lwo"].h, gate)
                if i >= 8:
                    eng.wait_ge(S["mm"].h, nMM2 + i - 7)
                dmaw(eng, wsem[i % 2], wstr[:, i % 8, :],
                     src_ap[dmc * 128:(dmc + 1) * 128, og * 512:(og + 1) * 512])

            @block.sync
            def _(sy):
                for i in range(0, 2 * DMC, 2):
                    ld(sy, i)

            @block.tensor
            def _(t):
                nld = {0: 0, 1: 0}
                for i in range(2 * DMC):
                    og, dmc = divmod(i, DMC)
                    nld[i % 2] += 16
                    t.wait_ge(wsem[i % 2].h, bW2[i % 2] + nld[i % 2])
                    if dmc == 0 and og == 1:
                        t.wait_ge(S["ev"].h, nEV2 + 8)
                    last = None
                    for oc in range(4):
                        for half in range(2):
                            last = t.matmul(PS[:, oc * 2 + half, :],
                                            wstr[:, i % 8, oc * 128:(oc + 1) * 128],
                                            xn_T[:, dmc, half * 512:(half + 1) * 512],
                                            start=(dmc == 0), stop=(dmc == DMC - 1))
                    S["mm"].inc(last, 1)

            @block.scalar
            def _(s):
                for og in range(2):
                    for i in range(og * DMC + 1, (og + 1) * DMC, 2):
                        ld(s, i)
                    s.wait_ge(S["mm"].h, nMM2 + (og + 1) * DMC)
                    last = None
                    for oc in range(4):
                        for half in range(2):
                            last = s.copy(OUT_sb[:, og * 4 + oc, half * 512:(half + 1) * 512],
                                          PS[:, oc * 2 + half, :])
                    S["ev"].inc(last, 8)

        proj_bf(c_wk, KT, gate=caches.get("lwo_wk"))
        barrier(block, S, waits=[(S["ev"], S["ev"].n)])

        nEVk1 = S["ev"].n

        @block.gpsimd
        def _(g):
            g.wait_ge(S["ev"].h, nEVk1)
            dmaw(g, S["kv"], kt_own.rearrange("(c p) t -> p c t", p=128), KT[:, :, :])
            g.wait_ge(S["kv"].h, S["kv"].n)
            S["cc"].inc(g.collective_compute(
                "AllGather", ALU.bypass,
                replica_groups=[[0, 1, 2, 3], [4, 5, 6, 7]],
                ins=[kt_own.opt()], outs=[kt_all.opt()]), 1)

        nMM3 = S["mm"].n
        nEVv = S["ev"].n
        bWv = {0: S["w"].n, 1: S["w2"].n}

        def vld(eng, i):
            dv, dmc = divmod(i, DMC)
            if i < 2 and caches.get("lwo_wv") is not None:
                eng.wait_ge(S["lwo"].h, caches["lwo_wv"])
            if i >= 8:
                eng.wait_ge(S["mm"].h, nMM3 + i - 7)
            dmaw(eng, wsem[i % 2], wstr[:, i % 8, :],
                 c_wv[dmc * 128:(dmc + 1) * 128, dv * 512:(dv + 1) * 512])

        @block.sync
        def _(sy):
            for i in range(0, 2 * DMC, 2):
                vld(sy, i)

        @block.tensor
        def _(t):
            nld = {0: 0, 1: 0}
            for dv in range(2):
                for dmc in range(DMC):
                    i = dv * DMC + dmc
                    nld[i % 2] += 16
                    t.wait_ge(wsem[i % 2].h, bWv[i % 2] + nld[i % 2])
                    if dmc == 0 and dv == 1:
                        t.wait_ge(S["ev"].h, nEVv + 1)
                    last = None
                    for tc in range(TC):
                        last = t.matmul(PS[:, tc, :], xn_T[:, dmc, tc * 128:(tc + 1) * 128],
                                        wstr[:, i % 8, :],
                                        start=(dmc == 0), stop=(dmc == DMC - 1))
                    S["mm"].inc(last, 1)

        @block.scalar
        def _(s):
            for dv in range(2):
                for i in range(dv * DMC + 1, (dv + 1) * DMC, 2):
                    vld(s, i)
                s.wait_ge(S["mm"].h, nMM3 + (dv + 1) * DMC)
                last = None
                for tc in range(TC):
                    dst = v_full[:, tc, :].rearrange("p (h c) -> p h c", c=129)[:, dv * 4:(dv + 1) * 4, 0:128]
                    last = s.copy(dst, PS[:, tc, :].rearrange("p (h e) -> p h e", e=128))
                S["ev"].inc(last, 1)

        @block.gpsimd
        def _(g):
            g.wait_ge(S["ev"].h, S["ev"].n)
            dmaw(g, S["kv"], v_own.rearrange("(c p) n -> p c n", p=128), v_full[:, :, :])
            g.wait_ge(S["kv"].h, S["kv"].n)
            S["cc"].inc(g.collective_compute(
                "AllGather", ALU.bypass,
                replica_groups=[[0, 1, 2, 3], [4, 5, 6, 7]],
                ins=[v_own.opt()], outs=[v_all.opt()]), 1)

        barrier(block, S, waits=[(S["ev"], S["ev"].n)])

        proj_bf(c_wqe, QT, gate=caches.get("lwo_wq"))
        barrier(block, S, waits=[(S["ev"], S["ev"].n)])

        nMM5 = S["mm"].n
        nR1 = S["r1"].n

        @block.tensor
        def _(t):
            last = None
            for dmc in range(DMC):
                for tc in range(TC):
                    last = t.matmul(PS[:, tc, 0:8], xn_T[:, dmc, tc * 128:(tc + 1) * 128],
                                    RT_sb[:, dmc, :],
                                    start=(dmc == 0), stop=(dmc == DMC - 1))
            S["mm"].inc(last, 1)

        @block.vector
        def _(v):
            v.wait_ge(S["mm"].h, nMM5 + 1)
            last = None
            for tc in range(TC):
                last = v.tensor_copy(logits[:, tc, :], PS[:, tc, 0:8])
            S["r1"].inc(last, 1)

        @block.tensor
        def _(t):
            t.wait_ge(S["r1"].h, nR1 + 1)
            for half in range(2):
                last = None
                for dmc in range(DMC):
                    last = t.matmul(PS[:, 1 + half, :], AT_sb[:, dmc, :],
                                    xn_T[:, dmc, half * 512:(half + 1) * 512],
                                    start=(dmc == 0), stop=(dmc == DMC - 1))
                S["mm"].inc(last, 1)

        @block.scalar
        def _(s):
            s.wait_ge(S["mm"].h, nMM5 + 3)
            for half in range(2):
                S["ev"].inc(s.copy(low_T[:, half * 512:(half + 1) * 512], PS[:, 1 + half, :]), 1)

        @block.vector
        def _(v):
            vch = S["vch"]
            v.wait_ge(S["r1"].h, S["r1"].n)
            for tc in range(TC):
                vch.inc(v.reduce_max(rmax[:, :], logits[:, tc, :], axis=mybir.AxisListType.X), 1)
                v.wait_ge(vch.h, vch.n)
                S["r1"].inc(v.tensor_scalar(stok_bf[:, tc, :], logits[:, tc, :],
                                            rmax[:, :], rmax[:, :], ALU.is_ge, ALU.mult), 1)

        @block.tensor
        def _(t):
            t.wait_ge(S["r1"].h, S["r1"].n)
            last = None
            for tc in range(TC):
                last = t.transpose(ps_sb8[0:8, tc * 128:(tc + 1) * 128],
                                   stok_bf[:, tc, :], consts["ident_bf"][:, :])
            S["att"].inc(last, 1)

        @block.scalar
        def _(s):
            s.wait_ge(S["att"].h, S["att"].n)
            S["ev"].inc(s.copy(sT[0:8, :], ps_sb8[0:8, :]), 1)

        @block.tensor
        def _(t):
            t.wait_ge(S["ev"].h, S["ev"].n)
            for half in range(2):
                S["mm"].inc(t.matmul(PS[:, 4 + half, :], E_exp[0:8, :],
                                     sT[0:8, half * 512:(half + 1) * 512],
                                     start=True, stop=True), 1)

        @block.vector
        def _(v):
            v.wait_ge(S["mm"].h, S["mm"].n)
            v.wait_ge(S["ev"].h, S["ev"].n)
            last = None
            for half in range(2):
                last = v.tensor_tensor(low_s[:, half * 512:(half + 1) * 512],
                                       low_T[:, half * 512:(half + 1) * 512],
                                       PS[:, 4 + half, :], ALU.mult)
            S["r1"].inc(last, 1)

        barrier(block, S, waits=[(S["r1"], S["r1"].n), (S["cc"], S["cc"].n)])

        if dbg:
            @block.sync
            def _(sy):
                dmaw(sy, S["aux"], d_xnT, xn_T[:, :, :])
                dmaw(sy, S["aux"], d_QT, QT[:, :, :])
                dmaw(sy, S["aux"], d_KT, KT[:, :, :])
                dmaw(sy, S["aux"], d_V, v_full[:, :, :])
                dmaw(sy, S["aux"], d_logits, logits[:, :, :])
                dmaw(sy, S["aux"], d_lows, low_s[:, :])

        kt_view = kt_all.rearrange("(r c p) t -> p r c t", r=4, p=128)
        v_view = v_all.rearrange("(kc p) n -> p kc n", p=128)
        kldD = S["kld"].n
        attD0 = S["att"].n
        bsS1, bpexp1, bpvm1, bep1, batev1 = (S["sS"].n, S["pexp"].n, S["pvm"].n,
                                             S["ep"].n, S["atev"].n)
        s_idx = [0]

        @block.sync
        def _(sy):
            for h in range(8):
                if h >= 2:
                    sy.wait_ge(S["ep"].h, bep1 + (h - 1) * 2)
                dmaw(sy, S["kld"], K_h[:, h % 2, :, :], kt_view[:, :, h, :])
                for q in range(4):
                    dmaw(sy, S["kld"], V_h[:, h % 2, q * 8:(q + 1) * 8, :],
                         v_view[:, q * 8:(q + 1) * 8, 129 * h:129 * (h + 1)])

        @block.tensor
        def _(t):
            def emit_S(h, half, kc, si):
                bk = si % 4
                last = t.matmul(PS[:, bk, :],
                                K_h[:, h % 2, kc // 8, (kc % 8) * 128:(kc % 8 + 1) * 128],
                                QT[:, h, half * 512:(half + 1) * 512],
                                start=True, stop=True)
                S["sS"].inc(last, 1)

            def emit_PV(h, half, kc, si):
                if kc == 0:
                    e = h * 2 + half
                    if e > 0:
                        t.wait_ge(S["ep"].h, bep1 + e)
                for qc in range(4):
                    lastpv = t.matmul(PS[:, 4 + qc, 0:129],
                                      P_sb[:, si % 4, qc * 128:(qc + 1) * 128],
                                      V_h[:, h % 2, kc, :],
                                      start=(kc == 0), stop=(kc == 31))
                S["pvm"].inc(lastpv, 1)
            for h in range(8):
                t.wait_ge(S["kld"].h, kldD + (h + 1) * 80)
                if h >= 1:
                    t.wait_ge(S["atev"].h, batev1 + h)
                pend = []
                for half in range(2):
                    for kc in range(32):
                        si = s_idx[0]
                        if si % 2 == 0 and si >= 2:
                            t.wait_ge(S["pexp"].h, bpexp1 + si - 2)
                        emit_S(h, half, kc, si)
                        if si % 2 == 1 and len(pend) >= 2:
                            a = pend.pop(0)
                            b = pend.pop(0)
                            t.wait_ge(S["pexp"].h, bpexp1 + b[3] + 1)
                            emit_PV(*a)
                            emit_PV(*b)
                        pend.append((h, half, kc, si))
                        s_idx[0] += 1
                t.wait_ge(S["pexp"].h, bpexp1 + pend[-1][3] + 1)
                for d in pend:
                    emit_PV(*d)
                t.wait_ge(S["ep"].h, bep1 + (h + 1) * 2)
                t.wait_ge(S["pexp"].h, bpexp1 + (h + 1) * 64)
                last = None
                for i in range(8):
                    last = t.transpose(ps_atp[:, i * 128:(i + 1) * 128],
                                       attn_stage[:, i, :], consts["ident_bf"][:, :])
                S["att"].inc(last, 1)

        @block.scalar
        def _(s):
            si = 0

            def atev_copy(hh):
                s.wait_ge(S["att"].h, attD0 + hh + 1)
                S["atev"].inc(s.copy(attn_T[:, hh, :], ps_atp[:, :]), 1)
            for h in range(8):
                if h >= 1:
                    atev_copy(h - 1)
                for half in range(2):
                    for kc in range(32):
                        bk = si % 4
                        s.wait_ge(S["sS"].h, bsS1 + si + 1)
                        if si >= 4:
                            s.wait_ge(S["pvm"].h, bpvm1 + si - 3)
                        S["pexp"].inc(
                            s.activation(out=P_sb[:, bk, :], in_=PS[:, bk, :],
                                         func=AF.Exp, scale=consts["sc128"][:, :]), 1)
                        si += 1
            atev_copy(7)

        @block.vector
        def _(v):
            vch = S["vch"]
            for h in range(8):
                for half in range(2):
                    e = h * 2 + half
                    v.wait_ge(S["pvm"].h, bpvm1 + (e + 1) * 32)
                    if h >= 1 and half == 0:
                        v.wait_ge(S["att"].h, attD0 + h)
                    last = None
                    for qc in range(4):
                        vch.inc(v.reciprocal(lrecip[:, :], PS[:, 4 + qc, 128:129]), 1)
                        v.wait_ge(vch.h, vch.n)
                        last = v.tensor_scalar(attn_stage[:, half * 4 + qc, :],
                                               PS[:, 4 + qc, 0:128],
                                               lrecip[:, :], None, ALU.mult)
                    S["ep"].inc(last, 1)

        barrier(block, S, waits=[(S["atev"], S["atev"].n)])

        if dbg:
            @block.sync
            def _(sy):
                dmaw(sy, S["aux"], d_attnT, attn_T[:, :, :])

        nW4 = S["w"].n
        nWC4 = S["wc"].n
        nPO = S["po"].n
        bXR1 = S["w2"].n
        bOUT1 = S["out"].n
        bRESV1 = S["resv"].n
        OCN, OW = 8, 512
        wobf = ARb[:, 0:8192].rearrange("p (b c n) -> p b c n", b=2, c=8)

        @block.sync
        def _(sy):
            for oc in range(OCN):
                if oc == 0 and caches.get("lwo_all") is not None:
                    sy.wait_ge(S["lwo"].h, caches["lwo_all"])
                if oc >= 2:
                    sy.wait_ge(S["po"].h, nPO + (oc - 1) * 8)
                dmaw(sy, S["w"], wobf[:, oc % 2, :, :],
                     c_wo.rearrange("(c p) o -> p c o", p=128)[:, :, oc * OW:(oc + 1) * OW])

        @block.scalar
        def _(s):
            for oc in range(OCN):
                for tc in range(TC):
                    i = oc * 8 + tc
                    if i >= 2:
                        s.wait_ge(S["resv"].h, bRESV1 + i - 1)
                    dmaw(s, S["w2"], x_res[:, i % 2, :],
                         x_ap[tc * 128:(tc + 1) * 128, oc * OW:(oc + 1) * OW])

        @block.vector
        def _(v):
            for oc in range(OCN):
                for tc in range(TC):
                    i = oc * 8 + tc
                    v.wait_ge(S["po"].h, nPO + i + 1)
                    v.wait_ge(S["w2"].h, bXR1 + (i + 1) * 16)
                    if i >= 2:
                        v.wait_ge(S["out"].h, bOUT1 + (i - 1) * 16)
                    S["resv"].inc(v.scalar_tensor_tensor(
                        out=xo_sb[:, i % 2, :], in0=PS[:, i % 4, 0:OW], scalar=1.0,
                        in1=x_res[:, i % 2, :], op0=ALU.mult, op1=ALU.add), 1)

        @block.tensor
        def _(t):
            for oc in range(OCN):
                t.wait_ge(S["w"].h, nW4 + (oc + 1) * 16)
                for tc in range(TC):
                    i = oc * 8 + tc
                    b = i % 4
                    if i >= 4:
                        t.wait_ge(S["resv"].h, bRESV1 + i - 3)
                    for h in range(8):
                        t.matmul(PS[:, b, 0:OW], attn_T[:, h, tc * 128:(tc + 1) * 128],
                                 wobf[:, oc % 2, h, :], start=(h == 0), stop=False)
                    S["po"].inc(t.matmul(PS[:, b, 0:OW], low_s[:, tc * 128:(tc + 1) * 128],
                                         BT_sb[:, oc * OW:(oc + 1) * OW],
                                         start=False, stop=True), 1)

        @block.gpsimd
        def _(g):
            for oc in range(OCN):
                for tc in range(TC):
                    i = oc * 8 + tc
                    g.wait_ge(S["resv"].h, bRESV1 + i + 1)
                    dmaw(g, S["out"], xout[tc * 128:(tc + 1) * 128, oc * OW:(oc + 1) * OW],
                         xo_sb[:, i % 2, :])
            g.wait_ge(S["out"].h, S["out"].n)
            if dbg:
                g.wait_ge(S["aux"].h, S["aux"].n)

        @block.sync
        def _(sy):
            sy.wait_ge(S["out"].h, S["out"].n)

    ctx.close()
    return nc


def _host_layout_l1(inputs, bslice, x1=None):
    A = inputs["L1_ffn_A"]
    AT = np.ascontiguousarray(A.reshape(128, 4096).T)
    Bw = inputs["L1_ffn_Bw"]
    BT = np.ascontiguousarray(Bw.transpose(0, 2, 1).reshape(128, 4096))
    Rw = inputs["L1_ffn_route_w"]
    RT = np.ascontiguousarray(Rw.T.reshape(32, 128, 8).transpose(1, 0, 2).reshape(128, 256))
    d = {
        "wq": np.ascontiguousarray(inputs["L1_attn_wq"]),
        "wk": np.ascontiguousarray(inputs["L1_attn_wk"]),
        "wv": np.ascontiguousarray(inputs["L1_attn_wv"]),
        "wo": np.ascontiguousarray(inputs["L1_attn_wo"]),
        "AT1": AT, "BT1": BT, "RT": RT,
    }
    if x1 is not None:
        d["x"] = np.ascontiguousarray(x1.reshape(2, 4, 1024, 4096)[bslice[0], bslice[1]])
    return d


def build_fused():
    nc = bass.Bass("TRN2", target_bir_lowering=False, debug=False, detect_race_conditions=False)
    octx = ExitStack()
    S = _mk_sems(nc, octx)
    x_in = _dram(nc, "x", [TOK, D], F32, "ExternalInput")
    x1 = _dram(nc, "x1", [TOK, D], F32, "Internal")
    xout = _dram(nc, "xout", [TOK, D], F32, "ExternalOutput")
    prep = {
        "wq": _dram(nc, "wq", [D, D], F32, "ExternalInput"),
        "wk": _dram(nc, "wk", [D, 1024], F32, "ExternalInput"),
        "wv": _dram(nc, "wv", [D, 1024], F32, "ExternalInput"),
        "wo": _dram(nc, "wo", [1024, D], F32, "ExternalInput"),
        "c_wqe": _dram(nc, "c_wqe", [D, 1024], BF16),
        "c_wk": _dram(nc, "c_wk", [D, 1024], BF16),
        "c_wv": _dram(nc, "c_wv", [D, 1024], BF16),
        "c_wo": _dram(nc, "c_wo", [1024, D], BF16),
        "c_wo0": _dram(nc, "c_wo0", [512, D], BF16),
    }
    build_l0(shell=(nc, S), io=(x_in, x1), prep=prep)
    caches = {k: prep[k] for k in ("c_wqe", "c_wk", "c_wv", "c_wo",
                               "lwo_wk", "lwo_wv", "lwo_wq", "lwo_all")}
    build_l1(shell=(nc, S), io=(x1, xout), caches=caches, skip_w=True)
    octx.close()
    return nc


def kernel(**inputs):
    nc = build_fused()
    in_maps = []
    for c in range(8):
        m = _host_layout_l0(inputs, divmod(c, 4))
        m.update(_host_layout_l1(inputs, divmod(c, 4)))
        in_maps.append(m)
    res = run_bass_kernel_spmd(nc, in_maps, core_ids=list(range(8)))
    return np.stack([r["xout"] for r in res.results]).reshape(2, 4096, 4096)


# version 19
# speedup vs baseline: 1.1170x; 1.0073x over previous
import numpy as np
import ml_dtypes
from contextlib import ExitStack
import concourse.bass as bass
import concourse.mybir as mybir
from concourse.bass_utils import run_bass_kernel_spmd

F32 = mybir.dt.float32
BF16 = mybir.dt.bfloat16
AF = mybir.ActivationFunctionType
ALU = mybir.AluOpType

TOK = 1024
D = 4096
DMC = 32
TC = 8
INV_SQRT512 = float(np.float32(1.0 / np.sqrt(512.0)))
INV_SQRT128 = float(np.float32(1.0 / np.sqrt(128.0)))
L0_SHIFT = -40.0


class Sem:
    def __init__(self, nc, ctx, name):
        self.h = ctx.enter_context(nc.semaphore(name))
        self.n = 0

    def inc(self, inst, k):
        inst.then_inc(self.h, k)
        self.n += k
        return inst


def _dram(nc, name, shape, dtype, kind="Internal"):
    return nc.dram_tensor(name, list(shape), dtype, kind=kind).ap()


ENGS = ["sync", "gpsimd", "vector", "scalar", "tensor"]


ENGS_NOG = ["sync", "vector", "scalar", "tensor"]


def barrier(block, S, waits=(), engines=None):
    engines = ENGS_NOG if engines is None else engines
    b = S["bar"]
    target = b.n + len(engines)

    for ename in engines:
        def mk(e):
            b.inc(e.nop(), 1)
            e.wait_ge(b.h, target)
            for (sem, tgt) in waits:
                e.wait_ge(sem.h, tgt)
        getattr(block, ename)(mk)



def dmaw(eng, sem, out, in_):
    eng.wait_ge(sem.h, sem.n)
    return sem.inc(eng.dma_start(out=out, in_=in_), 16)

SEM_NAMES = ["lw", "lwc", "lwo", "out2", "w2", "lbr", "const", "bar", "x", "v1", "sc", "xn", "tp", "tpev",
             "vch", "w", "wg", "wc", "mm", "ev", "kv", "cc", "kld", "sS", "pexp", "pvm",
             "ep", "att", "atev", "r1", "r2", "xr", "po", "resv", "out", "aux"]


def _mk_sems(nc, ctx):
    return {k: Sem(nc, ctx, "s_" + k) for k in SEM_NAMES}


def _alloc_common(nc, ctx, S=None, sfx=""):
    sb = lambda n, shp, dt: ctx.enter_context(nc.sbuf_tensor(n + sfx, shp, dt))
    if S is None:
        S = _mk_sems(nc, ctx)
    return sb, S


def _emit_ln(nc, block, S, x_ap, x_sb, xn_tm, xn_T, ln, ps_tp, consts, cN=1):
    bx, bxn, bv1, bsc, btp, btpev = (S["x"].n, S["xn"].n, S["v1"].n,
                                     S["sc"].n, S["tp"].n, S["tpev"].n)

    @block.sync
    def _(sy):
        for tc in range(TC):
            if tc >= 2:
                sy.wait_ge(S["xn"].h, bxn + tc - 1)
            dmaw(sy, S["x"], x_sb[:, tc % 2, 0:D], x_ap[tc * 128:(tc + 1) * 128, :])

    @block.vector
    def _(v):
        vch = S["vch"]

        def edge(inst):
            vch.inc(inst, 1)
            v.wait_ge(vch.h, vch.n)
        v.wait_ge(S["const"].h, cN)
        for tc in range(TC):
            v.wait_ge(S["x"].h, bx + (tc + 1) * 16)
            for i in range(8):
                last = v.bn_stats(ln["stats"][:, i, :], x_sb[:, tc % 2, i * 512:(i + 1) * 512])
            edge(last)
            edge(v.bn_aggr(ln["mv"][:, :], ln["stats"][:, :, :]))
            v.tensor_tensor(ln["vtmp"][:, :], ln["mv"][:, 1:2], consts["eps"][:, :], ALU.add)
            S["v1"].inc(v.tensor_scalar_mul(ln["negmean"][:, :], ln["mv"][:, 0:1], -1.0), 1)
            v.wait_ge(S["sc"].h, bsc + tc + 1)
            edge(v.reciprocal(ln["rstd"][:, :], ln["sd"][:, :]))
            if tc >= 2:
                v.wait_ge(S["tp"].h, btp + (tc - 1) * 4)
            S["xn"].inc(v.tensor_scalar(xn_tm[:, tc % 2, 0:D], x_sb[:, tc % 2, 0:D],
                                        ln["negmean"][:, :], ln["rstd"][:, :],
                                        ALU.add, ALU.mult), 1)

    @block.tensor
    def _(t):
        t.wait_ge(S["const"].h, cN)
        for tc in range(TC):
            t.wait_ge(S["xn"].h, bxn + tc + 1)
            for grp in range(4):
                gi = tc * 4 + grp
                if gi >= 2:
                    t.wait_ge(S["tpev"].h, btpev + gi - 1)
                last = None
                for i in range(8):
                    dmc = grp * 8 + i
                    last = t.transpose(ps_tp[:, gi % 2, i * 128:(i + 1) * 128],
                                       xn_tm[:, tc % 2, dmc * 128:(dmc + 1) * 128],
                                       consts["ident_bf"][:, :])
                S["tp"].inc(last, 1)

    @block.scalar
    def _(s):
        def do_sqrt(tc):
            s.wait_ge(S["v1"].h, bv1 + tc + 1)
            S["sc"].inc(s.sqrt(ln["sd"][:, :], ln["vtmp"][:, :]), 1)
        do_sqrt(0)
        do_sqrt(1)
        for tc in range(TC):
            for grp in range(4):
                gi = tc * 4 + grp
                s.wait_ge(S["tp"].h, btp + gi + 1)
                dst = xn_T[:, grp * 8:(grp + 1) * 8, tc * 128:(tc + 1) * 128]
                S["tpev"].inc(
                    s.copy(dst, ps_tp[:, gi % 2, :].rearrange("p (c n) -> p c n", c=8)), 1)
            if tc + 2 < TC:
                do_sqrt(tc + 2)


def build_l0(dbg=False, shell=None, io=None, prep=None):
    if shell is None:
        nc = bass.Bass("TRN2", target_bir_lowering=False, debug=False, detect_race_conditions=False)
        S0 = None
    else:
        nc, S0 = shell
    if io is None:
        x_ap = _dram(nc, "x", [TOK, D], F32, "ExternalInput")
        xout = _dram(nc, "xout", [TOK, D], F32, "ExternalOutput")
    else:
        x_ap, xout = io
    wqkT_q = _dram(nc, "wqkT_q", [D, 512], F32, "ExternalInput")
    wqkT_k = _dram(nc, "wqkT_k", [D, 512], F32, "ExternalInput")
    wvT = _dram(nc, "wvT", [D, 512], F32, "ExternalInput")
    woT = _dram(nc, "woT", [512, D], F32, "ExternalInput")
    bq_h = _dram(nc, "bq_h", [4, 128], F32, "ExternalInput")
    bk_h = _dram(nc, "bk_h", [4, 128], F32, "ExternalInput")
    bv_ap = _dram(nc, "bv", [512], F32, "ExternalInput")
    bo_ap = _dram(nc, "bo", [D], F32, "ExternalInput")
    AT_ap = _dram(nc, "AT0", [D, 128], F32, "ExternalInput")
    BT_ap = _dram(nc, "BT0", [128, D], F32, "ExternalInput")
    WrT_ap = _dram(nc, "WrT", [128, 256], F32, "ExternalInput")
    br_ap = _dram(nc, "br", [8], F32, "ExternalInput")

    kt_own_p = [_dram(nc, "kt_own0_%d" % j, [128, TOK], BF16) for j in range(4)]
    kt_all_p = [_dram(nc, "kt_all0_%d" % j, [4 * 128, TOK], BF16) for j in range(4)]
    v_own_p = [_dram(nc, "v_own0_%d" % j, [TOK, 130], BF16) for j in range(4)]
    v_all_p = [_dram(nc, "v_all0_%d" % j, [4 * TOK, 130], BF16) for j in range(4)]

    if dbg:
        d_xnT = _dram(nc, "d_xnT", [128, DMC, TOK], BF16, "ExternalOutput")
        d_QT = _dram(nc, "d_QT", [128, 4, TOK], BF16, "ExternalOutput")
        d_KT = _dram(nc, "d_KT", [128, 4, TOK], BF16, "ExternalOutput")
        d_V = _dram(nc, "d_V", [128, TC, 520], BF16, "ExternalOutput")
        d_logits = _dram(nc, "d_logits", [128, TC, 8], F32, "ExternalOutput")
        d_lows = _dram(nc, "d_lows", [128, TOK], BF16, "ExternalOutput")
        d_attnT = _dram(nc, "d_attnT", [128, 4, TOK], BF16, "ExternalOutput")
        d_xsb = _dram(nc, "d_xsb", [128, D], F32, "ExternalOutput")
        d_mv = _dram(nc, "d_mv", [128, 2], F32, "ExternalOutput")
        d_rstd = _dram(nc, "d_rstd", [128, 1], F32, "ExternalOutput")
        d_xnl = _dram(nc, "d_xnl", [128, D], BF16, "ExternalOutput")
        d_xnT_early = _dram(nc, "d_xnT_early", [128, DMC, TOK], BF16, "ExternalOutput")

    ctx = ExitStack()
    sb, S = _alloc_common(nc, ctx, S0, sfx="_a")
    ps = lambda n, shp, dt: ctx.enter_context(nc.psum_tensor(n + "_a", shp, dt))

    x_sb = sb("x_sb0", [128, 2, 4224], F32)
    xn_tm = sb("xn_tm", [128, 2, D], BF16)
    wfstage = xn_tm.ap().rearrange("p b n -> p (b n)").bitcast(F32)
    xn_T = sb("xn_T", [128, DMC, TOK], BF16)
    ln = {"stats": sb("ln_stats", [128, 8, 6], F32), "mv": sb("ln_mv", [128, 2], F32),
          "vtmp": sb("ln_vtmp", [128, 1], F32), "negmean": sb("ln_negmean", [128, 1], F32),
          "rstd": sb("ln_rstd", [128, 1], F32), "sd": sb("ln_sd", [128, 1], F32)}
    consts = {"ident_bf": sb("ident_bf", [128, 128], BF16),
              "eps": sb("eps", [128, 1], F32),
              "ones_bf": sb("ones_bf", [1, 128], BF16),
              "ones_f32": sb("ones_f32", [1, 128], F32),
              "shift": sb("shift", [128, 1], F32),
              "maskAB": sb("maskAB", [1, 2, 128], BF16)}
    wf32 = sb("wf32", [128, 2, 1024], F32)
    wbf = sb("wbf", [128, 2, 1024], BF16)
    QT = sb("QT", [128, 4, TOK], BF16)
    KT = sb("KT", [128, 4, TOK], BF16)
    v_full = sb("v_full", [128, TC, 520], BF16)
    AT_sb = sb("AT_sb", [128, DMC, 128], BF16)
    BT_sb = sb("BT_sb", [128, D], BF16)
    WrT_sb = sb("WrT_sb", [128, DMC, 8], BF16)
    low_T = sb("low_T", [128, TOK], F32)
    low_s = sb("low_s", [128, TOK], BF16)
    logits = sb("logits", [128, TC, 8], F32)
    sroute = sb("sroute", [128, TC, 8], F32)
    sroute_bf = sb("sroute_bf", [128, TC, 8], BF16)
    sT = sb("sT", [8, TOK], BF16)
    E_exp = sb("E_exp", [8, 128], BF16)
    b_pair_q = sb("b_pair_q", [128, 4], F32)
    b_pair_k = sb("b_pair_k", [128, 4], F32)
    bvrow = sb("bvrow", [1, 512], F32)
    bias_bcast = sb("bias_bcast", [128, 512], F32)
    borow_bf = sb("borow_bf", [1, D], BF16)
    brrow = sb("brrow", [1, 8], F32)
    br_bcast = sb("br_bcast", [128, 8], F32)
    rmax = sb("rmax", [128, 1], F32)
    rnegmax = sb("rnegmax", [128, 1], F32)
    rsum = sb("rsum", [128, 1], F32)
    rrecip = sb("rrecip", [128, 1], F32)
    KVar = x_sb.ap().rearrange("p b n -> p (b n)").bitcast(BF16)
    K_pair = KVar[:, 0:8192].rearrange("p (b r t) -> p b r t", b=2, r=4)
    V_pair = KVar[:, 8192:16512].rearrange("p (b k c) -> p b k c", b=2, k=32)
    P_sb = sb("P_sb", [128, 4, 512], BF16)
    attn_T = sb("attn_T", [128, 4, TOK], BF16)
    lbf_pair = sb("lbf_pair", [1, 4, 512], BF16)
    rcpD = sb("rcpD", [128, 512], F32)
    if prep is not None:
        _pxt = xn_tm.ap().rearrange("p b n -> p (b n)").bitcast(F32).rearrange(
            "p (b n) -> p b n", b=2)
        pf32s = [_pxt[:, 0, :], _pxt[:, 1, :],
                 KT.ap().rearrange("p b n -> p (b n)").bitcast(F32),
                 v_full.ap().rearrange("p b n -> p (b n)").bitcast(F32)[:, 0:2048]]
        pbf = wf32.ap().rearrange("p b n -> p (b n)").bitcast(BF16).rearrange(
            "p (b n) -> p b n", b=2)
        pscr = wbf.ap().rearrange("p b n -> p (b n)").bitcast(F32)[:, 0:512]
    wof32 = x_sb[:, 0, 0:2048].rearrange("p (c n) -> p c n", c=4)
    wobf = x_sb[:, 0, 2048:4096].bitcast(BF16).rearrange("p (b c n) -> p b c n", b=2, c=4)
    x_res = x_sb[:, 1, 0:1024].rearrange("p (b n) -> p b n", b=2)
    xo_sb = x_sb[:, 1, 1024:2048].rearrange("p (b n) -> p b n", b=2)

    PS = ps("PS0", [128, 8, 512], F32)
    ps_tp = PS[:, 0:2, :].rearrange("p b n -> p (b n)").bitcast(BF16).rearrange(
        "p (b n) -> p b n", b=2)
    pvflat = PS[:, 4:6, :].rearrange("p b n -> p (b n)")
    ps_sb8 = PS[:, 7, :].bitcast(BF16)


    with nc.Block() as block:
        @block.gpsimd
        def _(g):
            aux = S["aux"]

            def step(inst):
                aux.inc(inst, 1)
                g.wait_ge(aux.h, aux.n)
            step(g.memset(consts["ident_bf"][:, :], 0.0))
            step(g.affine_select(out=consts["ident_bf"][:, :], in_=consts["ident_bf"][:, :],
                                 compare_op=ALU.not_equal, fill=1.0, base=0,
                                 pattern=[[-1, 128]], channel_multiplier=1))
            step(g.memset(consts["eps"][:, :], 1e-5))
            step(g.memset(consts["ones_bf"][:, :], 1.0))
            step(g.memset(consts["ones_f32"][:, :], 1.0))
            step(g.memset(consts["shift"][:, :], L0_SHIFT))
            step(g.memset(consts["maskAB"][:, :, :], 0.0))
            step(g.memset(consts["maskAB"][:, 0, 0:64], 1.0))
            step(g.memset(consts["maskAB"][:, 1, 64:128], 1.0))
            step(g.memset(E_exp[:, :], 1.0))
            step(g.affine_select(out=E_exp[:, :], in_=E_exp[:, :],
                                 compare_op=ALU.is_ge, fill=0.0, base=0,
                                 pattern=[[1, 128]], channel_multiplier=-16))
            step(g.affine_select(out=E_exp[:, :], in_=E_exp[:, :],
                                 compare_op=ALU.is_ge, fill=0.0, base=15,
                                 pattern=[[-1, 128]], channel_multiplier=16))
            step(g.memset(v_full[:, :, :].rearrange("p t (h c) -> p t h c", c=65)[:, :, :, 64:65], 1.0))
            S["const"].inc(g.nop(), 1)

        _emit_ln(nc, block, S, x_ap, x_sb, xn_tm, xn_T, ln, ps_tp, consts, cN=S["const"].n)

        @block.gpsimd
        def _(g):
            for j in range(4):
                dmaw(g, S["wg"], b_pair_q[:, j:j + 1], bq_h[j:j + 1, :].rearrange("a p -> p a"))
                dmaw(g, S["wg"], b_pair_k[:, j:j + 1], bk_h[j:j + 1, :].rearrange("a p -> p a"))
            dmaw(g, S["wg"], bvrow[:, :], bv_ap.rearrange("(a n) -> a n", a=1))
            dmaw(g, S["wg"], brrow[:, :], br_ap.rearrange("(a n) -> a n", a=1))

        prep_casts = []
        PRE = 32
        if prep is not None:
            blw, blwc, blwo = S["lw"].n, S["lwc"].n, S["lwo"].n
            prep["wo0"] = woT
            plan = []
            for dmc in range(DMC):
                plan.append(("wk", dmc * 128, 0, 1024, "cast"))
            for dmc in range(DMC):
                plan.append(("wv", dmc * 128, 0, 1024, "cast"))
            for c4 in range(4):
                for ch in range(2):
                    plan.append(("wo0", c4 * 128, ch * 2048, 2048, "cast"))
            prep["lwo_wo0"] = blwo + 72 * 16
            prep["lwo_wk"] = blwo + 32 * 16
            prep["lwo_wv"] = blwo + 64 * 16
            prep["lwo_wq"] = blwo + 136 * 16
            prep["lwo_all"] = blwo + 152 * 16
            for dmc in range(DMC):
                for ch in range(2):
                    plan.append(("wq", dmc * 128, ch * 2048, 2048, "presum"))
            for c8 in range(8):
                for ch in range(2):
                    plan.append(("wo", c8 * 128, ch * 2048, 2048, "cast"))

            pre_f32 = [attn_T.ap().rearrange("p b n -> p (b n)").bitcast(F32)[:, 0:1024],
                       attn_T.ap().rearrange("p b n -> p (b n)").bitcast(F32)[:, 1024:2048],
                       P_sb.ap().rearrange("p b n -> p (b n)").bitcast(F32)]
            pre_bf = rcpD.ap().bitcast(BF16)

            def _dst(w, r0, c0, ncols, kind):
                if kind == "presum":
                    return prep["c_wqe"][r0:r0 + 128, (c0 // 2048) * 512:(c0 // 2048) * 512 + 512], 512
                return prep["c_" + w][r0:r0 + 128, c0:c0 + ncols], ncols

            @block.gpsimd
            def _(g):
                for k in range(PRE):
                    w, r0, c0, ncols, kind = plan[k]
                    if k >= 3:
                        g.wait_ge(S["lwc"].h, blwc + k - 2)
                    dmaw(g, S["lw"], pre_f32[k % 3][:, 0:ncols], prep[w][r0:r0 + 128, c0:c0 + ncols])
                    if k >= 1:
                        dst, dn = _dst(*plan[k - 1])
                        g.wait_ge(S["lwc"].h, blwc + k)
                        dmaw(g, S["lwo"], dst, pre_bf[:, 0:dn])
                dst, dn = _dst(*plan[PRE - 1])
                g.wait_ge(S["lwc"].h, blwc + PRE)
                dmaw(g, S["lwo"], dst, pre_bf[:, 0:dn])

            def mk_cast(k, w, r0, c0, ncols, kind):
                def go(v):
                    vch = S["vch"]
                    v.wait_ge(S["lw"].h, blw + (k + 1) * 16)
                    if k < PRE:
                        if k >= 1:
                            v.wait_ge(S["lwo"].h, blwo + k * 16)
                        S["lwc"].inc(v.tensor_copy(pre_bf[:, 0:ncols],
                                                   pre_f32[k % 3][:, 0:ncols]), 1)
                        return
                    if k >= 2:
                        v.wait_ge(S["lwo"].h, blwo + (k - 1) * 16)
                    if kind == "presum":
                        wv4 = pf32s[k % 4].rearrange("p (kv g c) -> p kv g c", kv=4, g=4)
                        scr = pscr.rearrange("p (kv c) -> p kv c", kv=4)
                        vch.inc(v.tensor_tensor(scr, wv4[:, :, 0, :], wv4[:, :, 1, :], ALU.add), 1)
                        v.wait_ge(vch.h, vch.n)
                        vch.inc(v.tensor_tensor(scr, scr, wv4[:, :, 2, :], ALU.add), 1)
                        v.wait_ge(vch.h, vch.n)
                        S["lwc"].inc(v.tensor_tensor(
                            pbf[:, k % 2, 0:512].rearrange("p (kv c) -> p kv c", kv=4),
                            scr, wv4[:, :, 3, :], ALU.add), 1)
                    else:
                        S["lwc"].inc(v.tensor_copy(pbf[:, k % 2, 0:ncols], pf32s[k % 4][:, 0:ncols]), 1)
                return go
            prep_casts = [mk_cast(k, *cfg) for k, cfg in enumerate(plan)]

        _ci = [0]

        def drain_casts_on(v, n):
            while _ci[0] < len(prep_casts) and n > 0:
                prep_casts[_ci[0]](v)
                _ci[0] += 1
                n -= 1

        barrier(block, S, waits=[(S["tpev"], S["tpev"].n), (S["const"], S["const"].n)])

        if dbg:
            @block.gpsimd
            def _(g):
                dmaw(g, S["kv"], d_xnT_early, xn_T[:, :, :])
                g.wait_ge(S["kv"].h, S["kv"].n)

        bWC2 = S["wc"].n
        bMM2 = S["mm"].n

        @block.sync
        def _(sy):
            dmaw(sy, S["w"], wfstage.rearrange("p (c n) -> p c n", c=DMC), AT_ap.rearrange("(c p) n -> p c n", p=128))

        @block.vector
        def _(v):
            v.wait_ge(S["w"].h, S["w"].n)
            S["wc"].inc(v.tensor_copy(AT_sb[:, :, :],
                                      wfstage.rearrange("p (c n) -> p c n", c=DMC)), 1)

        @block.sync
        def _(sy):
            sy.wait_ge(S["wc"].h, bWC2 + 1)
            dmaw(sy, S["w"], wfstage, BT_ap)

        @block.vector
        def _(v):
            v.wait_ge(S["w"].h, S["w"].n)
            S["wc"].inc(v.tensor_copy(BT_sb[:, :], wfstage), 1)

        @block.sync
        def _(sy):
            sy.wait_ge(S["wc"].h, bWC2 + 2)
            dmaw(sy, S["w"], wfstage[:, 0:256], WrT_ap[:, :])
            sy.wait_ge(S["wc"].h, bWC2 + 3)
            dmaw(sy, S["w"], wfstage[0:1, 0:D], bo_ap.rearrange("(a n) -> a n", a=1))

        @block.vector
        def _(v):
            v.wait_ge(S["w"].h, S["w"].n - 16)
            S["wc"].inc(v.tensor_copy(WrT_sb[:, :, :],
                                      wfstage[:, 0:256].rearrange("p (c e) -> p c e", c=DMC)), 1)
            v.wait_ge(S["w"].h, S["w"].n)
            S["wc"].inc(v.tensor_copy(borow_bf[:, :], wfstage[0:1, 0:D]), 1)

        @block.tensor
        def _(t):
            t.wait_ge(S["w"].h, S["w"].n)
            t.wait_ge(S["wg"].h, S["wg"].n)
            t.matmul(PS[:, 3, 0:512], consts["ones_f32"][0:1, :], bvrow[0:1, :],
                     start=True, stop=True)
            S["mm"].inc(t.matmul(PS[:, 2, 0:8], consts["ones_f32"][0:1, :], brrow[0:1, :],
                                 start=True, stop=True), 1)

        @block.scalar
        def _(s):
            s.wait_ge(S["mm"].h, bMM2 + 1)
            s.copy(bias_bcast[:, :], PS[:, 3, 0:512])
            S["ev"].inc(s.copy(br_bcast[:, :], PS[:, 2, 0:8]), 1)

        barrier(block, S, waits=[(S["wc"], S["wc"].n), (S["ev"], S["ev"].n), (S["wg"], S["wg"].n)])

        nMM = S["mm"].n
        nEV = S["ev"].n
        bWC1 = S["wc"].n
        wf32s = wf32.ap().rearrange("p b n -> p (b n)").rearrange("p (b n) -> p b n", b=4)
        wbfs = wbf.ap().rearrange("p b n -> p (b n)").rearrange("p (b n) -> p b n", b=4)
        wsem = {0: S["w"], 1: S["w2"]}
        bW1 = {0: S["w"].n, 1: S["w2"].n}

        def c1_load(eng, i):
            og, dmc = divmod(i, DMC)
            if i >= 4:
                eng.wait_ge(S["wc"].h, bWC1 + i - 3)
            q = wsem[i % 2]
            dmaw(eng, q, wf32s[:, i % 4, 0:256],
                 wqkT_q[dmc * 128:(dmc + 1) * 128, og * 256:(og + 1) * 256])
            dmaw(eng, q, wf32s[:, i % 4, 256:512],
                 wqkT_k[dmc * 128:(dmc + 1) * 128, og * 256:(og + 1) * 256])

        @block.sync
        def _(sy):
            for i in range(0, 2 * DMC, 2):
                c1_load(sy, i)

        @block.vector
        def _(v):
            nld = {0: 0, 1: 0}
            for i in range(2 * DMC):
                nld[i % 2] += 32
                v.wait_ge(wsem[i % 2].h, bW1[i % 2] + nld[i % 2])
                if i >= 4:
                    v.wait_ge(S["mm"].h, nMM + i - 3)
                S["wc"].inc(v.tensor_copy(wbfs[:, i % 4, :], wf32s[:, i % 4, :]), 1)
                if prep is not None and i % 2 == 1:
                    drain_casts_on(v, 1)

        @block.tensor
        def _(t):
            for i in range(2 * DMC):
                og, dmc = divmod(i, DMC)
                t.wait_ge(S["wc"].h, bWC1 + i + 1)
                if dmc == 0 and og == 1:
                    t.wait_ge(S["ev"].h, nEV + 8)
                last = None
                for j2 in range(2):
                    for half in range(2):
                        t.matmul(PS[:, j2 * 2 + half, :], wbfs[:, i % 4, j2 * 128:(j2 + 1) * 128],
                                 xn_T[:, dmc, half * 512:(half + 1) * 512],
                                 start=(dmc == 0), stop=(dmc == DMC - 1))
                        last = t.matmul(PS[:, 4 + j2 * 2 + half, :],
                                        wbfs[:, i % 4, 256 + j2 * 128:256 + (j2 + 1) * 128],
                                        xn_T[:, dmc, half * 512:(half + 1) * 512],
                                        start=(dmc == 0), stop=(dmc == DMC - 1))
                S["mm"].inc(last, 1)

        @block.scalar
        def _(s):
            for og in range(2):
                for i in range(og * DMC + 1, (og + 1) * DMC, 2):
                    c1_load(s, i)
                s.wait_ge(S["mm"].h, nMM + (og + 1) * DMC)
                last = None
                for j2 in range(2):
                    j = og * 2 + j2
                    for half in range(2):
                        s.activation(out=QT[:, j, half * 512:(half + 1) * 512],
                                     in_=PS[:, j2 * 2 + half, :],
                                     func=AF.Identity, bias=b_pair_q[:, j:j + 1], scale=1.0)
                        last = s.activation(out=KT[:, j, half * 512:(half + 1) * 512],
                                            in_=PS[:, 4 + j2 * 2 + half, :],
                                            func=AF.Identity, bias=b_pair_k[:, j:j + 1], scale=1.0)
                S["ev"].inc(last, 8)

        barrier(block, S, waits=[(S["ev"], S["ev"].n)])

        nEVk0 = S["ev"].n

        ncc0 = S["cc"].n
        CC_K = [ncc0 + 1, ncc0 + 2, ncc0 + 4, ncc0 + 5]
        CC_V = [ncc0 + 3, ncc0 + 6, ncc0 + 7, ncc0 + 8]

        @block.gpsimd
        def _(g):
            for j in range(4):
                g.wait_ge(S["ev"].h, nEVk0 - 16 + (8 if j < 2 else 16))
                dmaw(g, S["kv"], kt_own_p[j], KT[:, j, :])
            g.wait_ge(S["kv"].h, S["kv"].n)
            for j in range(2):
                S["cc"].inc(g.collective_compute(
                    "AllGather", ALU.bypass,
                    replica_groups=[[0, 1, 2, 3], [4, 5, 6, 7]],
                    ins=[kt_own_p[j].opt()], outs=[kt_all_p[j].opt()]), 1)

        nWC2 = S["wc"].n
        nMM2 = S["mm"].n
        bW3 = {0: S["w"].n, 1: S["w2"].n}

        def c3_load(eng, dmc):
            if dmc >= 4:
                eng.wait_ge(S["wc"].h, nWC2 + dmc - 3)
            dmaw(eng, wsem[dmc % 2], wf32s[:, dmc % 4, 0:512],
                 wvT[dmc * 128:(dmc + 1) * 128, :])

        @block.sync
        def _(sy):
            for dmc in range(0, DMC, 2):
                c3_load(sy, dmc)

        @block.scalar
        def _(s):
            for dmc in range(1, DMC, 2):
                c3_load(s, dmc)

        @block.vector
        def _(v):
            nld = {0: 0, 1: 0}
            for dmc in range(DMC):
                nld[dmc % 2] += 16
                v.wait_ge(wsem[dmc % 2].h, bW3[dmc % 2] + nld[dmc % 2])
                if dmc >= 4:
                    v.wait_ge(S["mm"].h, nMM2 + dmc - 3)
                S["wc"].inc(v.tensor_copy(wbfs[:, dmc % 4, :], wf32s[:, dmc % 4, :]), 1)

        @block.tensor
        def _(t):
            for dmc in range(DMC):
                t.wait_ge(S["wc"].h, nWC2 + dmc + 1)
                last = None
                for tc in range(TC):
                    last = t.matmul(PS[:, tc, :], xn_T[:, dmc, tc * 128:(tc + 1) * 128],
                                    wbfs[:, dmc % 4, :],
                                    start=(dmc == 0), stop=(dmc == DMC - 1))
                S["mm"].inc(last, 1)

        @block.vector
        def _(v):
            v.wait_ge(S["mm"].h, nMM2 + DMC)
            last = None
            for tc in range(TC):
                dst = v_full[:, tc, :].rearrange("p (h c) -> p h c", c=65)[:, :, 0:64]
                last = v.scalar_tensor_tensor(
                    out=dst, in0=PS[:, tc, :].rearrange("p (h e) -> p h e", e=64),
                    scalar=1.0, in1=bias_bcast[:, :].rearrange("p (h e) -> p h e", e=64),
                    op0=ALU.mult, op1=ALU.add)
            S["ev"].inc(last, 1)

        @block.gpsimd
        def _(g):
            g.wait_ge(S["ev"].h, S["ev"].n)
            for j in range(4):
                dmaw(g, S["kv"], v_own_p[j].rearrange("(c p) n -> p c n", p=128),
                     v_full[:, :, 130 * j:130 * (j + 1)])
            g.wait_ge(S["kv"].h, S["kv"].n)

            def ag(src, dst):
                S["cc"].inc(g.collective_compute(
                    "AllGather", ALU.bypass,
                    replica_groups=[[0, 1, 2, 3], [4, 5, 6, 7]],
                    ins=[src.opt()], outs=[dst.opt()]), 1)
            ag(v_own_p[0], v_all_p[0])
            ag(kt_own_p[2], kt_all_p[2])
            ag(kt_own_p[3], kt_all_p[3])
            for j in range(1, 4):
                ag(v_own_p[j], v_all_p[j])

        barrier(block, S, waits=[(S["ev"], S["ev"].n)])

        if prep is not None:
            @block.gpsimd
            def _(g):
                for k in range(PRE, len(plan)):
                    w, r0, c0, ncols, kind = plan[k]
                    if k >= PRE + 4:
                        g.wait_ge(S["lwc"].h, blwc + k - 3)
                    dmaw(g, S["lw"], pf32s[k % 4][:, 0:ncols], prep[w][r0:r0 + 128, c0:c0 + ncols])
                    if k >= PRE + 1:
                        pw = k - 1
                        dst, dn = _dst(*plan[pw])
                        g.wait_ge(S["lwc"].h, blwc + pw + 1)
                        dmaw(g, S["lwo"], dst, pbf[:, pw % 2, 0:dn])
                pw = len(plan) - 1
                dst, dn = _dst(*plan[pw])
                g.wait_ge(S["lwc"].h, blwc + pw + 1)
                dmaw(g, S["lwo"], dst, pbf[:, pw % 2, 0:dn])

        nMM3 = S["mm"].n
        nR1c = S["r1"].n

        @block.tensor
        def _(t):
            last = None
            for dmc in range(DMC):
                for tc in range(TC):
                    last = t.matmul(PS[:, tc, 0:8], xn_T[:, dmc, tc * 128:(tc + 1) * 128],
                                    WrT_sb[:, dmc, :],
                                    start=(dmc == 0), stop=(dmc == DMC - 1))
            S["mm"].inc(last, 1)

        @block.vector
        def _(v):
            v.wait_ge(S["mm"].h, nMM3 + 1)
            last = None
            for tc in range(TC):
                last = v.tensor_tensor(logits[:, tc, :], PS[:, tc, 0:8],
                                       br_bcast[:, :], ALU.add)
            S["r1"].inc(last, 1)

        @block.tensor
        def _(t):
            t.wait_ge(S["r1"].h, nR1c + 1)
            for half in range(2):
                last = None
                for dmc in range(DMC):
                    last = t.matmul(PS[:, 1 + half, :], AT_sb[:, dmc, :],
                                    xn_T[:, dmc, half * 512:(half + 1) * 512],
                                    start=(dmc == 0), stop=(dmc == DMC - 1))
                S["mm"].inc(last, 1)

        @block.scalar
        def _(s):
            s.wait_ge(S["mm"].h, nMM3 + 3)
            for half in range(2):
                S["ev"].inc(s.copy(low_T[:, half * 512:(half + 1) * 512], PS[:, 1 + half, :]), 1)

        @block.vector
        def _(v):
            v.wait_ge(S["r1"].h, S["r1"].n)
            vch = S["vch"]
            bR2 = S["r2"].n
            for tc in range(TC):
                vch.inc(v.reduce_max(rmax[:, :], logits[:, tc, :], axis=mybir.AxisListType.X), 1)
                v.wait_ge(vch.h, vch.n)
                S["r1"].inc(v.tensor_scalar_mul(rnegmax[:, :], rmax[:, :], -1.0), 1)
                v.wait_ge(S["r2"].h, bR2 + tc + 1)
                vch.inc(v.reciprocal(rrecip[:, :], rsum[:, :]), 1)
                v.wait_ge(vch.h, vch.n)
                S["r1"].inc(v.tensor_scalar(sroute_bf[:, tc, :], sroute[:, tc, :],
                                            rrecip[:, :], 16.0, ALU.mult, ALU.mult), 1)

        @block.scalar
        def _(s):
            base = S["r1"].n - 2 * TC
            for tc in range(TC):
                s.wait_ge(S["r1"].h, base + 2 * tc + 1)
                S["r2"].inc(s.activation(out=sroute[:, tc, :], in_=logits[:, tc, :],
                                         func=AF.Exp, bias=rnegmax[:, :], scale=1.0,
                                         accum_out=rsum[:, :]), 1)

        @block.tensor
        def _(t):
            t.wait_ge(S["r1"].h, S["r1"].n)
            last = None
            for tc in range(TC):
                last = t.transpose(ps_sb8[0:8, tc * 128:(tc + 1) * 128],
                                   sroute_bf[:, tc, :], consts["ident_bf"][:, :])
            S["att"].inc(last, 1)

        @block.scalar
        def _(s):
            s.wait_ge(S["att"].h, S["att"].n)
            S["ev"].inc(s.copy(sT[0:8, :], ps_sb8[0:8, :]), 1)

        @block.tensor
        def _(t):
            t.wait_ge(S["ev"].h, S["ev"].n)
            for half in range(2):
                S["mm"].inc(t.matmul(PS[:, 4 + half, :], E_exp[0:8, :],
                                     sT[0:8, half * 512:(half + 1) * 512],
                                     start=True, stop=True), 1)

        @block.vector
        def _(v):
            v.wait_ge(S["mm"].h, S["mm"].n)
            v.wait_ge(S["ev"].h, S["ev"].n)
            last = None
            for half in range(2):
                last = v.tensor_tensor(low_s[:, half * 512:(half + 1) * 512],
                                       low_T[:, half * 512:(half + 1) * 512],
                                       PS[:, 4 + half, :], ALU.mult)
            S["r1"].inc(last, 1)

        barrier(block, S, waits=[(S["r1"], S["r1"].n)])

        if dbg:
            @block.sync
            def _(sy):
                dmaw(sy, S["aux"], d_xnT, xn_T[:, :, :])
                dmaw(sy, S["aux"], d_QT, QT[:, :, :])
                dmaw(sy, S["aux"], d_KT, KT[:, :, :])
                dmaw(sy, S["aux"], d_V, v_full[:, :, :])
                dmaw(sy, S["aux"], d_logits, logits[:, :, :])
                dmaw(sy, S["aux"], d_lows, low_s[:, :])
                dmaw(sy, S["aux"], d_xsb, x_sb[:, 1, 0:D])
                dmaw(sy, S["aux"], d_mv, ln["mv"][:, :])
                dmaw(sy, S["aux"], d_rstd, ln["rstd"][:, :])
                dmaw(sy, S["aux"], d_xnl, xn_tm[:, 1, 0:D])

        kt_views = [ka.rearrange("(r p) t -> p r t", p=128) for ka in kt_all_p]
        v_views = [va.rearrange("(kc p) n -> p kc n", p=128) for va in v_all_p]
        cnt = {"kld": S["kld"].n}
        bxrD = S["xr"].n
        bsS, bpexp, bpvm, bep, batev, blbr, batt = (S["sS"].n, S["pexp"].n, S["pvm"].n,
                                                    S["ep"].n, S["atev"].n, S["lbr"].n,
                                                    S["att"].n)
        s_idx = [0]

        @block.sync
        def _(sy):
            for j in range(4):
                if j >= 2:
                    sy.wait_ge(S["pvm"].h, bpvm + (j - 1) * 128)
                sy.wait_ge(S["cc"].h, CC_K[j])
                dmaw(sy, S["kld"], K_pair[:, j % 2, :, :], kt_views[j][:, :, :])
                sy.wait_ge(S["cc"].h, CC_V[j])
                for q in range(4):
                    dmaw(sy, S["xr"], V_pair[:, j % 2, q * 8:(q + 1) * 8, :],
                         v_views[j][:, q * 8:(q + 1) * 8, :])

        @block.tensor
        def _(t):
            def emit_S(j, kc, u, si):
                hh, qh = divmod(u, 2)
                S["sS"].inc(t.matmul(
                    PS[:, si % 4, :],
                    K_pair[hh * 64:(hh + 1) * 64, j % 2, kc // 8, (kc % 8) * 128:(kc % 8 + 1) * 128],
                    QT[hh * 64:(hh + 1) * 64, j, qh * 512:(qh + 1) * 512],
                    start=True, stop=True, tile_position=(hh * 64, 0)), 1)

            def emit_PV(j, kc, u, si):
                hh = u // 2
                S["pvm"].inc(t.matmul(
                    PS[0:65, 4 + u, :],
                    V_pair[:, j % 2, kc, hh * 65:(hh + 1) * 65],
                    P_sb[:, si % 4, :],
                    start=(kc == 0), stop=(kc == 31)), 1)

            for j in range(4):
                t.wait_ge(S["kld"].h, cnt["kld"] + (j + 1) * 16)
                base_si = s_idx[0]
                for kc in range(32):
                    if s_idx[0] >= 4:
                        t.wait_ge(S["pexp"].h, bpexp + s_idx[0])
                    for u in range(4):
                        emit_S(j, kc, u, s_idx[0])
                        s_idx[0] += 1
                    if kc >= 1:
                        if kc == 1:
                            t.wait_ge(S["ep"].h, bep + j)
                            t.wait_ge(S["lbr"].h, blbr + j * 2)
                            t.wait_ge(S["xr"].h, bxrD + (j + 1) * 64)
                        t.wait_ge(S["pexp"].h, bpexp + base_si + 4 * kc)
                        for u in range(4):
                            emit_PV(j, kc - 1, u, base_si + 4 * (kc - 1) + u)
                t.wait_ge(S["pexp"].h, bpexp + base_si + 128)
                for u in range(4):
                    emit_PV(j, 31, u, base_si + 4 * 31 + u)
                for qh in range(2):
                    t.wait_ge(S["ep"].h, bep + j + 1)
                    t.matmul(PS[:, 4 + qh, :], consts["maskAB"][:, 0, :], lbf_pair[:, qh, :],
                             start=True, stop=False)
                    S["atev"].inc(t.matmul(PS[:, 4 + qh, :], consts["maskAB"][:, 1, :],
                                           lbf_pair[:, 2 + qh, :],
                                           start=False, stop=True), 1)

        @block.scalar
        def _(s):
            si = 0
            for j in range(4):
                for kc in range(32):
                    for u in range(4):
                        s.wait_ge(S["sS"].h, bsS + si + 1)
                        if si >= 4:
                            s.wait_ge(S["pvm"].h, bpvm + si - 3)
                        S["pexp"].inc(s.activation(
                            out=P_sb[:, si % 4, :], in_=PS[:, si % 4, :],
                            func=AF.Exp, bias=consts["shift"][:, :], scale=1.0), 1)
                        si += 1

        @block.vector
        def _(v):
            for j in range(4):
                drain_casts_on(v, 16)
                v.wait_ge(S["pvm"].h, bpvm + (j + 1) * 128)
                v.tensor_copy(attn_T[0:64, j, 0:512], PS[0:64, 4, :])
                v.tensor_copy(attn_T[0:64, j, 512:1024], PS[0:64, 5, :])
                v.tensor_copy(attn_T[64:128, j, 0:512], PS[0:64, 6, :])
                v.tensor_copy(attn_T[64:128, j, 512:1024], PS[0:64, 7, :])
                last = None
                for u in range(4):
                    last = v.tensor_copy(lbf_pair[:, u, :], PS[64:65, 4 + u, :])
                S["ep"].inc(last, 1)
                v.wait_ge(S["ep"].h, S["ep"].n)
                for qh in range(2):
                    v.wait_ge(S["atev"].h, batev + j * 2 + qh + 1)
                    S["lbr"].inc(v.reciprocal(rcpD[:, :], PS[:, 4 + qh, :]), 1)
                    v.wait_ge(S["lbr"].h, S["lbr"].n)
                    inst = v.scalar_tensor_tensor(
                        out=attn_T[:, j, qh * 512:(qh + 1) * 512], in0=rcpD[:, :],
                        scalar=INV_SQRT512, in1=attn_T[:, j, qh * 512:(qh + 1) * 512],
                        op0=ALU.mult, op1=ALU.mult)
                    if qh == 1:
                        S["att"].inc(inst, 1)

        barrier(block, S, waits=[(S["att"], S["att"].n)])

        if dbg:
            @block.sync
            def _(sy):
                dmaw(sy, S["aux"], d_attnT, attn_T[:, :, :])

        nW3 = S["w"].n
        nWC3 = S["wc"].n
        nPO = S["po"].n
        bXR = S["xr"].n
        bOUT = S["out2"].n
        bRESV = S["resv"].n

        @block.sync
        def _(sy):
            def load_wo(oc):
                if prep is not None:
                    if oc == 0:
                        sy.wait_ge(S["lwo"].h, prep["lwo_wo0"])
                    if oc >= 2:
                        sy.wait_ge(S["po"].h, nPO + (oc - 1) * 8)
                    dmaw(sy, S["w"], wobf[:, oc % 2, :, :],
                         prep["c_wo0"].rearrange("(c p) o -> p c o", p=128)[:, :, oc * 512:(oc + 1) * 512])
                else:
                    dmaw(sy, S["w"], wof32[:, :, :],
                         woT.rearrange("(c p) o -> p c o", p=128)[:, :, oc * 512:(oc + 1) * 512])
            load_wo(0)
            for oc in range(8):
                for tc in range(TC):
                    i = oc * 8 + tc
                    if i >= 2:
                        sy.wait_ge(S["resv"].h, bRESV + i - 1)
                    dmaw(sy, S["xr"], x_res[:, i % 2, :], x_ap[tc * 128:(tc + 1) * 128, oc * 512:(oc + 1) * 512])
                    if tc == 1 and oc + 1 < 8 and prep is not None:
                        load_wo(oc + 1)
                    if tc == 1 and oc + 1 < 8 and prep is None:
                        sy.wait_ge(S["wc"].h, nWC3 + oc + 1)
                        load_wo(oc + 1)

        @block.vector
        def _(v):
            def cast(oc):
                v.wait_ge(S["w"].h, nW3 + (oc + 1) * 16)
                if oc >= 2:
                    v.wait_ge(S["po"].h, nPO + (oc - 1) * 8)
                S["wc"].inc(v.tensor_copy(wobf[:, oc % 2, :, :], wof32[:, :, :]), 1)
            if prep is None:
                cast(0)
                cast(1)
            for oc in range(8):
                for tc in range(TC):
                    i = oc * 8 + tc
                    v.wait_ge(S["po"].h, nPO + i + 1)
                    v.wait_ge(S["xr"].h, bXR + (i + 1) * 16)
                    if i >= 2:
                        v.wait_ge(S["out2"].h, bOUT + (i - 1) * 16)
                    S["resv"].inc(v.scalar_tensor_tensor(
                        out=xo_sb[:, i % 2, :], in0=PS[:, i % 4, :], scalar=1.0,
                        in1=x_res[:, i % 2, :], op0=ALU.mult, op1=ALU.add), 1)
                    if prep is not None:
                        drain_casts_on(v, 1)
                if oc + 2 < 8 and prep is None:
                    cast(oc + 2)
            if prep is not None:
                drain_casts_on(v, 10 ** 9)

        @block.tensor
        def _(t):
            for oc in range(8):
                if prep is not None:
                    t.wait_ge(S["w"].h, nW3 + (oc + 1) * 16)
                else:
                    t.wait_ge(S["wc"].h, nWC3 + oc + 1)
                for tc in range(TC):
                    i = oc * 8 + tc
                    b = i % 4
                    if i >= 4:
                        t.wait_ge(S["resv"].h, bRESV + i - 3)
                    for jj in range(4):
                        t.matmul(PS[:, b, :], attn_T[:, jj, tc * 128:(tc + 1) * 128],
                                 wobf[:, oc % 2, jj, :], start=(jj == 0), stop=False)
                    t.matmul(PS[:, b, :], low_s[:, tc * 128:(tc + 1) * 128],
                             BT_sb[:, oc * 512:(oc + 1) * 512], start=False, stop=False)
                    S["po"].inc(t.matmul(PS[:, b, :], consts["ones_bf"][0:1, :],
                                         borow_bf[0:1, oc * 512:(oc + 1) * 512],
                                         start=False, stop=True), 1)

        @block.scalar
        def _(sc):
            for oc in range(8):
                for tc in range(TC):
                    i = oc * 8 + tc
                    sc.wait_ge(S["resv"].h, bRESV + i + 1)
                    dmaw(sc, S["out2"], xout[tc * 128:(tc + 1) * 128, oc * 512:(oc + 1) * 512],
                         xo_sb[:, i % 2, :])
            sc.wait_ge(S["out2"].h, S["out2"].n)

        @block.gpsimd
        def _(g):
            g.wait_ge(S["out2"].h, S["out2"].n)
            if dbg:
                g.wait_ge(S["aux"].h, S["aux"].n)

        @block.sync
        def _(sy):
            sy.wait_ge(S["out"].h, S["out"].n)

    ctx.close()
    return nc


def _host_layout_l0(inputs, bslice):
    Wqk = np.ascontiguousarray(inputs["L0_attn_Wqk_w"])
    rows = np.arange(1024)
    h = rows // 128
    qrows = np.empty(512, np.int64)
    krows = np.empty(512, np.int64)
    for j in range(4):
        for hh in range(2):
            hlin = 2 * j + hh
            e = np.arange(64)
            qrows[j * 128 + hh * 64 + e] = hlin * 128 + e * 2 + 0
            krows[j * 128 + hh * 64 + e] = hlin * 128 + e * 2 + 1
    wqkT_q = np.ascontiguousarray(Wqk[qrows, :].T)
    wqkT_k = np.ascontiguousarray(Wqk[krows, :].T)
    bq_h = np.ascontiguousarray(inputs["L0_attn_Wqk_b"][qrows].reshape(4, 128))
    bk_h = np.ascontiguousarray(inputs["L0_attn_Wqk_b"][krows].reshape(4, 128))
    wvT = np.ascontiguousarray(inputs["L0_attn_Wv_w"].T)
    woT = np.ascontiguousarray(inputs["L0_attn_Wo_w"].T)
    A = inputs["L0_ffn_A"]
    AT = np.ascontiguousarray(A.reshape(128, 4096).T)
    B = inputs["L0_ffn_B"]
    BT = np.ascontiguousarray(B.transpose(0, 2, 1).reshape(128, 4096))
    Wr = inputs["L0_ffn_Wr"]
    WrT = np.ascontiguousarray(Wr.T.reshape(32, 128, 8).transpose(1, 0, 2).reshape(128, 256))
    return {
        "x": np.ascontiguousarray(inputs["L0_attn_x"].reshape(2, 4, 1024, 4096)[bslice[0], bslice[1]]),
        "wqkT_q": wqkT_q, "wqkT_k": wqkT_k, "wvT": wvT, "woT": woT,
        "bq_h": bq_h, "bk_h": bk_h,
        "bv": np.ascontiguousarray(inputs["L0_attn_Wv_b"]),
        "bo": np.ascontiguousarray(inputs["L0_attn_Wo_b"]),
        "AT0": AT, "BT0": BT, "WrT": WrT,
        "br": np.ascontiguousarray(inputs["L0_ffn_br"]),
    }




def build_l1(dbg=False, shell=None, io=None, caches=None, skip_w=False):
    if shell is None:
        nc = bass.Bass("TRN2", target_bir_lowering=False, debug=False, detect_race_conditions=False)
        S0 = None
    else:
        nc, S0 = shell
    if io is None:
        x_ap = _dram(nc, "x", [TOK, D], F32, "ExternalInput")
        xout = _dram(nc, "xout", [TOK, D], F32, "ExternalOutput")
    else:
        x_ap, xout = io
    if not skip_w:
        wq_ap = _dram(nc, "wq", [D, D], F32, "ExternalInput")
        wk_ap = _dram(nc, "wk", [D, 1024], F32, "ExternalInput")
        wv_ap = _dram(nc, "wv", [D, 1024], F32, "ExternalInput")
        wo_ap = _dram(nc, "wo", [1024, D], F32, "ExternalInput")
    AT_ap = _dram(nc, "AT1", [D, 128], F32, "ExternalInput")
    BT_ap = _dram(nc, "BT1", [128, D], F32, "ExternalInput")
    RT_ap = _dram(nc, "RT", [128, 256], F32, "ExternalInput")

    kt_own_p = [_dram(nc, "kt_own1_%d" % j, [512, TOK], BF16) for j in range(2)]
    kt_all_p = [_dram(nc, "kt_all1_%d" % j, [4 * 512, TOK], BF16) for j in range(2)]
    v_own_p = [_dram(nc, "v_own1_%d" % j, [TOK, 516], BF16) for j in range(2)]
    v_all_p = [_dram(nc, "v_all1_%d" % j, [4 * TOK, 516], BF16) for j in range(2)]
    if caches is None:
        caches = {"c_wqe": _dram(nc, "c_wqe", [D, 1024], BF16, "ExternalInput"),
                  "c_wk": _dram(nc, "c_wk", [D, 1024], BF16, "ExternalInput"),
                  "c_wv": _dram(nc, "c_wv", [D, 1024], BF16, "ExternalInput"),
                  "c_wo": _dram(nc, "c_wo", [1024, D], BF16, "ExternalInput")}
    c_wqe, c_wk, c_wv, c_wo = caches["c_wqe"], caches["c_wk"], caches["c_wv"], caches["c_wo"]

    if dbg:
        d_xnT = _dram(nc, "d_xnT", [128, DMC, TOK], BF16, "ExternalOutput")
        d_QT = _dram(nc, "d_QT", [128, 8, TOK], BF16, "ExternalOutput")
        d_KT = _dram(nc, "d_KT", [128, 8, TOK], BF16, "ExternalOutput")
        d_V = _dram(nc, "d_V", [128, TC, 1032], BF16, "ExternalOutput")
        d_logits = _dram(nc, "d_logits", [128, TC, 8], F32, "ExternalOutput")
        d_lows = _dram(nc, "d_lows", [128, TOK], BF16, "ExternalOutput")
        d_attnT = _dram(nc, "d_attnT", [128, 8, TOK], BF16, "ExternalOutput")

    ctx = ExitStack()
    sb, S = _alloc_common(nc, ctx, S0, sfx="_b")
    ps = lambda n, shp, dt: ctx.enter_context(nc.psum_tensor(n + "_b", shp, dt))

    x_sb = sb("x_sb1", [128, 2, 4224], F32)
    xn_tm = sb("xn_tm", [128, 2, D], BF16)
    wfstage = xn_tm.ap().rearrange("p b n -> p (b n)").bitcast(F32)
    xn_T = sb("xn_T", [128, DMC, TOK], BF16)
    ln = {"stats": sb("ln_stats", [128, 8, 6], F32), "mv": sb("ln_mv", [128, 2], F32),
          "vtmp": sb("ln_vtmp", [128, 1], F32), "negmean": sb("ln_negmean", [128, 1], F32),
          "rstd": sb("ln_rstd", [128, 1], F32), "sd": sb("ln_sd", [128, 1], F32)}
    consts = {"ident_bf": sb("ident_bf", [128, 128], BF16),
              "eps": sb("eps", [128, 1], F32),
              "ones_bf": sb("ones_bf", [1, 128], BF16),
              "sc128": sb("sc128", [128, 1], F32)}
    QT = sb("QT", [128, 8, TOK], BF16)
    KT = sb("KT", [128, 8, TOK], BF16)
    v_full = sb("v_full", [128, TC, 1032], BF16)
    AT_sb = sb("AT_sb", [128, DMC, 128], BF16)
    BT_sb = sb("BT_sb", [128, D], BF16)
    RT_sb = sb("RT_sb", [128, DMC, 8], BF16)
    low_T = sb("low_T", [128, TOK], F32)
    low_s = sb("low_s", [128, TOK], BF16)
    logits = sb("logits", [128, TC, 8], F32)
    stok_bf = sb("stok_bf", [128, TC, 8], BF16)
    E_exp = sb("E_exp", [8, 128], BF16)
    rmax = sb("rmax", [128, 1], F32)
    P_sb = sb("P_sb", [128, 4, 512], BF16)
    sT = P_sb.ap().rearrange("p b n -> p (b n)")[0:8, 0:1024]
    attn_stage = sb("attn_stage", [128, 8, 128], BF16)
    attn_T = KT
    lrecip = sb("lrecip", [128, 1], F32)

    AR = x_sb.ap().rearrange("p b n -> p (b n)")
    ARb = AR.bitcast(BF16)
    wqf32 = AR[:, 0:4096]
    wqbf = ARb[:, 8192:12288]
    K_h = ARb[:, 0:8192].rearrange("p (b r t) -> p b r t", b=2, r=4)
    V_h = ARb[:, 8192:16448].rearrange("p (b k c) -> p b k c", b=2, k=32)
    x_res = AR[:, 4608:5632].rearrange("p (b n) -> p b n", b=2)
    xo_sb = AR[:, 5632:6656].rearrange("p (b n) -> p b n", b=2)

    PS = ps("PS1", [128, 8, 512], F32)
    ps_tp = PS[:, 0:2, :].rearrange("p b n -> p (b n)").bitcast(BF16).rearrange(
        "p (b n) -> p b n", b=2)
    ps_atp = PS[:, 0, :].bitcast(BF16)
    ps_sb8 = PS[:, 3, :].bitcast(BF16)

    with nc.Block() as block:
        @block.gpsimd
        def _(g):
            aux = S["aux"]

            def step(inst):
                aux.inc(inst, 1)
                g.wait_ge(aux.h, aux.n)
            step(g.memset(consts["ident_bf"][:, :], 0.0))
            step(g.affine_select(out=consts["ident_bf"][:, :], in_=consts["ident_bf"][:, :],
                                 compare_op=ALU.not_equal, fill=1.0, base=0,
                                 pattern=[[-1, 128]], channel_multiplier=1))
            step(g.memset(consts["eps"][:, :], 1e-5))
            step(g.memset(consts["ones_bf"][:, :], 1.0))
            step(g.memset(consts["sc128"][:, :], INV_SQRT128))
            step(g.memset(E_exp[:, :], 1.0))
            step(g.affine_select(out=E_exp[:, :], in_=E_exp[:, :],
                                 compare_op=ALU.is_ge, fill=0.0, base=0,
                                 pattern=[[1, 128]], channel_multiplier=-16))
            step(g.affine_select(out=E_exp[:, :], in_=E_exp[:, :],
                                 compare_op=ALU.is_ge, fill=0.0, base=15,
                                 pattern=[[-1, 128]], channel_multiplier=16))
            step(g.memset(v_full[:, :, :].rearrange("p t (h c) -> p t h c", c=129)[:, :, :, 128:129], 1.0))
            S["const"].inc(g.nop(), 1)

        _emit_ln(nc, block, S, x_ap, x_sb, xn_tm, xn_T, ln, ps_tp, consts, cN=S["const"].n)

        barrier(block, S, waits=[(S["tpev"], S["tpev"].n), (S["const"], S["const"].n)])

        bWCl1 = S["wc"].n

        @block.sync
        def _(sy):
            dmaw(sy, S["w"], wfstage.rearrange("p (c n) -> p c n", c=DMC),
                 AT_ap.rearrange("(c p) n -> p c n", p=128))

        @block.vector
        def _(v):
            v.wait_ge(S["w"].h, S["w"].n)
            S["wc"].inc(v.tensor_copy(AT_sb[:, :, :],
                                      wfstage.rearrange("p (c n) -> p c n", c=DMC)), 1)

        @block.sync
        def _(sy):
            sy.wait_ge(S["wc"].h, bWCl1 + 1)
            dmaw(sy, S["w"], wfstage, BT_ap)

        @block.vector
        def _(v):
            v.wait_ge(S["w"].h, S["w"].n)
            S["wc"].inc(v.tensor_copy(BT_sb[:, :], wfstage), 1)

        @block.sync
        def _(sy):
            sy.wait_ge(S["wc"].h, bWCl1 + 2)
            dmaw(sy, S["w"], wfstage[:, 0:256], RT_ap[:, :])

        @block.vector
        def _(v):
            v.wait_ge(S["w"].h, S["w"].n)
            S["wc"].inc(v.tensor_copy(RT_sb[:, :, :],
                                      wfstage[:, 0:256].rearrange("p (c e) -> p c e", c=DMC)), 1)

        barrier(block, S, waits=[(S["wc"], S["wc"].n)])

        wstr = ARb[:, 0:4096].rearrange("p (b n) -> p b n", b=8)
        wsem = {0: S["w"], 1: S["w2"]}

        def proj_bf(src_ap, OUT_sb, gate=None):
            nMM2 = S["mm"].n
            nEV2 = S["ev"].n
            bW2 = {0: S["w"].n, 1: S["w2"].n}

            def ld(eng, i):
                og, dmc = divmod(i, DMC)
                if i < 2 and gate is not None:
                    eng.wait_ge(S["lwo"].h, gate)
                if i >= 8:
                    eng.wait_ge(S["mm"].h, nMM2 + i - 7)
                dmaw(eng, wsem[i % 2], wstr[:, i % 8, :],
                     src_ap[dmc * 128:(dmc + 1) * 128, og * 512:(og + 1) * 512])

            @block.sync
            def _(sy):
                for i in range(0, 2 * DMC, 2):
                    ld(sy, i)

            @block.tensor
            def _(t):
                nld = {0: 0, 1: 0}
                for i in range(2 * DMC):
                    og, dmc = divmod(i, DMC)
                    nld[i % 2] += 16
                    t.wait_ge(wsem[i % 2].h, bW2[i % 2] + nld[i % 2])
                    if dmc == 0 and og == 1:
                        t.wait_ge(S["ev"].h, nEV2 + 8)
                    last = None
                    for oc in range(4):
                        for half in range(2):
                            last = t.matmul(PS[:, oc * 2 + half, :],
                                            wstr[:, i % 8, oc * 128:(oc + 1) * 128],
                                            xn_T[:, dmc, half * 512:(half + 1) * 512],
                                            start=(dmc == 0), stop=(dmc == DMC - 1))
                    S["mm"].inc(last, 1)

            @block.scalar
            def _(s):
                for og in range(2):
                    for i in range(og * DMC + 1, (og + 1) * DMC, 2):
                        ld(s, i)
                    s.wait_ge(S["mm"].h, nMM2 + (og + 1) * DMC)
                    last = None
                    for oc in range(4):
                        for half in range(2):
                            last = s.copy(OUT_sb[:, og * 4 + oc, half * 512:(half + 1) * 512],
                                          PS[:, oc * 2 + half, :])
                    S["ev"].inc(last, 8)

        proj_bf(c_wk, KT, gate=caches.get("lwo_wk"))
        barrier(block, S, waits=[(S["ev"], S["ev"].n)])

        nEVk1 = S["ev"].n
        ncc1 = S["cc"].n

        @block.gpsimd
        def _(g):
            for pc in range(2):
                g.wait_ge(S["ev"].h, nEVk1 - 16 + (pc + 1) * 8)
                dmaw(g, S["kv"], kt_own_p[pc].rearrange("(c p) t -> p c t", p=128),
                     KT[:, pc * 4:(pc + 1) * 4, :])
                g.wait_ge(S["kv"].h, S["kv"].n)
                S["cc"].inc(g.collective_compute(
                    "AllGather", ALU.bypass,
                    replica_groups=[[0, 1, 2, 3], [4, 5, 6, 7]],
                    ins=[kt_own_p[pc].opt()], outs=[kt_all_p[pc].opt()]), 1)

        nMM3 = S["mm"].n
        nEVv = S["ev"].n
        bWv = {0: S["w"].n, 1: S["w2"].n}

        def vld(eng, i):
            dv, dmc = divmod(i, DMC)
            if i < 2 and caches.get("lwo_wv") is not None:
                eng.wait_ge(S["lwo"].h, caches["lwo_wv"])
            if i >= 8:
                eng.wait_ge(S["mm"].h, nMM3 + i - 7)
            dmaw(eng, wsem[i % 2], wstr[:, i % 8, :],
                 c_wv[dmc * 128:(dmc + 1) * 128, dv * 512:(dv + 1) * 512])

        @block.sync
        def _(sy):
            for i in range(0, 2 * DMC, 2):
                vld(sy, i)

        @block.tensor
        def _(t):
            nld = {0: 0, 1: 0}
            for dv in range(2):
                for dmc in range(DMC):
                    i = dv * DMC + dmc
                    nld[i % 2] += 16
                    t.wait_ge(wsem[i % 2].h, bWv[i % 2] + nld[i % 2])
                    if dmc == 0 and dv == 1:
                        t.wait_ge(S["ev"].h, nEVv + 1)
                    last = None
                    for tc in range(TC):
                        last = t.matmul(PS[:, tc, :], xn_T[:, dmc, tc * 128:(tc + 1) * 128],
                                        wstr[:, i % 8, :],
                                        start=(dmc == 0), stop=(dmc == DMC - 1))
                    S["mm"].inc(last, 1)

        @block.scalar
        def _(s):
            for dv in range(2):
                for i in range(dv * DMC + 1, (dv + 1) * DMC, 2):
                    vld(s, i)
                s.wait_ge(S["mm"].h, nMM3 + (dv + 1) * DMC)
                last = None
                for tc in range(TC):
                    dst = v_full[:, tc, :].rearrange("p (h c) -> p h c", c=129)[:, dv * 4:(dv + 1) * 4, 0:128]
                    last = s.copy(dst, PS[:, tc, :].rearrange("p (h e) -> p h e", e=128))
                S["ev"].inc(last, 1)

        nEVv2 = S["ev"].n - 2

        @block.gpsimd
        def _(g):
            for pc in range(2):
                g.wait_ge(S["ev"].h, nEVv2 + pc + 1)
                dmaw(g, S["kv"], v_own_p[pc].rearrange("(c p) n -> p c n", p=128),
                     v_full[:, :, pc * 516:(pc + 1) * 516])
                g.wait_ge(S["kv"].h, S["kv"].n)
                S["cc"].inc(g.collective_compute(
                    "AllGather", ALU.bypass,
                    replica_groups=[[0, 1, 2, 3], [4, 5, 6, 7]],
                    ins=[v_own_p[pc].opt()], outs=[v_all_p[pc].opt()]), 1)

        barrier(block, S, waits=[(S["ev"], S["ev"].n)])

        proj_bf(c_wqe, QT, gate=caches.get("lwo_wq"))
        barrier(block, S, waits=[(S["ev"], S["ev"].n)])

        nMM5 = S["mm"].n
        nR1 = S["r1"].n

        @block.tensor
        def _(t):
            last = None
            for dmc in range(DMC):
                for tc in range(TC):
                    last = t.matmul(PS[:, tc, 0:8], xn_T[:, dmc, tc * 128:(tc + 1) * 128],
                                    RT_sb[:, dmc, :],
                                    start=(dmc == 0), stop=(dmc == DMC - 1))
            S["mm"].inc(last, 1)

        @block.vector
        def _(v):
            v.wait_ge(S["mm"].h, nMM5 + 1)
            last = None
            for tc in range(TC):
                last = v.tensor_copy(logits[:, tc, :], PS[:, tc, 0:8])
            S["r1"].inc(last, 1)

        @block.tensor
        def _(t):
            t.wait_ge(S["r1"].h, nR1 + 1)
            for half in range(2):
                last = None
                for dmc in range(DMC):
                    last = t.matmul(PS[:, 1 + half, :], AT_sb[:, dmc, :],
                                    xn_T[:, dmc, half * 512:(half + 1) * 512],
                                    start=(dmc == 0), stop=(dmc == DMC - 1))
                S["mm"].inc(last, 1)

        @block.scalar
        def _(s):
            s.wait_ge(S["mm"].h, nMM5 + 3)
            for half in range(2):
                S["ev"].inc(s.copy(low_T[:, half * 512:(half + 1) * 512], PS[:, 1 + half, :]), 1)

        @block.vector
        def _(v):
            vch = S["vch"]
            v.wait_ge(S["r1"].h, S["r1"].n)
            for tc in range(TC):
                vch.inc(v.reduce_max(rmax[:, :], logits[:, tc, :], axis=mybir.AxisListType.X), 1)
                v.wait_ge(vch.h, vch.n)
                S["r1"].inc(v.tensor_scalar(stok_bf[:, tc, :], logits[:, tc, :],
                                            rmax[:, :], rmax[:, :], ALU.is_ge, ALU.mult), 1)

        @block.tensor
        def _(t):
            t.wait_ge(S["r1"].h, S["r1"].n)
            last = None
            for tc in range(TC):
                last = t.transpose(ps_sb8[0:8, tc * 128:(tc + 1) * 128],
                                   stok_bf[:, tc, :], consts["ident_bf"][:, :])
            S["att"].inc(last, 1)

        @block.scalar
        def _(s):
            s.wait_ge(S["att"].h, S["att"].n)
            S["ev"].inc(s.copy(sT[0:8, :], ps_sb8[0:8, :]), 1)

        @block.tensor
        def _(t):
            t.wait_ge(S["ev"].h, S["ev"].n)
            for half in range(2):
                S["mm"].inc(t.matmul(PS[:, 4 + half, :], E_exp[0:8, :],
                                     sT[0:8, half * 512:(half + 1) * 512],
                                     start=True, stop=True), 1)

        @block.vector
        def _(v):
            v.wait_ge(S["mm"].h, S["mm"].n)
            v.wait_ge(S["ev"].h, S["ev"].n)
            last = None
            for half in range(2):
                last = v.tensor_tensor(low_s[:, half * 512:(half + 1) * 512],
                                       low_T[:, half * 512:(half + 1) * 512],
                                       PS[:, 4 + half, :], ALU.mult)
            S["r1"].inc(last, 1)

        barrier(block, S, waits=[(S["r1"], S["r1"].n)])

        if dbg:
            @block.sync
            def _(sy):
                dmaw(sy, S["aux"], d_xnT, xn_T[:, :, :])
                dmaw(sy, S["aux"], d_QT, QT[:, :, :])
                dmaw(sy, S["aux"], d_KT, KT[:, :, :])
                dmaw(sy, S["aux"], d_V, v_full[:, :, :])
                dmaw(sy, S["aux"], d_logits, logits[:, :, :])
                dmaw(sy, S["aux"], d_lows, low_s[:, :])

        kt_views1 = [ka.rearrange("(r c p) t -> p r c t", r=4, p=128) for ka in kt_all_p]
        v_views1 = [va.rearrange("(kc p) n -> p kc n", p=128) for va in v_all_p]
        kldD = S["kld"].n
        attD0 = S["att"].n
        bsS1, bpexp1, bpvm1, bep1, batev1 = (S["sS"].n, S["pexp"].n, S["pvm"].n,
                                             S["ep"].n, S["atev"].n)
        s_idx = [0]

        @block.sync
        def _(sy):
            for h in range(8):
                if h >= 2:
                    sy.wait_ge(S["ep"].h, bep1 + (h - 1) * 2)
                sy.wait_ge(S["cc"].h, ncc1 + (1 if h < 4 else 2))
                dmaw(sy, S["kld"], K_h[:, h % 2, :, :], kt_views1[h // 4][:, :, h % 4, :])
                sy.wait_ge(S["cc"].h, ncc1 + 2 + (1 if h < 4 else 2))
                for q in range(4):
                    dmaw(sy, S["kld"], V_h[:, h % 2, q * 8:(q + 1) * 8, :],
                         v_views1[h // 4][:, q * 8:(q + 1) * 8, 129 * (h % 4):129 * (h % 4 + 1)])

        @block.tensor
        def _(t):
            def emit_S(h, half, kc, si):
                bk = si % 4
                if si >= 4:
                    t.wait_ge(S["pexp"].h, bpexp1 + si - 3)
                last = t.matmul(PS[:, bk, :],
                                K_h[:, h % 2, kc // 8, (kc % 8) * 128:(kc % 8 + 1) * 128],
                                QT[:, h, half * 512:(half + 1) * 512],
                                start=True, stop=True)
                S["sS"].inc(last, 1)

            def emit_PV(h, half, kc, si):
                if kc == 0:
                    e = h * 2 + half
                    if e > 0:
                        t.wait_ge(S["ep"].h, bep1 + e)
                t.wait_ge(S["pexp"].h, bpexp1 + si + 1)
                for qc in range(4):
                    lastpv = t.matmul(PS[:, 4 + qc, 0:129],
                                      P_sb[:, si % 4, qc * 128:(qc + 1) * 128],
                                      V_h[:, h % 2, kc, :],
                                      start=(kc == 0), stop=(kc == 31))
                S["pvm"].inc(lastpv, 1)
            for h in range(8):
                t.wait_ge(S["kld"].h, kldD + (h + 1) * 80)
                if h >= 1:
                    t.wait_ge(S["atev"].h, batev1 + h)
                pend = []
                for half in range(2):
                    for kc in range(32):
                        si = s_idx[0]
                        emit_S(h, half, kc, si)
                        if si % 2 == 1 and len(pend) >= 2:
                            emit_PV(*pend.pop(0))
                            emit_PV(*pend.pop(0))
                        pend.append((h, half, kc, si))
                        s_idx[0] += 1
                for d in pend:
                    emit_PV(*d)
                t.wait_ge(S["ep"].h, bep1 + (h + 1) * 2)
                t.wait_ge(S["pexp"].h, bpexp1 + (h + 1) * 64)
                last = None
                for i in range(8):
                    last = t.transpose(ps_atp[:, i * 128:(i + 1) * 128],
                                       attn_stage[:, i, :], consts["ident_bf"][:, :])
                S["att"].inc(last, 1)

        @block.scalar
        def _(s):
            si = 0

            def atev_copy(hh):
                s.wait_ge(S["att"].h, attD0 + hh + 1)
                S["atev"].inc(s.copy(attn_T[:, hh, :], ps_atp[:, :]), 1)
            for h in range(8):
                if h >= 1:
                    atev_copy(h - 1)
                for half in range(2):
                    for kc in range(32):
                        bk = si % 4
                        s.wait_ge(S["sS"].h, bsS1 + si + 1)
                        if si >= 4:
                            s.wait_ge(S["pvm"].h, bpvm1 + si - 3)
                        S["pexp"].inc(
                            s.activation(out=P_sb[:, bk, :], in_=PS[:, bk, :],
                                         func=AF.Exp, scale=consts["sc128"][:, :]), 1)
                        si += 1
            atev_copy(7)

        @block.vector
        def _(v):
            vch = S["vch"]
            for h in range(8):
                for half in range(2):
                    e = h * 2 + half
                    v.wait_ge(S["pvm"].h, bpvm1 + (e + 1) * 32)
                    if h >= 1 and half == 0:
                        v.wait_ge(S["att"].h, attD0 + h)
                    last = None
                    for qc in range(4):
                        vch.inc(v.reciprocal(lrecip[:, :], PS[:, 4 + qc, 128:129]), 1)
                        v.wait_ge(vch.h, vch.n)
                        last = v.tensor_scalar(attn_stage[:, half * 4 + qc, :],
                                               PS[:, 4 + qc, 0:128],
                                               lrecip[:, :], None, ALU.mult)
                    S["ep"].inc(last, 1)

        barrier(block, S, waits=[(S["atev"], S["atev"].n)])

        if dbg:
            @block.sync
            def _(sy):
                dmaw(sy, S["aux"], d_attnT, attn_T[:, :, :])

        nW4 = S["w"].n
        nWC4 = S["wc"].n
        nPO = S["po"].n
        bXR1 = S["w2"].n
        bOUT1 = S["out"].n
        bRESV1 = S["resv"].n
        OCN, OW = 8, 512
        wobf = ARb[:, 0:8192].rearrange("p (b c n) -> p b c n", b=2, c=8)

        @block.sync
        def _(sy):
            for oc in range(OCN):
                if oc == 0 and caches.get("lwo_all") is not None:
                    sy.wait_ge(S["lwo"].h, caches["lwo_all"])
                if oc >= 2:
                    sy.wait_ge(S["po"].h, nPO + (oc - 1) * 8)
                dmaw(sy, S["w"], wobf[:, oc % 2, :, :],
                     c_wo.rearrange("(c p) o -> p c o", p=128)[:, :, oc * OW:(oc + 1) * OW])

        @block.scalar
        def _(s):
            for oc in range(OCN):
                for tc in range(TC):
                    i = oc * 8 + tc
                    if i >= 2:
                        s.wait_ge(S["resv"].h, bRESV1 + i - 1)
                    dmaw(s, S["w2"], x_res[:, i % 2, :],
                         x_ap[tc * 128:(tc + 1) * 128, oc * OW:(oc + 1) * OW])

        @block.vector
        def _(v):
            for oc in range(OCN):
                for tc in range(TC):
                    i = oc * 8 + tc
                    v.wait_ge(S["po"].h, nPO + i + 1)
                    v.wait_ge(S["w2"].h, bXR1 + (i + 1) * 16)
                    if i >= 2:
                        v.wait_ge(S["out"].h, bOUT1 + (i - 1) * 16)
                    S["resv"].inc(v.scalar_tensor_tensor(
                        out=xo_sb[:, i % 2, :], in0=PS[:, i % 4, 0:OW], scalar=1.0,
                        in1=x_res[:, i % 2, :], op0=ALU.mult, op1=ALU.add), 1)

        @block.tensor
        def _(t):
            for oc in range(OCN):
                t.wait_ge(S["w"].h, nW4 + (oc + 1) * 16)
                for tc in range(TC):
                    i = oc * 8 + tc
                    b = i % 4
                    if i >= 4:
                        t.wait_ge(S["resv"].h, bRESV1 + i - 3)
                    for h in range(8):
                        t.matmul(PS[:, b, 0:OW], attn_T[:, h, tc * 128:(tc + 1) * 128],
                                 wobf[:, oc % 2, h, :], start=(h == 0), stop=False)
                    S["po"].inc(t.matmul(PS[:, b, 0:OW], low_s[:, tc * 128:(tc + 1) * 128],
                                         BT_sb[:, oc * OW:(oc + 1) * OW],
                                         start=False, stop=True), 1)

        @block.gpsimd
        def _(g):
            for oc in range(OCN):
                for tc in range(TC):
                    i = oc * 8 + tc
                    g.wait_ge(S["resv"].h, bRESV1 + i + 1)
                    dmaw(g, S["out"], xout[tc * 128:(tc + 1) * 128, oc * OW:(oc + 1) * OW],
                         xo_sb[:, i % 2, :])
            g.wait_ge(S["out"].h, S["out"].n)
            if dbg:
                g.wait_ge(S["aux"].h, S["aux"].n)

        @block.sync
        def _(sy):
            sy.wait_ge(S["out"].h, S["out"].n)

    ctx.close()
    return nc


def _host_layout_l1(inputs, bslice, x1=None):
    A = inputs["L1_ffn_A"]
    AT = np.ascontiguousarray(A.reshape(128, 4096).T)
    Bw = inputs["L1_ffn_Bw"]
    BT = np.ascontiguousarray(Bw.transpose(0, 2, 1).reshape(128, 4096))
    Rw = inputs["L1_ffn_route_w"]
    RT = np.ascontiguousarray(Rw.T.reshape(32, 128, 8).transpose(1, 0, 2).reshape(128, 256))
    d = {
        "wq": np.ascontiguousarray(inputs["L1_attn_wq"]),
        "wk": np.ascontiguousarray(inputs["L1_attn_wk"]),
        "wv": np.ascontiguousarray(inputs["L1_attn_wv"]),
        "wo": np.ascontiguousarray(inputs["L1_attn_wo"]),
        "AT1": AT, "BT1": BT, "RT": RT,
    }
    if x1 is not None:
        d["x"] = np.ascontiguousarray(x1.reshape(2, 4, 1024, 4096)[bslice[0], bslice[1]])
    return d


def build_fused():
    nc = bass.Bass("TRN2", target_bir_lowering=False, debug=False, detect_race_conditions=False)
    octx = ExitStack()
    S = _mk_sems(nc, octx)
    x_in = _dram(nc, "x", [TOK, D], F32, "ExternalInput")
    x1 = _dram(nc, "x1", [TOK, D], F32, "Internal")
    xout = _dram(nc, "xout", [TOK, D], F32, "ExternalOutput")
    prep = {
        "wq": _dram(nc, "wq", [D, D], F32, "ExternalInput"),
        "wk": _dram(nc, "wk", [D, 1024], F32, "ExternalInput"),
        "wv": _dram(nc, "wv", [D, 1024], F32, "ExternalInput"),
        "wo": _dram(nc, "wo", [1024, D], F32, "ExternalInput"),
        "c_wqe": _dram(nc, "c_wqe", [D, 1024], BF16),
        "c_wk": _dram(nc, "c_wk", [D, 1024], BF16),
        "c_wv": _dram(nc, "c_wv", [D, 1024], BF16),
        "c_wo": _dram(nc, "c_wo", [1024, D], BF16),
        "c_wo0": _dram(nc, "c_wo0", [512, D], BF16),
    }
    build_l0(shell=(nc, S), io=(x_in, x1), prep=prep)
    caches = {k: prep[k] for k in ("c_wqe", "c_wk", "c_wv", "c_wo",
                               "lwo_wk", "lwo_wv", "lwo_wq", "lwo_all")}
    build_l1(shell=(nc, S), io=(x1, xout), caches=caches, skip_w=True)
    octx.close()
    return nc


def kernel(**inputs):
    nc = build_fused()
    in_maps = []
    for c in range(8):
        m = _host_layout_l0(inputs, divmod(c, 4))
        m.update(_host_layout_l1(inputs, divmod(c, 4)))
        in_maps.append(m)
    res = run_bass_kernel_spmd(nc, in_maps, core_ids=list(range(8)))
    return np.stack([r["xout"] for r in res.results]).reshape(2, 4096, 4096)
